# version 15
# baseline (speedup 1.0000x reference)
import math
import numpy as np
import ml_dtypes
from contextlib import ExitStack

import concourse.bass as bass
import concourse.mybir as mybir
from concourse.bass_utils import run_bass_kernel_spmd

F32 = mybir.dt.float32
BF16 = mybir.dt.bfloat16
I32 = mybir.dt.int32
ACT = mybir.ActivationFunctionType
ALU = mybir.AluOpType
AX = mybir.AxisListType
AP = bass.AP

D, T, BPC = 1024, 2048, 2
NT = BPC * T
H, DH = 16, 64
NCH, NTAU = 8, 32
IFF, NIC = 2816, 22
MH, NMH = 4096, 32
EPS = 1e-5
TWO_PI = 2.0 * math.pi
PI_CLAMP = 3.1415920
RND_CAP = 128.49

DBG = []


def build_program():
    nc = bass.Bass("TRN2", target_bir_lowering=False)

    def din(name, shape):
        return nc.dram_tensor(name, shape, F32, kind="ExternalInput")

    x_in = din("x", [NT, D])
    wq, wk, wv, wg, wp, wo = (din(n, [D, D]) for n in
                              ("wq", "wk", "wv", "wgt", "wp", "wo"))
    wclk = din("wclk", [D, H])
    b_gate, b_out, b_clk = din("b_gate", [D]), din("b_out", [D]), din("b_clk", [H])
    g1v, b1v = din("g1", [D]), din("b1", [D])
    g2v, b2v = din("g2", [D]), din("b2", [D])
    w1, bm1, w2, bm2 = din("w1", [D, MH]), din("bm1", [MH]), din("w2", [MH, D]), din("bm2", [D])
    clipx, clipd = din("clipx", [1]), din("clipd", [1])
    wgate, wup, wdown = din("wgate", [IFF, D]), din("wup", [IFF, D]), din("wdown", [D, IFF])
    c_idbf = nc.dram_tensor("c_idbf", [128, 128], BF16, kind="ExternalInput")
    c_ut = nc.dram_tensor("c_ut", [128, 128], BF16, kind="ExternalInput")
    c_sut = nc.dram_tensor("c_sut", [128, 128], BF16, kind="ExternalInput")
    c_onesb = nc.dram_tensor("c_onesb", [128, 128], BF16, kind="ExternalInput")
    c_onesf = nc.dram_tensor("c_onesf", [128, 128], F32, kind="ExternalInput")
    c_iota128 = nc.dram_tensor("c_iota128", [128, 1], F32, kind="ExternalInput")
    c_iota32 = nc.dram_tensor("c_iota32", [1, 32], F32, kind="ExternalInput")

    y_out = nc.dram_tensor("y", [NT, D], F32, kind="ExternalOutput")

    xn_spill = nc.dram_tensor("xn_spill", [D, NT], BF16)
    out_spill = nc.dram_tensor("out_spill", [D, NT], BF16)
    s_spill = nc.dram_tensor("s_spill", [D, NT], BF16)
    mid_spill = nc.dram_tensor("mid_spill", [IFF, NT], BF16)
    w1bf = nc.dram_tensor("w1bf", [NMH, NCH, 128, 128], BF16)
    wgT_sp = nc.dram_tensor("wgT_sp", [D, IFF], BF16)
    wuT_sp = nc.dram_tensor("wuT_sp", [D, IFF], BF16)
    wdT_sp = nc.dram_tensor("wdT_sp", [IFF, D], BF16)
    maxrt = nc.dram_tensor("maxrt", [1, 128], F32)
    maxrt2 = nc.dram_tensor("maxrt2", [1, 128], F32)
    cc1i, cc1o = nc.dram_tensor("cc1i", [1, 128], F32), nc.dram_tensor("cc1o", [1, 128], F32)
    cc2i, cc2o = nc.dram_tensor("cc2i", [1, 128], F32), nc.dram_tensor("cc2o", [1, 128], F32)

    taps = {}

    def tap(name, shape, dtype):
        if name in DBG and name not in taps:
            taps[name] = nc.dram_tensor("tap_" + name, shape, dtype,
                                        kind="ExternalOutput")
        return taps.get(name)

    stack = ExitStack()
    sems, cnt = {}, {}

    def sem(name):
        if name not in sems:
            sems[name] = stack.enter_context(nc.semaphore(name))
            cnt[name] = 0
        return sems[name]

    def inc(instr, name, k=1):
        instr.then_inc(sem(name), k)
        cnt[name] += k
        return cnt[name]

    def cget(name):
        sem(name)
        return cnt[name]

    dw_recent = {"vector": [], "scalar": [], "gpsimd": []}

    class DW:
        _PASS = {"wait_ge", "sem_inc", "nop"}
        _NOTRACK = {"dma_start"}
        WINDOW = 2

        def __init__(self, eng):
            self._e = eng
            tn = type(eng).__name__.lower()
            self._k = ("vector" if "vector" in tn else
                       "scalar" if "scalar" in tn else "gpsimd")

        def drain(self):
            dw_recent[self._k].clear()
            return self._e.drain()

        def __getattr__(self, n):
            f = getattr(self._e, n)
            if n in DW._PASS or n in DW._NOTRACK or not callable(f):
                return f
            rec = dw_recent[self._k]
            def g(*a, **k):
                names = []
                for o in list(a) + list(k.values()):
                    t = getattr(o, "tensor", None)
                    if t is not None:
                        names.append(getattr(t, "name", None) or id(t))
                outs, ins = set(names[:1]), set(names[1:])
                if any((ins & ro) or (outs & ro) or (outs & ri)
                       for ri, ro in rec):
                    self._e.drain()
                    rec.clear()
                r = f(*a, **k)
                rec.append((ins, outs))
                while len(rec) > DW.WINDOW:
                    rec.pop(0)
                return r
            return g

    def chain(steps, name="ch"):
        base = cget(name)
        with nc.Block() as blk:
            for en in ENGS:
                if not any(e == en for e, _ in steps):
                    continue

                def body(eng, en=en):
                    if en in ("vector", "scalar", "gpsimd"):
                        eng = DW(eng)
                    for i, (e2, fn) in enumerate(steps):
                        if e2 != en:
                            continue
                        if i > 0:
                            eng.wait_ge(sem(name), base + i)
                        last = fn(eng)
                        assert last is not None, f"chain step {i} returned None"
                        inc(last, name)
                getattr(blk, en)(body)

    def chain2(steps, name="c2"):
        if "serial2" in DBG or ("ser:" + name) in DBG:
            steps = [(e, fn, [i - 1] if i > 0 else [])
                     for i, (e, fn, deps) in enumerate(steps)]
        ordinals = []
        count = {}
        for e, fn, deps in steps:
            ordinals.append(count.get(e, 0))
            count[e] = count.get(e, 0) + 1
        bases = {e: cget(f"{name}_{e}") for e in count}
        with nc.Block() as blk:
            for en in list(count):
                def body(eng, en=en):
                    if en in ("vector", "scalar", "gpsimd"):
                        eng = DW(eng)
                    for i, (e, fn, deps) in enumerate(steps):
                        if e != en:
                            continue
                        need = {}
                        for d in deps:
                            de = steps[d][0]
                            if de == en:
                                continue
                            need[de] = max(need.get(de, -1), ordinals[d])
                        for de, o in need.items():
                            eng.wait_ge(sem(f"{name}_{de}"), bases[de] + o + 1)
                        last = fn(eng)
                        assert last is not None, f"chain2 step {i} returned None"
                        inc(last, f"{name}_{en}")
                getattr(blk, en)(body)

    bar_tgt = [0]
    ENGS = ["sync", "tensor", "vector", "scalar", "gpsimd"]

    def barrier():
        bar = sem("bar")
        bar_tgt[0] += 5
        tgt = bar_tgt[0]
        with nc.Block() as blk:
            for en in ENGS:
                def mk(en=en):
                    @getattr(blk, en)
                    def _(eng):
                        if en != "sync":
                            eng.drain()
                        eng.sem_inc(bar, 1)
                        eng.wait_ge(bar, tgt)
                mk()
        for rec in dw_recent.values():
            rec.clear()

    def sbuf(name, shape, dtype, st=None):
        return (st or stack).enter_context(nc.sbuf_tensor(name, shape, dtype))

    def psum(name, shape, dtype=F32, st=None):
        return (st or stack).enter_context(nc.psum_tensor(name, shape, dtype))

    idbf = sbuf("idbf", [128, 128], BF16)
    utb = sbuf("utb", [128, 128], BF16)
    onesb = sbuf("onesb", [128, 128], BF16)
    onesf = sbuf("onesf", [128, 128], F32)
    iota128 = sbuf("iota128", [128, 1], F32)
    iota32 = sbuf("iota32", [1, 32], F32)

    bocol = sbuf("bocol", [128, 8], F32)
    g2col = sbuf("g2col", [128, 8], F32)
    b2col = sbuf("b2col", [128, 8], F32)
    bm2col = sbuf("bm2col", [128, 8], F32)
    bm1col = sbuf("bm1col", [128, 32], F32)

    rmaxcol = sbuf("rmaxcol", [128, 1], F32)
    rmaxcol2 = sbuf("rmaxcol2", [128, 1], F32)
    rowm = sbuf("rowm", [1, 128], F32)
    ccrow = sbuf("ccrow", [1, 128], F32)
    cc2row = sbuf("cc2row", [1, 128], F32)
    sxv = sbuf("sxv", [128, 1], F32)
    rsxv = sbuf("rsxv", [128, 1], F32)
    sdv = sbuf("sdv", [128, 1], F32)
    rsdv = sbuf("rsdv", [128, 1], F32)
    scg = sbuf("scg", [128, NIC], F32)
    scu = sbuf("scu", [128, NIC], F32)
    scd = sbuf("scd", [128, NCH], F32)
    kcol = sbuf("kcol", [128, 4], F32)

    stXN1 = ExitStack()
    xn1_fm = sbuf("xn1_fm", [128, NCH * NT], BF16, stXN1)
    stTab = ExitStack()
    cosT2 = sbuf("cosT2", [128, NTAU * 64], F32, stTab)
    sinT = sbuf("sinT", [128, NTAU * 64], F32, stTab)
    clockS = sbuf("clockS", [128, NTAU * H], F32, stTab)
    maskadd = sbuf("maskadd", [128, 4 * 512], BF16, stTab)
    brepc = sbuf("brepc", [128, H], F32, stTab)
    growB = sbuf("growB", [1, D], F32, stTab)
    st_early = ExitStack()
    brow = sbuf("brow", [1, 4096], F32, st_early)
    g1rep = sbuf("g1rep", [128, D], F32, st_early)
    b1rep = sbuf("b1rep", [128, D], F32, st_early)
    stArena = ExitStack()
    arena1 = sbuf("arena1", [128, NCH * NT], BF16, stArena)

    psA = psum("psA", [128, 1024], BF16)
    psB = psum("psB", [128, 1024], BF16)
    psP = psum("psP", [128, 512], F32)
    psQ = psum("psQ", [128, 512], F32)
    psS1 = psum("psS1", [128, 512], F32)
    psS2 = psum("psS2", [128, 512], F32)
    psO = psum("psO", [128, 512], F32)
    psR = psum("psR", [128, 512], F32)

    with nc.Block() as blk:
        @blk.sync
        def _(sy):
            for dst, src in [(idbf, c_idbf), (utb, c_ut),
                             (onesb, c_onesb), (onesf, c_onesf),
                             (iota128, c_iota128), (iota32, c_iota32)]:
                inc(sy.dma_start(out=dst[:, :], in_=src[:, :]), "dA", 16)

        @blk.gpsimd
        def _(g):
            for kc in range(NCH):
                inc(g.dma_start(
                    out=AP(w1bf, kc * 128 * 128,
                           [[128, 128], [8 * 128 * 128, NMH], [1, 128]]),
                    in_=AP(w1, kc * 128 * MH,
                           [[MH, 128], [128, NMH], [1, 128]])), "dW1c", 16)

        @blk.vector
        def _(v):
            v = DW(v)
            v.wait_ge(sem("dA"), cget("dA"))
            for s_ in range(4):
                base = s_ * 512
                if s_ > 0:
                    v.memset(maskadd[:, base:base + 128 * s_], -4000.0)
                v.tensor_scalar(maskadd[:, base + 128 * s_:
                                        base + 128 * (s_ + 1)],
                                utb[:, :], -1.0, 4000.0, ALU.add, ALU.mult)
                if s_ < 3:
                    v.memset(maskadd[:, base + 128 * (s_ + 1):base + 512], 0.0)
            v.memset(rmaxcol[:, :], 0.0)
            v.memset(rmaxcol2[:, :], 0.0)
            v.memset(ccrow[:, :], 0.0)
            v.memset(cc2row[:, :], 0.0)
            v.memset(kcol[:, 0:1], EPS)
            v.memset(kcol[:, 1:2], EPS * EPS)
            v.memset(kcol[:, 2:3], 1.0)
            v.memset(kcol[:, 3:4], 0.5)

    barrier()

    st0 = ExitStack()
    thS = sbuf("thS", [128, NTAU * 32], F32, st0)
    thC = sbuf("thC", [128, NTAU * 32], F32, st0)
    snR = sbuf("snR", [128, NTAU * 32], F32, st0)
    rpF = sbuf("rpF", [128, 32], F32, st0)
    rpB = sbuf("rpB", [128, 32], F32, st0)
    rpC = sbuf("rpC", [128, 32], F32, st0)
    rpI = sbuf("rpI", [128, 32], I32, st0)
    tvcol = sbuf("tvcol", [128, 1], F32, st0)

    def range_reduce(v, src_ap, dst_ap, shift):
        v.tensor_scalar(rpB[:, :], src_ap, float(shift), 1.0 / TWO_PI,
                        ALU.add, ALU.mult)
        v.tensor_copy(rpI[:, :], rpB[:, :])
        v.tensor_scalar(rpC[:, :], src_ap, float(shift), None, ALU.add)
        v.scalar_tensor_tensor(rpB[:, :], rpI[:, :], -TWO_PI, rpC[:, :],
                               ALU.mult, ALU.add)
        v.tensor_scalar(rpC[:, :], rpB[:, :], math.pi, -TWO_PI,
                        ALU.is_ge, ALU.mult)
        v.tensor_tensor(rpB[:, :], rpB[:, :], rpC[:, :], ALU.add)
        v.tensor_scalar(rpC[:, :], rpB[:, :], -math.pi, TWO_PI,
                        ALU.is_le, ALU.mult)
        v.tensor_tensor(rpB[:, :], rpB[:, :], rpC[:, :], ALU.add)
        return v.tensor_scalar(dst_ap, rpB[:, :], PI_CLAMP, -PI_CLAMP,
                               ALU.min, ALU.max)

    with nc.Block() as blk:
        @blk.scalar
        def _(s):
            s = DW(s)
            inc(s.activation(brow[0:1, 0:32], iota32[0:1, :], ACT.Exp,
                             bias=0.0, scale=-math.log(10000.0) / 32.0), "sA")
            s.wait_ge(sem("vA_rope"), 1)
            s.activation(snR[:, :], thS[:, :], ACT.Sin)
            inc(s.activation(thC[:, :], thC[:, :], ACT.Sin), "sA")

        @blk.tensor
        def _(t):
            t.wait_ge(sem("sA"), cget("sA") - 1)
            inc(t.matmul(psR[:, 0:32], onesf[0:1, :], brow[0:1, 0:32],
                         start=True, stop=True), "tA")

        @blk.vector
        def _(v):
            v = DW(v)
            v.wait_ge(sem("tA"), cget("tA"))
            v.tensor_copy(rpF[:, :], psR[:, 0:32])
            for tau in range(NTAU):
                pos = (tau % 16) * 128
                v.tensor_scalar(tvcol[:, :], iota128[:, :], float(pos), None, ALU.add)
                v.tensor_scalar(thS[:, tau * 32:(tau + 1) * 32], rpF[:, :],
                                tvcol[:, 0:1], None, ALU.mult)
                range_reduce(v, thS[:, tau * 32:(tau + 1) * 32],
                             thC[:, tau * 32:(tau + 1) * 32], math.pi / 2)
                last = range_reduce(v, thS[:, tau * 32:(tau + 1) * 32],
                                    thS[:, tau * 32:(tau + 1) * 32], 0.0)
            inc(last, "vA_rope")
            v.wait_ge(sem("sA"), cget("sA"))
            v.tensor_scalar(AP(sinT, 0, [[NTAU * 64, 128], [2, NTAU * 32]]),
                            snR[:, :], -1.0, None, ALU.mult)
            v.tensor_copy(AP(sinT, 1, [[NTAU * 64, 128], [2, NTAU * 32]]),
                          snR[:, :])
            v.tensor_copy(AP(cosT2, 0, [[NTAU * 64, 128], [2, NTAU * 32]]),
                          thC[:, :])
            inc(v.tensor_copy(AP(cosT2, 1, [[NTAU * 64, 128], [2, NTAU * 32]]),
                              thC[:, :]), "vA")

    barrier()
    st0.close()

    jobs = [
        ("rep", g1v, D, g1rep), ("rep", b1v, D, b1rep), ("rep", b_clk, H, brepc),
        ("col", b_out, D, bocol), ("col", g2v, D, g2col), ("col", b2v, D, b2col),
        ("col", bm2, D, bm2col), ("col", bm1, MH, bm1col),
    ]
    with nc.Block() as blk:
        @blk.sync
        def _(sy):
            inc(sy.dma_start(out=growB[0:1, :], in_=b_gate.ap()), "dB", 16)
    nvb = []
    for kind, src_, n, dst in jobs:
        nvb.append(1 if kind == "rep" else (n // 128 + 7) // 8)

    with nc.Block() as blk:
        @blk.sync
        def _(sy):
            base_vb = cget("vB")
            for j, (kind, src_, n, dst) in enumerate(jobs):
                if j > 0:
                    sy.wait_ge(sem("vB"), base_vb + sum(nvb[:j]))
                inc(sy.dma_start(out=brow[0:1, 0:n], in_=src_.ap()), "dB", 16)

        @blk.tensor
        def _(t):
            base_db = cget("dB") - 16 * len(jobs)
            base_vb = cget("vB")
            base_tb = cget("tB")
            vb_done = 0
            for j, (kind, src_, n, dst) in enumerate(jobs):
                t.wait_ge(sem("dB"), base_db + 16 * (j + 1))
                if kind == "rep":
                    for o in range(0, n, 512):
                        w_ = min(512, n - o)
                        pst = psR if o == 0 else psQ
                        mm = t.matmul(pst[:, 0:w_], onesf[0:1, :],
                                      brow[0:1, o:o + w_], start=True, stop=True)
                    inc(mm, "tB")
                    vb_done += 1
                else:
                    ncol = n // 128
                    for gidx in range(0, ncol, 8):
                        if vb_done > 0 or gidx > 0:
                            t.wait_ge(sem("vB"), base_vb + vb_done)
                        for jj in range(gidx, min(gidx + 8, ncol)):
                            mm = t.matmul(psR[:, (jj % 8):(jj % 8) + 1],
                                          brow[0:1, 128 * jj:128 * (jj + 1)],
                                          onesf[0:1, 0:1], start=True, stop=True)
                        inc(mm, "tB")
                        vb_done += 1

        @blk.vector
        def _(v):
            v = DW(v)
            base_tb = cget("tB") - sum(nvb)
            ng = 0
            for j, (kind, src_, n, dst) in enumerate(jobs):
                if kind == "rep":
                    ng += 1
                    v.wait_ge(sem("tB"), base_tb + ng)
                    for o in range(0, n, 512):
                        w_ = min(512, n - o)
                        pst = psR if o == 0 else psQ
                        cp = v.tensor_copy(dst[:, o:o + w_], pst[:, 0:w_])
                    inc(cp, "vB")
                else:
                    ncol = n // 128
                    for gidx in range(0, ncol, 8):
                        ng += 1
                        v.wait_ge(sem("tB"), base_tb + ng)
                        for jj in range(gidx, min(gidx + 8, ncol)):
                            cp = v.tensor_copy(dst[:, jj:jj + 1],
                                               psR[:, (jj % 8):(jj % 8) + 1])
                        inc(cp, "vB")

    barrier()

    stA = ExitStack()
    xa = [sbuf("xa0", [128, D], F32, stA), sbuf("xa1", [128, D], F32, stA)]
    xcf = sbuf("xcf", [128, D], F32, stA)
    xsq = sbuf("xsq", [128, D], F32, stA)
    xnb = [sbuf("xnb0", [128, D], BF16, stA), sbuf("xnb1", [128, D], BF16, stA)]
    x1b = [sbuf("x1b0", [128, D], BF16, stA), sbuf("x1b1", [128, D], BF16, stA)]
    smA = sbuf("smA", [128, 1], F32, stA)
    smB = sbuf("smB", [128, 1], F32, stA)
    smC = sbuf("smC", [128, 1], F32, stA)
    smD = sbuf("smD", [128, 1], F32, stA)
    smE = sbuf("smE", [128, 1], F32, stA)
    smF = sbuf("smF", [128, 1], F32, stA)
    smG = sbuf("smG", [128, 1], F32, stA)
    smH = sbuf("smH", [128, 1], F32, stA)
    smI = sbuf("smI", [128, 1], F32, stA)
    statT = sbuf("statT", [128, 128], F32, stA)

    with nc.Block() as blk:
        @blk.sync
        def _(sy):
            for tau in range(NTAU):
                if tau >= 2:
                    sy.wait_ge(sem("vA2"), tau - 1)
                inc(sy.dma_start(out=xa[tau % 2][:, :],
                                 in_=x_in[128 * tau:128 * (tau + 1), :]),
                    f"dX{tau % 2}", 16)

        @blk.vector
        def _(v):
            v = DW(v)
            for tau in range(NTAU):
                a = xa[tau % 2]
                if tau > 0:
                    v.wait_ge(sem("tA2"), tau)
                    v.tensor_copy(
                        AP(arena1, 128 * (tau - 1), [[NCH * NT, 128], [NT, NCH], [1, 128]]),
                        psA[:, :])
                    inc(v.tensor_copy(
                        AP(xn1_fm, 128 * (tau - 1), [[NCH * NT, 128], [NT, NCH], [1, 128]]),
                        psB[:, :]), "vE")
                v.wait_ge(sem(f"dX{tau % 2}"), 16 * (tau // 2 + 1))
                v.reduce_sum(smA[:, :], a[:, :], AX.X)
                v.tensor_scalar(smB[:, :], smA[:, :], 1.0 / D, None, ALU.mult)
                v.tensor_scalar(xcf[:, :], a[:, :], smB[:, 0:1], None, ALU.subtract)
                v.tensor_tensor(xsq[:, :], xcf[:, :], xcf[:, :], ALU.mult)
                inc(v.reduce_sum(smC[:, :], xsq[:, :], AX.X), "vA1")
                v.wait_ge(sem("sA2"), tau + 1)
                v.reciprocal(smF[:, :], smD[:, :])
                v.reciprocal(smG[:, :], smE[:, :])
                v.tensor_scalar(xnb[tau % 2][:, :], xcf[:, :], smF[:, 0:1],
                                None, ALU.mult)
                v.reduce_max(smH[:, :], xcf[:, :], AX.X, apply_absolute_value=True)
                v.tensor_tensor(smH[:, :], smH[:, :], smF[:, :], ALU.mult)
                v.tensor_tensor(rmaxcol[:, :], rmaxcol[:, :], smH[:, :], ALU.max)
                v.scalar_tensor_tensor(xsq[:, :], xcf[:, :], smG[:, 0:1],
                                       g1rep[:, :], ALU.mult, ALU.mult)
                v.tensor_copy(statT[:, tau:tau + 1], smB[:, :])
                v.tensor_copy(statT[:, 32 + tau:33 + tau], smF[:, :])
                v.tensor_copy(statT[:, 64 + tau:65 + tau], smG[:, :])
                v.tensor_copy(statT[:, 96 + tau:97 + tau], smC[:, :])
                inc(v.tensor_tensor(x1b[tau % 2][:, :], xsq[:, :], b1rep[:, :],
                                    ALU.add), "vA2")
            v.wait_ge(sem("tA2"), NTAU)
            v.tensor_copy(
                AP(arena1, 128 * (NTAU - 1), [[NCH * NT, 128], [NT, NCH], [1, 128]]),
                psA[:, :])
            inc(v.tensor_copy(
                AP(xn1_fm, 128 * (NTAU - 1), [[NCH * NT, 128], [NT, NCH], [1, 128]]),
                psB[:, :]), "vE")

        @blk.scalar
        def _(s):
            s = DW(s)
            for tau in range(NTAU):
                s.wait_ge(sem("vA1"), tau + 1)
                s.activation(smD[:, :], smC[:, :], ACT.Sqrt,
                             bias=kcol[:, 0:1], scale=1.0 / D)
                s.activation(smE[:, :], smC[:, :], ACT.Sqrt,
                             bias=kcol[:, 1:2], scale=(1.0 + EPS) / D)
                inc(s.activation(smI[:, :], smD[:, :], ACT.Copy), "sA2")

        @blk.tensor
        def _(t):
            for tau in range(NTAU):
                t.wait_ge(sem("vA2"), tau + 1)
                if tau > 0:
                    t.wait_ge(sem("vE"), tau)
                for c in range(NCH):
                    t.transpose(psA[:, 128 * c:128 * (c + 1)],
                                xnb[tau % 2][:, 128 * c:128 * (c + 1)], idbf[:, :])
                    tr = t.transpose(psB[:, 128 * c:128 * (c + 1)],
                                     x1b[tau % 2][:, 128 * c:128 * (c + 1)], idbf[:, :])
                inc(tr, "tA2")

    barrier()

    with nc.Block() as blk:
        @blk.sync
        def _(sy):
            for c in range(NCH):
                inc(sy.dma_start(out=xn_spill[128 * c:128 * (c + 1), :],
                                 in_=arena1[:, NT * c:NT * (c + 1)]), "dO", 16)
            inc(sy.dma_start(out=maxrt[0:1, :], in_=rmaxcol[:, 0:1]), "dO", 16)
            sy.wait_ge(sem("dO"), cget("dO"))
            inc(sy.dma_start(out=rowm[0:1, :], in_=maxrt[0:1, :]), "dA", 16)

        @blk.vector
        def _(v):
            v = DW(v)
            v.wait_ge(sem("dA"), cget("dA"))
            inc(v.reduce_max(ccrow[0:1, 0:1], rowm[0:1, :], AX.X), "vA")

        @blk.gpsimd
        def _(g):
            g = DW(g)
            g.wait_ge(sem("vA"), cget("vA"))
            inc(g.dma_start(out=cc1i[0:1, :], in_=ccrow[0:1, :]), "dW", 16)
            g.wait_ge(sem("dW"), cget("dW"))
            if "nocc" in DBG:
                inc(g.dma_start(out=cc1o[0:1, :], in_=cc1i[0:1, :]), "cc", 16)
            else:
                inc(g.collective_compute(
                    "AllReduce", ALU.max, replica_groups=[list(range(8))],
                    ins=[cc1i.ap().opt()], outs=[cc1o.ap().opt()]), "cc", 1)

    barrier()

    tp0 = tap("stats", [128, 128], F32)
    tp1 = tap("xn_fm", [128, NCH * NT], BF16)
    tp2 = tap("xn1_fm", [128, NCH * NT], BF16)
    tp3 = tap("gmax", [1, 128], F32)
    if any(t is not None for t in (tp0, tp1, tp2, tp3)):
        with nc.Block() as blk:
            @blk.sync
            def _(sy):
                if tp0 is not None:
                    inc(sy.dma_start(out=tp0[:, :], in_=statT[:, :]), "dO", 16)
                if tp1 is not None:
                    inc(sy.dma_start(out=tp1[:, :], in_=arena1[:, :]), "dO", 16)
                if tp2 is not None:
                    inc(sy.dma_start(out=tp2[:, :], in_=xn1_fm[:, :]), "dO", 16)
                if tp3 is not None:
                    sy.wait_ge(sem("cc"), cget("cc"))
                    inc(sy.dma_start(out=rowm[0:1, :], in_=cc1o[0:1, :]), "dA", 16)
                    sy.wait_ge(sem("dA"), cget("dA"))
                    inc(sy.dma_start(out=tp3[0:1, :], in_=rowm[0:1, :]), "dO", 16)
                sy.wait_ge(sem("dO"), cget("dO"))
        barrier()


    stA.close()
    stArena.close()
    st_early.close()

    stB = ExitStack()
    Wall = sbuf("Wall", [128, 8 * 640], BF16, stB)
    wstg = [sbuf("wstg0", [128, 640], F32, stB),
            sbuf("wstg1", [128, 640], F32, stB)]
    bufQ = sbuf("bufQ", [128, NT], BF16, stB)
    bufK = sbuf("bufK", [128, NT], BF16, stB)
    bufG = sbuf("bufG", [128, NT], BF16, stB)
    bufW1 = sbuf("bufW1", [128, NT], BF16, stB)
    bufW2 = sbuf("bufW2", [128, NT], BF16, stB)
    gcpT = sbuf("gcpT", [128, NT], BF16, stB)
    pexpT = sbuf("pexpT", [128, NT], BF16, stB)
    pcsT = sbuf("pcsT", [128, NT], BF16, stB)
    v0c = sbuf("v0c", [128, NT], BF16, stB)
    v1c = sbuf("v1c", [128, NT], BF16, stB)
    eSb01 = [sbuf("eSb0", [128, 512], BF16, stB),
             sbuf("eSb1", [128, 512], BF16, stB)]
    den2 = sbuf("den2", [128, 512], F32, stB)
    spB = sbuf("spB", [128, 1024], F32, stB)
    nrmB = sbuf("nrmB", [128, 64], F32, stB)
    rqk = sbuf("rqk", [128, 64], F32, stB)
    negpm = sbuf("negpm", [128, 2], F32, stB)
    offF = sbuf("offF", [1, 128], F32, stB)
    offB = sbuf("offB", [1, 128], BF16, stB)
    totF = sbuf("totF", [1, 128], F32, stB)
    cgf = sbuf("cgf", [128, 256], F32, stB)
    brepg = sbuf("brepg", [128, 128], F32, stB)
    clkw = sbuf("clkw", [128, 128], BF16, stB)
    clkstgF = sbuf("clkstgF", [128, 128], F32, stB)

    with nc.Block() as blk:
        @blk.sync
        def _(sy):
            for kc in range(NCH):
                inc(sy.dma_start(out=clkstgF[:, 16 * kc:16 * (kc + 1)],
                                 in_=wclk[128 * kc:128 * (kc + 1), :]), "dCLK", 16)

        @blk.vector
        def _(v):
            v = DW(v)
            v.wait_ge(sem("dCLK"), cget("dCLK"))
            inc(v.tensor_copy(clkw[:, :], clkstgF[:, :]), "vCLK")

    with nc.Block() as blk:
        @blk.tensor
        def _(t):
            t.wait_ge(sem("vCLK"), cget("vCLK"))

    steps = []
    for tau in range(NTAU):
        def t_clk(t, tau=tau):
            for kc in range(NCH):
                mm = t.matmul(psP[:, 0:16],
                              xn1_fm[:, kc * NT + 128 * tau:kc * NT + 128 * (tau + 1)],
                              clkw[:, 16 * kc:16 * (kc + 1)],
                              start=(kc == 0), stop=(kc == NCH - 1))
            return mm
        def v_clk(v, tau=tau):
            return v.tensor_tensor(clockS[:, 16 * tau:16 * (tau + 1)],
                                   psP[:, 0:16], brepc[:, :], ALU.add)
        steps.append(("tensor", t_clk))
        steps.append(("vector", v_clk))
    steps += [
        ("scalar", lambda s: s.activation(spB[:, 0:512], clockS[:, :], ACT.Abs)),
        ("scalar", lambda s: s.activation(den2[:, :], spB[:, 0:512], ACT.Exp,
                                          bias=0.0, scale=-1.0)),
        ("scalar", lambda s: s.activation(spB[:, 0:512], den2[:, :], ACT.Ln,
                                          bias=1.0, scale=1.0)),
        ("vector", lambda v: v.tensor_scalar(den2[:, :], clockS[:, :], 0.0, None,
                                             ALU.max)),
        ("vector", lambda v: v.tensor_tensor(clockS[:, :], den2[:, :],
                                             spB[:, 0:512], ALU.add)),
    ]
    chain(steps)
    barrier()

    def clock_ap(c):
        return AP(clockS, 2 * c, [[NTAU * H, 128], [H, NTAU], [1, 2], [0, 64]])

    COS_AP = AP(cosT2, 0, [[NTAU * 64, 128], [64, NTAU], [0, 2], [1, 64]])
    SIN_AP = AP(sinT, 0, [[NTAU * 64, 128], [64, NTAU], [0, 2], [1, 64]])
    RQK_AP = AP(rqk, 0, [[64, 128], [2, NTAU], [1, 2], [0, 64]])
    NRM_OUT = AP(nrmB, 0, [[64, 128], [2, NTAU], [1, 2]])

    def ev_even(buf):
        return AP(buf, 0, [[NT, 128], [128, NTAU], [64, 2], [2, 32]])

    def ev_odd(buf):
        return AP(buf, 1, [[NT, 128], [128, NTAU], [64, 2], [2, 32]])

    for c in range(NCH):
        wl = "wlB"
        with nc.Block() as blk:
            @blk.gpsimd
            def _(g):
                if c > 0:
                    g.wait_ge(sem("pp_tensor"), cget("pp_tensor"))
                if "oldwl" in DBG:
                    return
                for widx, w in enumerate([wq, wk, wg, wv, wp]):
                    inc(g.dma_start(
                        out=AP(Wall, 128 * widx,
                               [[8 * 640, 128], [640, 8], [1, 128]]),
                        in_=AP(w, 128 * c,
                               [[1024, 128], [128 * 1024, 8], [1, 128]])),
                        wl + "d", 16)

        if "oldwl" in DBG:
            with nc.Block() as blk:
                @blk.sync
                def _(sy):
                    for kc in range(NCH):
                        if kc >= 2:
                            sy.wait_ge(sem(wl + "v"), cget(wl + "v") - 1)
                        for widx, w in enumerate([wq, wk, wg, wv, wp]):
                            inc(sy.dma_start(
                                out=wstg[kc % 2][:, 128 * widx:128 * (widx + 1)],
                                in_=w[128 * kc:128 * (kc + 1),
                                      128 * c:128 * (c + 1)]),
                                wl + f"d{kc % 2}", 16)

                @blk.vector
                def _(v):
                    v = DW(v)
                    for kc in range(NCH):
                        v.wait_ge(sem(wl + f"d{kc % 2}"),
                                  cget(wl + f"d{kc % 2}") - 80 * ((NCH - 1 - kc) // 2))
                        inc(v.tensor_copy(Wall[:, 640 * kc:640 * (kc + 1)],
                                          wstg[kc % 2][:, :]), wl + "v")

        with nc.Block() as blk:
            @blk.tensor
            def _(t):
                if "oldwl" in DBG:
                    t.wait_ge(sem(wl + "v"), cget(wl + "v"))
                else:
                    t.wait_ge(sem(wl + "d"), cget(wl + "d"))
                inc(t.matmul(psR[:, 0:128], onesf[0:1, :],
                             growB[0:1, 128 * c:128 * (c + 1)],
                             start=True, stop=True), "tBG")

            @blk.vector
            def _(v):
                v = DW(v)
                v.wait_ge(sem("tBG"), cget("tBG"))
                v.tensor_copy(brepg[:, :], psR[:, 0:128])
                v.memset(v0c[:, :], 1.0)
                inc(v.memset(v1c[:, :], 1.0), "vBG")

        with nc.Block() as blk:
            @blk.tensor
            def _(t):
                t.wait_ge(sem("vBG"), cget("vBG"))

        steps1 = []
        for tau in range(NTAU):
            ps = psP if tau % 2 == 0 else psQ
            def t_proj(t, tau=tau, ps=ps):
                for kc in range(NCH):
                    mm = t.matmul(ps[:, 0:512],
                                  xn1_fm[:, kc * NT + 128 * tau:kc * NT + 128 * (tau + 1)],
                                  Wall[:, 640 * kc:640 * kc + 512],
                                  start=(kc == 0), stop=(kc == NCH - 1))
                return mm
            def v_ev(v, tau=tau, ps=ps):
                o = 128 * tau
                v.tensor_copy(bufQ[:, o:o + 128], ps[:, 0:128])
                v.tensor_copy(bufK[:, o:o + 128], ps[:, 128:256])
                v.tensor_tensor(bufG[:, o:o + 128], ps[:, 256:384], brepg[:, :],
                                ALU.add)
                v.tensor_copy(v0c[:, o:o + 64], ps[:, 384:448])
                return v.tensor_copy(v1c[:, o + 64:o + 128], ps[:, 448:512])
            ti = len(steps1)
            steps1.append(("tensor", t_proj,
                           [ti - 3] if tau >= 2 else []))
            steps1.append(("vector", v_ev, [ti]))
        chain2(steps1, name="pj")
        barrier()

        chain([
            ("scalar", lambda s: s.activation(bufW1[:, :], bufG[:, :], ACT.Abs)),
            ("scalar", lambda s: s.activation(bufW2[:, :], bufW1[:, :], ACT.Exp,
                                              bias=0.0, scale=-1.0)),
            ("scalar", lambda s: s.activation(bufW1[:, :], bufW2[:, :], ACT.Ln,
                                              bias=1.0, scale=1.0)),
            ("vector", lambda v: v.tensor_scalar(bufW2[:, :], bufG[:, :], 0.0,
                                                 None, ALU.max)),
            ("vector", lambda v: v.scalar_tensor_tensor(bufG[:, :], bufW2[:, :],
                                                        -1.0, bufW1[:, :],
                                                        ALU.mult, ALU.subtract)),
        ])

        def cumsum_chain(srcT, dstT, clip_exp, tag):
            steps = []
            v_idx = {}
            s_idx = {}
            it = 0
            for b in range(BPC):
                for tt in range(16):
                    tau = 16 * b + tt
                    ps = psP if tt % 2 == 0 else psQ
                    cg = cgf[:, 128 * (tt % 2):128 * (tt % 2 + 1)]
                    def t_cs(t, b=b, tt=tt, ps=ps):
                        for tp in range(tt + 1):
                            o = 128 * (16 * b + tp)
                            mm = t.matmul(ps[:, 0:128],
                                          utb[:, :] if tp == tt else onesb[:, :],
                                          srcT[:, o:o + 128],
                                          start=(tp == 0), stop=(tp == tt))
                        return mm
                    ti = len(steps)
                    steps.append(("tensor", t_cs,
                                  [v_idx[it - 2]] if it >= 2 else []))
                    if clip_exp:
                        def v_cs(v, ps=ps, cg=cg):
                            return v.tensor_scalar(cg, ps[:, 0:128], 50.0, -60.0,
                                                   ALU.min, ALU.max)
                        def s_cs(s, tau=tau, cg=cg):
                            o = 128 * tau
                            return s.activation(dstT[:, o:o + 128], cg, ACT.Exp)
                        vi = len(steps)
                        steps.append(("vector", v_cs,
                                      [ti] + ([s_idx[it - 2]] if it >= 2 else [])))
                        s_idx[it] = len(steps)
                        steps.append(("scalar", s_cs, [vi]))
                        v_idx[it] = vi
                    else:
                        def v_cs(v, tau=tau, ps=ps):
                            o = 128 * tau
                            return v.tensor_copy(dstT[:, o:o + 128], ps[:, 0:128])
                        v_idx[it] = len(steps)
                        steps.append(("vector", v_cs, [ti]))
                    it += 1
            chain2(steps, name=tag)
            barrier()

        barrier()
        cumsum_chain(bufG, gcpT, True, "cg")

        steps = []
        vproj = {}
        for m in range(8):
            ps = psP if m % 2 == 0 else psQ
            def t_p(t, m=m, ps=ps):
                for kc in range(NCH):
                    mm = t.matmul(ps[:, 0:512],
                                  Wall[:, 640 * kc + 512:640 * (kc + 1)],
                                  xn1_fm[:, kc * NT + 512 * m:kc * NT + 512 * (m + 1)],
                                  start=(kc == 0), stop=(kc == NCH - 1))
                return mm
            def v_p(v, m=m, ps=ps):
                return v.tensor_copy(bufW1[:, 512 * m:512 * (m + 1)], ps[:, 0:512])
            ti = len(steps)
            steps.append(("tensor", t_p, [vproj[m - 2]] if m >= 2 else []))
            steps.append(("vector", v_p, [ti]))
            vproj[m] = ti + 1
        spe = {}
        for b in range(BPC):
            def v_pm(v, b=b):
                return v.reduce_max(negpm[:, b:b + 1], bufW1[:, T * b:T * (b + 1)],
                                    AX.X, negate=True)
            def s_pe(s, b=b):
                return s.activation(bufW2[:, T * b:T * (b + 1)],
                                    bufW1[:, T * b:T * (b + 1)], ACT.Exp,
                                    bias=negpm[:, b:b + 1], scale=1.0)
            vi = len(steps)
            steps.append(("vector", v_pm, [vproj[4 * b + 3]]))
            steps.append(("scalar", s_pe, [vi]))
            spe[b] = vi + 1
        vtr = {}
        for g in range(8):
            bank = psA if g % 2 == 0 else psB
            def t_tr(t, g=g, bank=bank):
                for u in range(4):
                    tau = 4 * g + u
                    tr = t.transpose(bank[:, 128 * u:128 * (u + 1)],
                                     bufW2[:, 128 * tau:128 * (tau + 1)], idbf[:, :])
                return tr
            def v_tr(v, g=g, bank=bank):
                return v.tensor_copy(bufW1[:, 512 * g:512 * (g + 1)],
                                     bank[:, 0:512])
            ti = len(steps)
            deps = [spe[g // 4]]
            if g >= 2:
                deps.append(vtr[g - 2])
            steps.append(("tensor", t_tr, deps))
            steps.append(("vector", v_tr, [ti]))
            vtr[g] = ti + 1
        steps.append(("vector", lambda v: v.tensor_tensor(
            pexpT[:, :], bufW1[:, :], clock_ap(c), ALU.mult)), )
        steps[-1] = ("vector", steps[-1][0] if False else steps[-1][1], [vtr[7]])
        chain2(steps, name="pp")
        barrier()

        cumsum_chain(pexpT, pcsT, False, "cp")

        HL = NT // 2

        def rqk_h(h):
            return AP(rqk, 32 * h, [[64, 128], [2, 16], [1, 2], [0, 64]])

        def sin_h(h):
            return AP(sinT, 64 * 16 * h,
                      [[NTAU * 64, 128], [64, 16], [0, 2], [1, 64]])

        def cos_h(h):
            return AP(cosT2, 64 * 16 * h,
                      [[NTAU * 64, 128], [64, 16], [0, 2], [1, 64]])

        def evn_h(buf, h, odd):
            return AP(buf, HL * h + odd,
                      [[NT, 128], [128, 16], [64, 2], [2, 32]])

        def qk_prep(bufX, wnum, wden):
            steps = []

            def addp(eng, fn, deps):
                steps.append((eng, fn, [d for d in deps if d is not None]))
                return len(steps) - 1

            def split(fn_of_h, deps_v, deps_g):
                iv = addp("vector", lambda v, f=fn_of_h: f(v, 0), deps_v)
                ig = addp("gpsimd", lambda g, f=fn_of_h: f(g, 1), deps_g)
                return iv, ig

            def hs(h):
                return slice(HL * h, HL * (h + 1))

            i1v, i1g = split(
                lambda e, h: e.tensor_tensor(bufW1[:, hs(h)], bufX[:, hs(h)],
                                             bufX[:, hs(h)], ALU.mult), [], [])
            i2 = addp("vector", lambda v: v.reduce_sum(
                NRM_OUT, AP(bufW1, 0, [[NT, 128], [128, NTAU], [64, 2],
                                       [1, 64]]), AX.X), [i1g])
            addp("vector", lambda v: v.tensor_scalar(nrmB[:, :], nrmB[:, :],
                                                     1e-24, None, ALU.max), [])
            i4 = addp("scalar", lambda s: s.activation(den2[:, 0:64],
                                                       nrmB[:, :], ACT.Ln),
                      [len(steps) - 1])
            i5 = addp("scalar", lambda s: s.activation(rqk[:, :], den2[:, 0:64],
                                                       ACT.Exp, bias=0.0,
                                                       scale=-0.5), [])
            i6v, i6g = split(
                lambda e, h: e.tensor_tensor(bufW2[:, hs(h)], bufX[:, hs(h)],
                                             rqk_h(h), ALU.mult), [i5], [i5])
            i7v, i7g = split(
                lambda e, h: e.tensor_copy(evn_h(bufW1, h, 0),
                                           evn_h(bufW2, h, 1)),
                [i6g], [i2, i6v])
            i8v, i8g = split(
                lambda e, h: e.tensor_copy(evn_h(bufW1, h, 1),
                                           evn_h(bufW2, h, 0)), [], [])
            i9v, i9g = split(
                lambda e, h: e.tensor_tensor(bufX[:, hs(h)], bufW1[:, hs(h)],
                                             sin_h(h), ALU.mult), [i8g], [i8v])
            i10v, i10g = split(
                lambda e, h: e.tensor_tensor(bufW1[:, hs(h)], bufW2[:, hs(h)],
                                             cos_h(h), ALU.mult), [], [])
            i11v, i11g = split(
                lambda e, h: e.tensor_tensor(bufW2[:, hs(h)], bufX[:, hs(h)],
                                             bufW1[:, hs(h)], ALU.add),
                [i10g], [i10v])
            i12 = addp("vector", lambda v: v.tensor_scalar(
                bufW1[:, :], wden[:, :], 1e-8, None, ALU.add), [i11g])

            imult = {}
            for m in range(4):
                def sL(s, m=m):
                    s.activation(spB[:, :], bufW1[:, 1024 * m:1024 * (m + 1)],
                                 ACT.Ln)
                    return s.activation(den2[:, :], spB[:, 0:512], ACT.Exp,
                                        bias=0.0, scale=-1.0)
                iL = addp("scalar", sL, [i12, imult.get(m - 1)])
                def sE(s):
                    return s.activation(spB[:, 0:512], spB[:, 512:1024],
                                        ACT.Exp, bias=0.0, scale=-1.0)
                iE = addp("scalar", sE, [])
                def vm(e, m=m):
                    e.tensor_tensor(bufW1[:, 1024 * m:1024 * m + 512],
                                    wnum[:, 1024 * m:1024 * m + 512],
                                    den2[:, :], ALU.mult)
                    return e.tensor_tensor(
                        bufW1[:, 1024 * m + 512:1024 * (m + 1)],
                        wnum[:, 1024 * m + 512:1024 * (m + 1)],
                        spB[:, 0:512], ALU.mult)
                eng = "vector" if m % 2 == 0 else "gpsimd"
                imult[m] = addp(eng, vm, [iE, i12, imult.get(m - 2)])
            i13v, i13g = split(
                lambda e, h: e.tensor_tensor(bufX[:, hs(h)], bufW2[:, hs(h)],
                                             bufW1[:, hs(h)], ALU.mult),
                [imult[0], imult[1], i11g],
                [imult[2], imult[3], i11v])
            chain2(steps, name="qp")

        barrier()
        qk_prep(bufQ, gcpT, pcsT)
        barrier()
        qk_prep(bufK, pexpT, gcpT)
        barrier()

        steps = []
        vtr2 = {}
        for part in range(2):
            srcb = bufQ if part == 0 else bufK
            dstb = bufW1 if part == 0 else bufW2
            for g in range(8):
                gg = 8 * part + g
                bank = psA if gg % 2 == 0 else psB
                def t_t2(t, g=g, srcb=srcb, bank=bank):
                    for u in range(4):
                        tau = 4 * g + u
                        tr = t.transpose(bank[:, 128 * u:128 * (u + 1)],
                                         srcb[:, 128 * tau:128 * (tau + 1)],
                                         idbf[:, :])
                    return tr
                def v_t2(v, g=g, dstb=dstb, bank=bank):
                    return v.tensor_copy(dstb[:, 512 * g:512 * (g + 1)],
                                         bank[:, 0:512])
                ti = len(steps)
                steps.append(("tensor", t_t2, [vtr2[gg - 2]] if gg >= 2 else []))
                steps.append(("vector", v_t2, [ti]))
                vtr2[gg] = ti + 1
        chain2(steps, name="t2")
        barrier()

        qfm, kfm, outfm = bufW1, bufW2, bufG
        psS = [psS1, psS2]
        steps = []

        def add(eng, fn, deps):
            steps.append((eng, fn, [d for d in deps if d is not None]))
            return len(steps) - 1

        last_psS_rd = [None, None]
        last_pv_eS = [None, None]
        last_spB_rd = [None, None]
        last_vout = [None]
        last_vout_bank = [None, None]
        pending_den = [None]
        psOx = [psO, psP]
        mcount = 0

        for b in range(BPC):
            for hl in range(2):
                vtm = v0c if hl == 0 else v1c
                for m in range(4):
                    ob = mcount % 2
                    psOb = psOx[ob]
                    mcount += 1
                    qc0 = T * b + 512 * m
                    nj = 4 * m + 4
                    iex = {}

                    def mk_pv(j, b=b, nj=nj, vtm=vtm, psOb=psOb, ob=ob):
                        eS = eSb01[j % 2]
                        def t_pv(t, j=j, eS=eS):
                            return t.matmul(
                                psOb[:, 0:512],
                                vtm[:, 128 * (16 * b + j):128 * (16 * b + j + 1)],
                                eS[:, :],
                                start=(j == 0), stop=(j == nj - 1))
                        deps = [iex[j]]
                        if j == 0:
                            deps.append(last_vout_bank[ob])
                        ipv = add("tensor", t_pv, deps)
                        last_pv_eS[j % 2] = ipv
                        return ipv

                    for j in range(nj):
                        p = j % 2
                        hp = slice(512 * p, 512 * (p + 1))
                        eS = eSb01[p]

                        def t_qk(t, b=b, hl=hl, qc0=qc0, j=j, p=p):
                            return t.matmul(
                                psS[p][:, 0:512],
                                kfm[64 * hl:64 * hl + 64,
                                    T * b + 128 * j:T * b + 128 * (j + 1)],
                                qfm[64 * hl:64 * hl + 64, qc0:qc0 + 512],
                                start=True, stop=True)
                        iqk = add("tensor", t_qk, [last_psS_rd[p]])

                        if j >= 4 * m:
                            def v_cm(v, p=p, hp=hp, mv=j - 4 * m):
                                return v.scalar_tensor_tensor(
                                    spB[:, hp], psS[p][:, 0:512], 500.0,
                                    maskadd[:, 512 * mv:512 * (mv + 1)],
                                    ALU.min, ALU.add)
                            icm = add("vector", v_cm, [iqk, last_spB_rd[p]])
                            last_psS_rd[p] = icm

                            def s_ex(s, eS=eS, hp=hp):
                                return s.activation(eS[:, :], spB[:, hp],
                                                    ACT.Exp, bias=0.0,
                                                    scale=0.125)
                            iex[j] = add("scalar", s_ex,
                                         [icm, last_pv_eS[p]])
                            last_spB_rd[p] = iex[j]
                        else:
                            def s_ex(s, eS=eS, p=p):
                                return s.activation(eS[:, :], psS[p][:, 0:512],
                                                    ACT.Exp, bias=0.0,
                                                    scale=0.125)
                            iex[j] = add("scalar", s_ex,
                                         [iqk, last_pv_eS[p]])
                            last_psS_rd[p] = iex[j]

                        if j == 1 and pending_den[0] is not None:
                            pending_den[0]()
                            pending_den[0] = None
                        if j >= 1:
                            ipv = mk_pv(j - 1)
                    ipv_last = mk_pv(nj - 1)

                    def den_emit(hl=hl, qc0=qc0, ipv_last=ipv_last,
                                 psOb=psOb, ob=ob):
                        global_last = last_vout[0]
                        if hl == 0:
                            def s_rc(s):
                                s.activation(den2[64:65, :], psOb[64:65, 0:512],
                                             ACT.Ln)
                                return s.activation(den2[64:65, :],
                                                    den2[64:65, :],
                                                    ACT.Exp, bias=0.0,
                                                    scale=-1.0)
                            def t_bc(t):
                                return t.matmul(psR[0:64, :],
                                                onesf[64:65, 0:64],
                                                den2[64:65, :], start=True,
                                                stop=True,
                                                tile_position=(64, 0))
                            def v_out(v, qc0=qc0, psOb=psOb):
                                v.tensor_copy(den2[0:64, 0:512], psR[0:64, :])
                                return v.tensor_tensor(
                                    outfm[0:64, qc0:qc0 + 512],
                                    psOb[0:64, 0:512],
                                    den2[0:64, 0:512], ALU.mult)
                        else:
                            def s_rc(s):
                                s.activation(den2[0:1, :], psOb[0:1, 0:512],
                                             ACT.Ln)
                                return s.activation(den2[0:1, :], den2[0:1, :],
                                                    ACT.Exp, bias=0.0,
                                                    scale=-1.0)
                            def t_bc(t):
                                return t.matmul(psR[64:128, :],
                                                onesf[0:1, 0:64],
                                                den2[0:1, :], start=True,
                                                stop=True,
                                                tile_position=(0, 64))
                            def v_out(v, qc0=qc0, psOb=psOb):
                                v.tensor_copy(den2[64:128, 0:512],
                                              psR[64:128, :])
                                return v.tensor_tensor(
                                    outfm[64:128, qc0:qc0 + 512],
                                    psOb[64:128, 0:512],
                                    den2[64:128, 0:512], ALU.mult)
                        irc = add("scalar", s_rc, [ipv_last])
                        itb = add("tensor", t_bc, [irc, global_last])
                        ivo = add("vector", v_out, [itb])
                        last_vout[0] = ivo
                        last_vout_bank[ob] = ivo
                        return ivo

                    pending_den[0] = den_emit
        if pending_den[0] is not None:
            pending_den[0]()
            pending_den[0] = None
        chain2(steps, name="at")

        with nc.Block() as blk:
            @blk.sync
            def _(sy):
                sy.wait_ge(sem("at_vector"), cget("at_vector"))
                inc(sy.dma_start(out=out_spill[128 * c:128 * (c + 1), :],
                                 in_=outfm[:, :]), "dO", 16)
                sy.wait_ge(sem("dO"), cget("dO"))

        if c == 0:
            tb = {nm: tap(nm, [128, NT], BF16)
                  for nm in ("qfm0", "kfm0", "gcp0", "pexp0", "pcs0", "outfm0",
                             "v0c0")}
            srcs = {"qfm0": bufW1, "kfm0": bufW2, "gcp0": gcpT, "pexp0": pexpT,
                    "pcs0": pcsT, "outfm0": bufG, "v0c0": v0c}
            if any(t_ is not None for t_ in tb.values()):
                with nc.Block() as blk:
                    @blk.sync
                    def _(sy):
                        sy.wait_ge(sem("at_vector"), cget("at_vector"))
                        for nm, t_ in tb.items():
                            if t_ is not None:
                                inc(sy.dma_start(out=t_[:, :], in_=srcs[nm][:, :]),
                                    "dO", 16)
                        sy.wait_ge(sem("dO"), cget("dO"))
        barrier()

    stB.close()



    if "stopB" in DBG:
        stZ2 = ExitStack()
        zb2 = sbuf("zb2", [128, D], F32, stZ2)
        with nc.Block() as blk:
            @blk.vector
            def _(v):
                inc(v.memset(zb2[:, :], 0.0), "vZ2")

            @blk.sync
            def _(sy):
                sy.wait_ge(sem("vZ2"), cget("vZ2"))
                for tau in range(NTAU):
                    inc(sy.dma_start(out=y_out[128 * tau:128 * (tau + 1), :],
                                     in_=zb2[:, :]), "dO", 16)
                sy.wait_ge(sem("dO"), cget("dO"))
        stZ2.close()
        stB.close()
        stTab.close()
        stXN1.close()
        stack.close()
        return nc, taps

    stP = ExitStack()
    stgF = [sbuf("stgF0", [128, MH], F32, stP), sbuf("stgF1", [128, IFF], F32, stP)]
    sA_ = sbuf("sA_", [128, IFF], F32, stP)
    sB_ = sbuf("sB_", [128, IFF], F32, stP)
    sI_ = sbuf("sI_", [128, IFF], I32, stP)
    sG_ = sbuf("sG_", [128, IFF], BF16, stP)
    wI_ = sbuf("wI_", [128, IFF], BF16, stP)
    wT_ = sbuf("wT_", [128, IFF], BF16, stP)
    rmx = sbuf("rmx", [128, 1], F32, stP)

    barrier()

    wT2_ = sbuf("wT2_", [128, IFF], BF16, stP)
    wTd = [wT_, wT2_]

    qw_specs = ([(wgate, rc, D, scg, wgT_sp) for rc in range(NIC)] +
                [(wup, rc, D, scu, wuT_sp) for rc in range(NIC)] +
                [(wdown, rc, IFF, scd, wdT_sp) for rc in range(NCH)])
    steps = []

    def addq(eng, fn, deps):
        steps.append((eng, fn, [d for d in deps if d is not None]))
        return len(steps) - 1

    i_in, i_sc, i_ab, i_rd, i_out = {}, {}, {}, {}, {}
    i_ttr, i_vtr = {}, {}
    last_vtr_bank = [None, None]
    gctr = 0
    for it, (wsrc, rc, ncol, sccol, dst_sp) in enumerate(qw_specs):
        stg = stgF[it % 2]
        wT = wTd[it % 2]
        ncc = ncol // 128

        def g_in(g, wsrc=wsrc, rc=rc, ncol=ncol, stg=stg):
            inc(g.dma_start(out=stg[:, 0:ncol],
                            in_=wsrc[128 * rc:128 * (rc + 1), :]), "dQW", 16)
            g.wait_ge(sem("dQW"), cget("dQW"))
            return g.nop()
        i_in[it] = addq("gpsimd", g_in, [i_sc.get(it - 2)])

        def v_scale(v, rc=rc, ncol=ncol, sccol=sccol, stg=stg):
            v.reduce_max(rmx[:, :], stg[:, 0:ncol], AX.X,
                         apply_absolute_value=True)
            v.tensor_scalar(sccol[:, rc:rc + 1], rmx[:, :], 1.0 / 127.0,
                            None, ALU.mult)
            v.reciprocal(rmx[:, :], rmx[:, :])
            v.tensor_scalar(rmx[:, :], rmx[:, :], 127.0, None, ALU.mult)
            return v.tensor_scalar(sA_[:, 0:ncol], stg[:, 0:ncol],
                                   rmx[:, 0:1], 128.5, ALU.mult, ALU.add)
        i_sc[it] = addq("vector", v_scale, [i_in[it]])

        def v_round(v, ncol=ncol):
            v.tensor_copy(sI_[:, 0:ncol], sA_[:, 0:ncol])
            v.scalar_tensor_tensor(sB_[:, 0:ncol], sI_[:, 0:ncol], 1.0,
                                   sA_[:, 0:ncol], ALU.mult, ALU.is_le)
            v.tensor_tensor(sA_[:, 0:ncol], sI_[:, 0:ncol], sB_[:, 0:ncol],
                            ALU.add)
            return v.tensor_scalar(wI_[:, 0:ncol], sA_[:, 0:ncol], -129.0,
                                   None, ALU.add)
        i_rd[it] = addq("vector", v_round,
                        [i_sc[it], i_ttr.get((it - 1, -1))])

        ngrp = (ncc + 3) // 4
        for gi, g0 in enumerate(range(0, ncc, 4)):
            gw = min(4, ncc - g0)
            bank = psA if gctr % 2 == 0 else psB

            def t_tr(t, g0=g0, gw=gw, bank=bank):
                for u in range(gw):
                    tr = t.transpose(bank[:, 128 * u:128 * (u + 1)],
                                     wI_[:, 128 * (g0 + u):128 * (g0 + u + 1)],
                                     idbf[:, :])
                return tr
            i_ttr[(it, gi)] = addq("tensor", t_tr,
                                   [i_rd[it], last_vtr_bank[gctr % 2]])

            def v_tr(s, g0=g0, gw=gw, bank=bank, wT=wT):
                return s.activation(wT[:, 128 * g0:128 * (g0 + gw)],
                                    bank[:, 0:128 * gw], ACT.Copy)
            deps = [i_ttr[(it, gi)]]
            if gi == 0:
                deps.append(i_out.get(it - 2))
            i_vtr[(it, gi)] = addq("scalar", v_tr, deps)
            last_vtr_bank[gctr % 2] = i_vtr[(it, gi)]
            gctr += 1
        i_ttr[(it, -1)] = i_ttr[(it, ngrp - 1)]

        def sy_out(sy, rc=rc, ncc=ncc, dst_sp=dst_sp, wT=wT):
            for cc in range(ncc):
                inc(sy.dma_start(
                    out=dst_sp[128 * cc:128 * (cc + 1),
                               128 * rc:128 * (rc + 1)],
                    in_=wT[:, 128 * cc:128 * (cc + 1)]), "dQWo", 16)
            sy.wait_ge(sem("dQWo"), cget("dQWo"))
            return sy.nop()
        i_out[it] = addq("sync", sy_out, [i_vtr[(it, ngrp - 1)]])
    chain2(steps, name="qw")
    barrier()
    stP.close()
    stTab.close()
    stXN1.close()

    stC1 = ExitStack()
    x1fm = sbuf("x1fm", [128, NCH * NT], BF16, stC1)
    stC2 = ExitStack()
    outR = sbuf("outR", [128, NCH * NT], BF16, stC2)
    woS = sbuf("woS", [128, NCH * 1024], BF16, stC2)
    wstgC = sbuf("wstgC", [128, 1024], F32, stC2)
    xnstg = sbuf("xnstg", [128, NT], BF16, stC2)
    spC = sbuf("spC", [128, 512], F32, stC2)

    with nc.Block() as blk:
        @blk.sync
        def _(sy):
            for cc in range(NCH):
                inc(sy.dma_start(out=outR[:, NT * cc:NT * (cc + 1)],
                                 in_=out_spill[128 * cc:128 * (cc + 1), :]),
                    "dC1", 16)

        @blk.gpsimd
        def _(g):
            for kc in range(NCH):
                inc(g.dma_start(out=woS[:, 1024 * kc:1024 * (kc + 1)],
                                in_=wo[128 * kc:128 * (kc + 1), :]), "dC2", 16)

    with nc.Block() as blk:
        @blk.sync
        def _(sy):
            sy.wait_ge(sem("dC1"), cget("dC1"))
            for oc in range(NCH):
                if oc >= 1:
                    sy.wait_ge(sem("vC2"), oc)
                inc(sy.dma_start(out=xnstg[:, :],
                                 in_=xn_spill[128 * oc:128 * (oc + 1), :]),
                    "dC3", 16)

        @blk.tensor
        def _(t):
            t.wait_ge(sem("dC2"), cget("dC2"))
            t.wait_ge(sem("dC1"), cget("dC1"))
            for oc in range(NCH):
                for m in range(8):
                    idx = 8 * oc + m
                    if idx >= 2:
                        t.wait_ge(sem("vC2x"), idx - 1)
                    ps = psP if m % 2 == 0 else psQ
                    for kc in range(NCH):
                        mm = t.matmul(
                            ps[:, 0:512],
                            woS[:, 1024 * kc + 128 * oc:1024 * kc + 128 * (oc + 1)],
                            outR[:, NT * kc + 512 * m:NT * kc + 512 * (m + 1)],
                            start=(kc == 0), stop=(kc == NCH - 1))
                    inc(mm, "tC1")

        @blk.vector
        def _(v):
            v = DW(v)
            for oc in range(NCH):
                v.wait_ge(sem("dC3"), 16 * (oc + 1))
                for m in range(8):
                    idx = 8 * oc + m
                    v.wait_ge(sem("tC1"), idx + 1)
                    ps = psP if m % 2 == 0 else psQ
                    v.tensor_scalar(spC[:, 0:512], ps[:, 0:512],
                                    bocol[:, oc:oc + 1], None, ALU.add)
                    last = v.tensor_tensor(
                        x1fm[:, NT * oc + 512 * m:NT * oc + 512 * (m + 1)],
                        spC[:, 0:512], xnstg[:, 512 * m:512 * (m + 1)], ALU.add)
                    inc(last, "vC2x")
                inc(v.memset(spC[0:1, 0:1], 0.0), "vC2")
    barrier()
    stC2.close()

    stC3 = ExitStack()
    w2S = sbuf("w2S", [128, NMH * 1024], BF16, stC3)
    w1blk = sbuf("w1blk", [128, 2048], BF16, stC3)
    hact = sbuf("hact", [128, NMH * 512], BF16, stC3)
    h0m = sbuf("h0m", [128, NCH * 512], BF16, stC3)
    sqm = sbuf("sqm", [128, NCH * 512], BF16, stC3)
    sm1 = sbuf("sm1", [128, 512], F32, stC3)
    sm2 = sbuf("sm2", [128, 512], F32, stC3)
    sm3 = sbuf("sm3", [128, 512], F32, stC3)
    wstgC2 = sbuf("wstgC2", [128, 1024], F32, stC3)
    sOut = sbuf("sOut", [128, 1024], BF16, stC3)

    with nc.Block() as blk:
        @blk.sync
        def _(sy):
            for hc in range(NMH):
                if hc >= 1:
                    sy.wait_ge(sem("vC3"), hc)
                inc(sy.dma_start(out=wstgC2[:, :],
                                 in_=w2[128 * hc:128 * (hc + 1), :]), "dC4", 16)

        @blk.vector
        def _(v):
            v = DW(v)
            for hc in range(NMH):
                v.wait_ge(sem("dC4"), 16 * (hc + 1))
                inc(v.tensor_copy(w2S[:, 1024 * hc:1024 * (hc + 1)],
                                  wstgC2[:, :]), "vC3")

    for m in range(8):
        steps = []

        def add(eng, fn, deps):
            steps.append((eng, fn, [d for d in deps if d is not None]))
            return len(steps) - 1

        def t_mu(t, m=m):
            for kc in range(NCH):
                mm = t.matmul(psQ[0:1, 0:512], onesb[:, 0:1],
                              x1fm[:, NT * kc + 512 * m:NT * kc + 512 * (m + 1)],
                              start=(kc == 0), stop=(kc == NCH - 1))
            return mm
        i_tmu = add("tensor", t_mu, [])

        def v_sq(v, m=m):
            for kc in range(NCH):
                cp = v.tensor_tensor(
                    sqm[:, 512 * kc:512 * (kc + 1)],
                    x1fm[:, NT * kc + 512 * m:NT * kc + 512 * (m + 1)],
                    x1fm[:, NT * kc + 512 * m:NT * kc + 512 * (m + 1)], ALU.mult)
            return cp
        i_vsq = add("vector", v_sq, [])

        def t_sq(t, m=m):
            for kc in range(NCH):
                mm = t.matmul(psR[0:1, 0:512], onesb[:, 0:1],
                              sqm[:, 512 * kc:512 * (kc + 1)],
                              start=(kc == 0), stop=(kc == NCH - 1))
            return mm
        i_tsq = add("tensor", t_sq, [i_vsq])

        def v_stats(v, m=m):
            v.tensor_scalar(sm1[0:1, :], psQ[0:1, 0:512], 1.0 / D, None, ALU.mult)
            v.tensor_scalar(sm2[0:1, :], psR[0:1, 0:512], 1.0 / D, None, ALU.mult)
            v.tensor_tensor(sm3[0:1, :], sm1[0:1, :], sm1[0:1, :], ALU.mult)
            return v.tensor_tensor(sm2[0:1, :], sm2[0:1, :], sm3[0:1, :],
                                   ALU.subtract)
        i_vst = add("vector", v_stats, [i_tmu, i_tsq])
        i_srs = add("scalar",
                    lambda s: s.activation(sm3[0:1, :], sm2[0:1, :], ACT.Sqrt,
                                           bias=kcol[0:1, 0:1], scale=1.0),
                    [i_vst])
        i_vrs = add("vector", lambda v: v.reciprocal(sm2[0:1, :], sm3[0:1, :]),
                    [i_srs])

        def t_bc(t, m=m):
            t.matmul(psQ[:, 0:512], onesf[0:1, :], sm1[0:1, :],
                     start=True, stop=True)
            return t.matmul(psR[:, 0:512], onesf[0:1, :], sm2[0:1, :],
                            start=True, stop=True)
        i_tbc = add("tensor", t_bc, [i_vrs])

        def v_bc(v, m=m):
            v.tensor_copy(sm1[:, :], psQ[:, 0:512])
            return v.tensor_copy(sm2[:, :], psR[:, 0:512])
        i_vbc = add("vector", v_bc, [i_tbc])

        def v_nrm(v, m=m):
            for kc in range(NCH):
                v.tensor_tensor(sm3[:, :],
                                x1fm[:, NT * kc + 512 * m:NT * kc + 512 * (m + 1)],
                                sm1[:, :], ALU.subtract)
                v.tensor_tensor(sm3[:, :], sm3[:, :], sm2[:, :], ALU.mult)
                cp = v.tensor_scalar(h0m[:, 512 * kc:512 * (kc + 1)], sm3[:, :],
                                     g2col[:, kc:kc + 1], b2col[:, kc:kc + 1],
                                     ALU.mult, ALU.add)
            return cp
        i_vnrm = add("vector", v_nrm, [i_vbc])

        i_sy = {}
        i_t1 = {}
        i_s1 = {}
        for hc in range(NMH):
            wslc = slice(1024 * (hc % 2), 1024 * (hc % 2 + 1))

            def sy_w1(sy, m=m, hc=hc, wslc=wslc):
                if hc == 0:
                    sy.wait_ge(sem("dW1c"), 128)
                inc(sy.dma_start(
                    out=w1blk[:, wslc],
                    in_=AP(w1bf, hc * NCH * 128 * 128,
                           [[128, 128], [128 * 128, NCH], [1, 128]])), "dW1b", 16)
                sy.wait_ge(sem("dW1b"), cget("dW1b"))
                return sy.nop()
            i_sy[hc] = add("sync", sy_w1, [i_t1.get(hc - 2)])

            def t_m1(t, m=m, hc=hc, wslc=wslc):
                ps = psP if hc % 2 == 0 else psQ
                for kc in range(NCH):
                    mm = t.matmul(ps[:, 0:512],
                                  w1blk[:, wslc][:, 128 * kc:128 * (kc + 1)],
                                  h0m[:, 512 * kc:512 * (kc + 1)],
                                  start=(kc == 0), stop=(kc == NCH - 1))
                return mm
            i_t1[hc] = add("tensor", t_m1,
                           [i_sy[hc], i_vnrm, i_s1.get(hc - 2)])

            def s_m1(s, m=m, hc=hc):
                ps = psP if hc % 2 == 0 else psQ
                return s.activation(hact[:, 512 * hc:512 * (hc + 1)],
                                    ps[:, 0:512], ACT.Gelu,
                                    bias=bm1col[:, hc:hc + 1], scale=1.0)
            i_s1[hc] = add("scalar", s_m1, [i_t1[hc]])

        i_t2 = {}
        i_v2 = {}
        i_sp = {}
        for oc in range(NCH):
            oslc = slice(512 * (oc % 2), 512 * (oc % 2 + 1))

            def t_m2(t, m=m, oc=oc):
                ps = psS1 if oc % 2 == 0 else psS2
                for hc in range(NMH):
                    mm = t.matmul(
                        ps[:, 0:512],
                        w2S[:, 1024 * hc + 128 * oc:1024 * hc + 128 * (oc + 1)],
                        hact[:, 512 * hc:512 * (hc + 1)],
                        start=(hc == 0), stop=(hc == NMH - 1))
                return mm
            i_t2[oc] = add("tensor", t_m2,
                           [i_s1[NMH - 1], i_v2.get(oc - 2)])

            def v_m2(v, m=m, oc=oc, oslc=oslc):
                ps = psS1 if oc % 2 == 0 else psS2
                v.tensor_scalar(sm3[:, :], ps[:, 0:512], bm2col[:, oc:oc + 1],
                                None, ALU.add)
                return v.tensor_tensor(
                    sOut[:, oslc], sm3[:, :],
                    x1fm[:, NT * oc + 512 * m:NT * oc + 512 * (m + 1)], ALU.add)
            i_v2[oc] = add("vector", v_m2, [i_t2[oc], i_sp.get(oc - 2)])

            def sy_sp(sy, m=m, oc=oc, oslc=oslc):
                inc(sy.dma_start(out=s_spill[128 * oc:128 * (oc + 1),
                                             512 * m:512 * (m + 1)],
                                 in_=sOut[:, oslc]), "dSo", 16)
                sy.wait_ge(sem("dSo"), cget("dSo"))
                return sy.nop()
            i_sp[oc] = add("sync", sy_sp, [i_v2[oc]])
        chain2(steps, name="c2m")
        barrier()
    barrier()
    stC3.close()

    tcx = tap("x1fm", [128, NCH * NT], BF16)
    if tcx is not None:
        with nc.Block() as blk:
            @blk.sync
            def _(sy):
                inc(sy.dma_start(out=tcx[:, :], in_=x1fm[:, :]), "dO", 16)
                sy.wait_ge(sem("dO"), cget("dO"))
    stC1.close()

    stD = ExitStack()
    xqA = sbuf("xqA", [128, NCH * NT], BF16, stD)
    stD1 = ExitStack()
    wgS = sbuf("wgS", [128, NCH * IFF], BF16, stD1)
    wuS = sbuf("wuS", [128, NCH * IFF], BF16, stD1)
    xnstg2 = sbuf("xnstg2", [128, NT], BF16, stD1)
    dq1 = sbuf("dq1", [128, 512], F32, stD1)
    dq2 = sbuf("dq2", [128, 512], F32, stD1)
    dqI = sbuf("dqI", [128, 512], I32, stD1)
    dqS = sbuf("dqS", [128, 512], BF16, stD1)
    svg = sbuf("svg", [128, NIC], F32, stD1)
    svu = sbuf("svu", [128, NIC], F32, stD1)
    clpS = sbuf("clpS", [1, 2], F32, stD1)
    midT = sbuf("midT", [128, 1024], BF16, stD1)
    sgT = sbuf("sgT", [128, 1024], BF16, stD1)
    dmx = sbuf("dmx", [128, 1], F32, stD1)

    with nc.Block() as blk:
        @blk.sync
        def _(sy):
            sy.wait_ge(sem("cc"), cget("cc"))
            inc(sy.dma_start(out=rowm[0:1, :], in_=cc1o[0:1, :]), "dD0", 16)
            inc(sy.dma_start(out=clpS[0:1, 0:1], in_=clipx.ap()), "dD0", 16)
            inc(sy.dma_start(out=clpS[0:1, 1:2], in_=clipd.ap()), "dD0", 16)

        @blk.scalar
        def _(s):
            s = DW(s)
            s.wait_ge(sem("dD0"), cget("dD0"))
            inc(s.activation(clpS[0:1, 0:1], clpS[0:1, 0:1], ACT.Exp,
                             bias=0.0, scale=-1.0), "sD0")

        @blk.vector
        def _(v):
            v = DW(v)
            v.wait_ge(sem("sD0"), cget("sD0"))
            v.tensor_scalar(clpS[0:1, 0:1], clpS[0:1, 0:1], 1.0, None, ALU.add)
            v.reciprocal(clpS[0:1, 0:1], clpS[0:1, 0:1])
            v.tensor_tensor(ccrow[0:1, 1:2], clpS[0:1, 0:1], rowm[0:1, 0:1],
                            ALU.mult)
            v.tensor_scalar(ccrow[0:1, 1:2], ccrow[0:1, 1:2], 1.0 / 127.0, None,
                            ALU.mult)
            inc(v.reciprocal(ccrow[0:1, 2:3], ccrow[0:1, 1:2]), "vD0")

        @blk.tensor
        def _(t):
            t.wait_ge(sem("vD0"), cget("vD0"))
            t.matmul(psR[:, 0:1], onesf[0:1, :], ccrow[0:1, 1:2],
                     start=True, stop=True)
            inc(t.matmul(psR[:, 1:2], onesf[0:1, :], ccrow[0:1, 2:3],
                         start=True, stop=True), "tD0")

    with nc.Block() as blk:
        @blk.vector
        def _(v):
            v = DW(v)
            v.wait_ge(sem("tD0"), cget("tD0"))
            v.tensor_copy(sxv[:, :], psR[:, 0:1])
            v.tensor_copy(rsxv[:, :], psR[:, 1:2])
            v.tensor_scalar(svg[:, :], scg[:, :], sxv[:, 0:1], None, ALU.mult)
            inc(v.tensor_scalar(svu[:, :], scu[:, :], sxv[:, 0:1], None,
                                ALU.mult), "vD1")

        @blk.gpsimd
        def _(g):
            for kc in range(NCH):
                inc(g.dma_start(out=wgS[:, IFF * kc:IFF * (kc + 1)],
                                in_=wgT_sp[128 * kc:128 * (kc + 1), :]), "dDg", 16)
                inc(g.dma_start(out=wuS[:, IFF * kc:IFF * (kc + 1)],
                                in_=wuT_sp[128 * kc:128 * (kc + 1), :]), "dDg", 16)

    def quant_tile(v, srcap, dstap, rvec, do_clip):
        v.tensor_scalar(dq1[:, :], srcap, rvec, None, ALU.mult)
        return None

    steps = []
    for cc in range(NCH):
        def sy_xn(sy, cc=cc):
            inc(sy.dma_start(out=xnstg2[:, :],
                             in_=xn_spill[128 * cc:128 * (cc + 1), :]), "dD1", 16)
            sy.wait_ge(sem("dD1"), cget("dD1"))
            return sy.nop()
        steps.append(("sync", sy_xn))
        for mm_ in range(8):
            def v_q(v, cc=cc, mm_=mm_):
                v.tensor_scalar(dq1[:, :],
                                xnstg2[:, 512 * mm_:512 * (mm_ + 1)],
                                rsxv[:, 0:1], 128.5, ALU.mult, ALU.add)
                v.tensor_scalar(dq1[:, :], dq1[:, :], 0.0, None, ALU.max)
                v.tensor_copy(dqI[:, :], dq1[:, :])
                v.scalar_tensor_tensor(dq2[:, :], dqI[:, :], 1.0, dq1[:, :],
                                       ALU.mult, ALU.is_le)
                v.tensor_tensor(dq1[:, :], dqI[:, :], dq2[:, :], ALU.add)
                return v.tensor_scalar(
                    xqA[:, NT * cc + 512 * mm_:NT * cc + 512 * (mm_ + 1)],
                    dq1[:, :], -129.0, 127.0, ALU.add, ALU.min)
            steps.append(("vector", v_q))
    with nc.Block() as blk:
        @blk.vector
        def _(v):
            v.wait_ge(sem("vD1"), cget("vD1"))
        @blk.sync
        def _(sy):
            sy.wait_ge(sem("dDg"), cget("dDg"))
    chain(steps)

    with nc.Block() as blk:
        @blk.tensor
        def _(t):
            t.wait_ge(sem("dDg"), cget("dDg"))
    barrier()

    steps = []

    def addd(eng, fn, deps):
        steps.append((eng, fn, [d for d in deps if d is not None]))
        return len(steps) - 1

    i_t = {}
    i_s = {}
    i_v = {}
    i_y = {}
    it = 0
    for m in range(8):
        for ic in range(NIC):
            pg = psP if it % 2 == 0 else psS1
            pu = psQ if it % 2 == 0 else psS2
            hp = slice(512 * (it % 2), 512 * (it % 2 + 1))

            def t_gu(t, m=m, ic=ic, pg=pg, pu=pu):
                for kc in range(NCH):
                    t.matmul(pg[:, 0:512],
                             wgS[:, IFF * kc + 128 * ic:IFF * kc + 128 * (ic + 1)],
                             xqA[:, NT * kc + 512 * m:NT * kc + 512 * (m + 1)],
                             start=(kc == 0), stop=(kc == NCH - 1))
                for kc in range(NCH):
                    mm = t.matmul(
                        pu[:, 0:512],
                        wuS[:, IFF * kc + 128 * ic:IFF * kc + 128 * (ic + 1)],
                        xqA[:, NT * kc + 512 * m:NT * kc + 512 * (m + 1)],
                        start=(kc == 0), stop=(kc == NCH - 1))
                return mm
            i_t[it] = addd("tensor", t_gu, [i_s.get(it - 2), i_v.get(it - 2)])

            def s_gu(s, m=m, ic=ic, pg=pg, hp=hp):
                return s.activation(sgT[:, hp], pg[:, 0:512], ACT.Silu,
                                    bias=0.0, scale=svg[:, ic:ic + 1])
            i_s[it] = addd("scalar", s_gu, [i_t[it], i_v.get(it - 2)])

            def v_gu(v, m=m, ic=ic, pu=pu, hp=hp):
                v.scalar_tensor_tensor(midT[:, hp], pu[:, 0:512],
                                       svu[:, ic:ic + 1], sgT[:, hp],
                                       ALU.mult, ALU.mult)
                v.reduce_max(dmx[:, :], midT[:, hp], AX.X,
                             apply_absolute_value=True)
                return v.tensor_tensor(rmaxcol2[:, :], rmaxcol2[:, :], dmx[:, :],
                                       ALU.max)
            i_v[it] = addd("vector", v_gu, [i_s[it], i_y.get(it - 2)])

            def sy_gu(sy, m=m, ic=ic, hp=hp):
                inc(sy.dma_start(out=mid_spill[128 * ic:128 * (ic + 1),
                                               512 * m:512 * (m + 1)],
                                 in_=midT[:, hp]), "dMo", 16)
                sy.wait_ge(sem("dMo"), cget("dMo"))
                return sy.nop()
            i_y[it] = addd("sync", sy_gu, [i_v[it]])
            it += 1
    chain2(steps, name="d2")
    barrier()
    stD1.close()

    stD2 = ExitStack()
    wdS = sbuf("wdS", [128, NIC * 1024], BF16, stD2)
    mqA = sbuf("mqA", [128, 2 * NIC * 512], BF16, stD2)
    mres = sbuf("mres", [128, 2 * NCH * 512], BF16, stD2)
    sStg = sbuf("sStg", [128, 1024], BF16, stD2)
    xStg = sbuf("xStg", [128, 2048], F32, stD2)
    ysb = sbuf("ysb", [128, 2048], F32, stD2)
    dq1b = sbuf("dq1b", [128, 512], F32, stD2)
    dq2b = sbuf("dq2b", [128, 512], F32, stD2)
    dqIb = sbuf("dqIb", [128, 512], I32, stD2)
    dqSb = sbuf("dqSb", [128, 512], BF16, stD2)
    sgT2 = sbuf("sgT2", [128, 1024], BF16, stD2)
    svdb = sbuf("svdb", [128, NCH], F32, stD2)
    clp2 = sbuf("clp2", [1, 2], F32, stD2)

    with nc.Block() as blk:
        @blk.sync
        def _(sy):
            inc(sy.dma_start(out=maxrt2[0:1, :], in_=rmaxcol2[:, 0:1]), "dD2", 16)
            sy.wait_ge(sem("dD2"), cget("dD2"))
            inc(sy.dma_start(out=rowm[0:1, :], in_=maxrt2[0:1, :]), "dD3", 16)

        @blk.vector
        def _(v):
            v = DW(v)
            v.wait_ge(sem("dD3"), cget("dD3"))
            inc(v.reduce_max(cc2row[0:1, 0:1], rowm[0:1, :], AX.X), "vD3")

        @blk.gpsimd
        def _(g):
            g.wait_ge(sem("vD3"), cget("vD3"))
            inc(g.dma_start(out=cc2i[0:1, :], in_=cc2row[0:1, :]), "dW", 16)
            g.wait_ge(sem("dW"), cget("dW"))
            if "nocc" in DBG:
                inc(g.dma_start(out=cc2o[0:1, :], in_=cc2i[0:1, :]), "cc2", 16)
            else:
                inc(g.collective_compute(
                    "AllReduce", ALU.max, replica_groups=[list(range(8))],
                    ins=[cc2i.ap().opt()], outs=[cc2o.ap().opt()]), "cc2", 1)

    with nc.Block() as blk:
        @blk.sync
        def _(sy):
            sy.wait_ge(sem("cc2"), cget("cc2"))
            inc(sy.dma_start(out=rowm[0:1, :], in_=cc2o[0:1, :]), "dD4", 16)
            inc(sy.dma_start(out=clp2[0:1, 1:2], in_=clipd.ap()), "dD4", 16)
            for ic in range(NIC):
                inc(sy.dma_start(out=wdS[:, 1024 * ic:1024 * (ic + 1)],
                                 in_=wdT_sp[128 * ic:128 * (ic + 1), :]),
                    "dD5", 16)

        @blk.scalar
        def _(s):
            s = DW(s)
            s.wait_ge(sem("dD4"), cget("dD4"))
            inc(s.activation(clp2[0:1, 1:2], clp2[0:1, 1:2], ACT.Exp,
                             bias=0.0, scale=-1.0), "sD1")

        @blk.vector
        def _(v):
            v = DW(v)
            v.wait_ge(sem("sD1"), cget("sD1"))
            v.tensor_scalar(clp2[0:1, 1:2], clp2[0:1, 1:2], 1.0, None, ALU.add)
            v.reciprocal(clp2[0:1, 1:2], clp2[0:1, 1:2])
            v.tensor_tensor(ccrow[0:1, 5:6], clp2[0:1, 1:2], rowm[0:1, 0:1],
                            ALU.mult)
            v.tensor_scalar(ccrow[0:1, 5:6], ccrow[0:1, 5:6], 1.0 / 127.0, None,
                            ALU.mult)
            inc(v.reciprocal(ccrow[0:1, 6:7], ccrow[0:1, 5:6]), "vD4")

        @blk.tensor
        def _(t):
            t.wait_ge(sem("vD4"), cget("vD4"))
            t.matmul(psR[:, 2:3], onesf[0:1, :], ccrow[0:1, 5:6],
                     start=True, stop=True)
            inc(t.matmul(psR[:, 3:4], onesf[0:1, :], ccrow[0:1, 6:7],
                         start=True, stop=True), "tD1")

    with nc.Block() as blk:
        @blk.vector
        def _(v):
            v = DW(v)
            v.wait_ge(sem("tD1"), cget("tD1"))
            v.tensor_copy(sdv[:, :], psR[:, 2:3])
            v.tensor_copy(rsdv[:, :], psR[:, 3:4])
            inc(v.tensor_scalar(svdb[:, :], scd[:, :], sdv[:, 0:1], None,
                                ALU.mult), "vD5")

    steps = []

    def addf(eng, fn, deps):
        steps.append((eng, fn, [d for d in deps if d is not None]))
        return len(steps) - 1

    i_mid = {}
    i_q = {}
    i_tdn = {}
    i_sdn = {}
    i_vdn = {}
    i_tf = {}
    i_vf = {}
    i_yy = {}
    for m in range(8):
        mq0 = NIC * 512 * (m % 2)

        def sy_mid(sy, m=m, mq0=mq0):
            for ic in range(NIC):
                inc(sy.dma_start(out=mqA[:, mq0 + 512 * ic:mq0 + 512 * (ic + 1)],
                                 in_=mid_spill[128 * ic:128 * (ic + 1),
                                               512 * m:512 * (m + 1)]),
                    "dD6", 16)
            sy.wait_ge(sem("dD6"), cget("dD6"))
            return sy.nop()
        i_mid[m] = addf("sync", sy_mid, [i_tdn.get((m - 2, NCH - 1))])

        for ic in range(NIC):
            def v_mq(v, m=m, ic=ic, mq0=mq0):
                v.tensor_scalar(dq1b[:, :],
                                mqA[:, mq0 + 512 * ic:mq0 + 512 * (ic + 1)],
                                rsdv[:, 0:1], 128.5, ALU.mult, ALU.add)
                v.tensor_scalar(dq1b[:, :], dq1b[:, :], 0.0, None, ALU.max)
                v.tensor_copy(dqIb[:, :], dq1b[:, :])
                v.scalar_tensor_tensor(dq2b[:, :], dqIb[:, :], 1.0, dq1b[:, :],
                                       ALU.mult, ALU.is_le)
                v.tensor_tensor(dq1b[:, :], dqIb[:, :], dq2b[:, :], ALU.add)
                return v.tensor_scalar(mqA[:, mq0 + 512 * ic:mq0 + 512 * (ic + 1)],
                                       dq1b[:, :], -129.0, 127.0, ALU.add,
                                       ALU.min)
            i_q[(m, ic)] = addf("vector", v_mq, [i_mid[m]])

        for oc in range(NCH):
            it2 = 8 * m + oc
            pd = psP if it2 % 2 == 0 else psQ
            hp = slice(512 * (it2 % 2), 512 * (it2 % 2 + 1))

            def t_dn(t, m=m, oc=oc, pd=pd, mq0=mq0):
                for ic in range(NIC):
                    mm = t.matmul(
                        pd[:, 0:512],
                        wdS[:, 1024 * ic + 128 * oc:1024 * ic + 128 * (oc + 1)],
                        mqA[:, mq0 + 512 * ic:mq0 + 512 * (ic + 1)],
                        start=(ic == 0), stop=(ic == NIC - 1))
                return mm
            i_tdn[(m, oc)] = addf("tensor", t_dn,
                                  [i_q[(m, NIC - 1)], i_sdn.get((m, oc - 2)),
                                   i_tf.get((m - 1, 3))])

            def s_dn(s, m=m, oc=oc, pd=pd, hp=hp):
                return s.activation(sgT2[:, hp], pd[:, 0:512], ACT.Silu,
                                    bias=0.0, scale=svdb[:, oc:oc + 1])
            i_sdn[(m, oc)] = addf("scalar", s_dn,
                                  [i_tdn[(m, oc)], i_vdn.get((m, oc - 2))])

            def sy_s(sy, m=m, oc=oc, hp=hp):
                inc(sy.dma_start(out=sStg[:, hp],
                                 in_=s_spill[128 * oc:128 * (oc + 1),
                                             512 * m:512 * (m + 1)]), "dD7", 16)
                sy.wait_ge(sem("dD7"), cget("dD7"))
                return sy.nop()
            isys = addf("sync", sy_s, [i_vdn.get((m, oc - 2))])

            def v_dn(v, m=m, oc=oc, hp=hp):
                return v.tensor_tensor(
                    mres[:, NCH * 512 * (m % 2) + 512 * oc:
                         NCH * 512 * (m % 2) + 512 * (oc + 1)],
                    sgT2[:, hp], sStg[:, hp], ALU.mult)
            i_vdn[(m, oc)] = addf("vector", v_dn,
                                  [i_sdn[(m, oc)], isys,
                                   i_tf.get((m - 1, 3))])

        for u in range(4):
            tau = 4 * m + u
            bank = psA if u % 2 == 0 else psB
            xh = slice(1024 * (u % 2), 1024 * (u % 2 + 1))

            def t_fin(t, m=m, u=u, bank=bank):
                for oc in range(NCH):
                    tr = t.transpose(
                        bank[:, 128 * oc:128 * (oc + 1)],
                        mres[:, NCH * 512 * (m % 2) + 512 * oc + 128 * u:
                             NCH * 512 * (m % 2) + 512 * oc + 128 * (u + 1)],
                        idbf[:, :])
                return tr
            i_tf[(m, u)] = addf("tensor", t_fin,
                                [i_vdn[(m, NCH - 1)], i_vf.get((m, u - 2))])

            def sy_x(sy, tau=tau, xh=xh):
                inc(sy.dma_start(out=xStg[:, xh],
                                 in_=x_in[128 * tau:128 * (tau + 1), :]), "dD8", 16)
                sy.wait_ge(sem("dD8"), cget("dD8"))
                return sy.nop()
            isyx = addf("sync", sy_x, [i_vf.get((m, u - 2))])

            def v_fin(v, tau=tau, bank=bank, xh=xh):
                return v.tensor_tensor(ysb[:, xh], bank[:, 0:1024], xStg[:, xh],
                                       ALU.add)
            i_vf[(m, u)] = addf("vector", v_fin,
                                [i_tf[(m, u)], isyx, i_yy.get((m, u - 2))])

            def sy_y(sy, tau=tau, xh=xh):
                inc(sy.dma_start(out=y_out[128 * tau:128 * (tau + 1), :],
                                 in_=ysb[:, xh]), "dY", 16)
                sy.wait_ge(sem("dY"), cget("dY"))
                return sy.nop()
            i_yy[(m, u)] = addf("sync", sy_y, [i_vf[(m, u)]])
    with nc.Block() as blk:
        @blk.tensor
        def _(t):
            t.wait_ge(sem("vD5"), cget("vD5"))
        @blk.sync
        def _(sy):
            sy.wait_ge(sem("dD5"), cget("dD5"))
    barrier()
    chain2(steps, name="d4")
    barrier()
    stD2.close()
    stD.close()


    stack.close()
    return nc, taps


def _consts():
    one = np.ones((128, 128), dtype=np.float32)
    return {
        "c_idbf": np.eye(128, dtype=ml_dtypes.bfloat16),
        "c_ut": np.triu(one).astype(ml_dtypes.bfloat16),
        "c_sut": np.triu(one, 1).astype(ml_dtypes.bfloat16),
        "c_onesb": one.astype(ml_dtypes.bfloat16),
        "c_onesf": one,
        "c_iota128": np.arange(128, dtype=np.float32).reshape(128, 1),
        "c_iota32": np.arange(32, dtype=np.float32).reshape(1, 32),
    }


def _in_maps(inputs):
    x = np.ascontiguousarray(np.asarray(inputs["L0_attn_x"], dtype=np.float32))
    shared = {
        "wq": inputs["L0_attn_Wq"], "wk": inputs["L0_attn_Wk"],
        "wv": inputs["L0_attn_Wv"], "wgt": inputs["L0_attn_gate_w"],
        "wp": inputs["L0_attn_p_w"], "wo": inputs["L0_attn_out_w"],
        "wclk": inputs["L0_attn_clock_w"],
        "b_gate": inputs["L0_attn_gate_b"], "b_out": inputs["L0_attn_out_b"],
        "b_clk": inputs["L0_attn_clock_b"],
        "g1": inputs["L0_attn_ln1_g"], "b1": inputs["L0_attn_ln1_b"],
        "g2": inputs["L0_attn_ln2_g"], "b2": inputs["L0_attn_ln2_b"],
        "w1": inputs["L0_attn_mlp_w1"], "bm1": inputs["L0_attn_mlp_b1"],
        "w2": inputs["L0_attn_mlp_w2"], "bm2": inputs["L0_attn_mlp_b2"],
        "clipx": inputs["L0_ffn_clip_x"], "clipd": inputs["L0_ffn_clip_down"],
        "wgate": inputs["L0_ffn_w_gate"], "wup": inputs["L0_ffn_w_up"],
        "wdown": inputs["L0_ffn_w_down"],
    }
    shared = {k: np.ascontiguousarray(np.asarray(v, dtype=np.float32))
              for k, v in shared.items()}
    shared.update(_consts())
    maps = []
    for core in range(8):
        m = dict(shared)
        m["x"] = np.ascontiguousarray(
            x[core * BPC:(core + 1) * BPC].reshape(NT, D))
        maps.append(m)
    return maps


_RES = {}


def kernel(**inputs):
    nc, taps = build_program()
    maps = _in_maps(inputs)
    res = run_bass_kernel_spmd(nc, maps, core_ids=list(range(8)))
    _RES["results"] = res.results
    ys = [np.asarray(res.results[c]["y"]).reshape(BPC, T, D) for c in range(8)]
    return np.concatenate(ys, axis=0)


# version 16
# speedup vs baseline: 1.0347x; 1.0347x over previous
import math
import numpy as np
import ml_dtypes
from contextlib import ExitStack

import concourse.bass as bass
import concourse.mybir as mybir
from concourse.bass_utils import run_bass_kernel_spmd

F32 = mybir.dt.float32
BF16 = mybir.dt.bfloat16
I32 = mybir.dt.int32
ACT = mybir.ActivationFunctionType
ALU = mybir.AluOpType
AX = mybir.AxisListType
AP = bass.AP

D, T, BPC = 1024, 2048, 2
NT = BPC * T
H, DH = 16, 64
NCH, NTAU = 8, 32
IFF, NIC = 2816, 22
MH, NMH = 4096, 32
EPS = 1e-5
TWO_PI = 2.0 * math.pi
PI_CLAMP = 3.1415920
RND_CAP = 128.49

DBG = []


def build_program():
    nc = bass.Bass("TRN2", target_bir_lowering=False)

    def din(name, shape):
        return nc.dram_tensor(name, shape, F32, kind="ExternalInput")

    x_in = din("x", [NT, D])
    wq, wk, wv, wg, wp, wo = (din(n, [D, D]) for n in
                              ("wq", "wk", "wv", "wgt", "wp", "wo"))
    wclk = din("wclk", [D, H])
    b_gate, b_out, b_clk = din("b_gate", [D]), din("b_out", [D]), din("b_clk", [H])
    g1v, b1v = din("g1", [D]), din("b1", [D])
    g2v, b2v = din("g2", [D]), din("b2", [D])
    w1, bm1, w2, bm2 = din("w1", [D, MH]), din("bm1", [MH]), din("w2", [MH, D]), din("bm2", [D])
    clipx, clipd = din("clipx", [1]), din("clipd", [1])
    wgate, wup, wdown = din("wgate", [IFF, D]), din("wup", [IFF, D]), din("wdown", [D, IFF])
    c_idbf = nc.dram_tensor("c_idbf", [128, 128], BF16, kind="ExternalInput")
    c_ut = nc.dram_tensor("c_ut", [128, 128], BF16, kind="ExternalInput")
    c_sut = nc.dram_tensor("c_sut", [128, 128], BF16, kind="ExternalInput")
    c_onesb = nc.dram_tensor("c_onesb", [128, 128], BF16, kind="ExternalInput")
    c_onesf = nc.dram_tensor("c_onesf", [128, 128], F32, kind="ExternalInput")
    c_iota128 = nc.dram_tensor("c_iota128", [128, 1], F32, kind="ExternalInput")
    c_iota32 = nc.dram_tensor("c_iota32", [1, 32], F32, kind="ExternalInput")

    y_out = nc.dram_tensor("y", [NT, D], F32, kind="ExternalOutput")

    xn_spill = nc.dram_tensor("xn_spill", [D, NT], BF16)
    out_spill = nc.dram_tensor("out_spill", [D, NT], BF16)
    s_spill = nc.dram_tensor("s_spill", [D, NT], BF16)
    mid_spill = nc.dram_tensor("mid_spill", [IFF, NT], BF16)
    w1bf = nc.dram_tensor("w1bf", [NMH, NCH, 128, 128], BF16)
    wgT_sp = nc.dram_tensor("wgT_sp", [D, IFF], BF16)
    wuT_sp = nc.dram_tensor("wuT_sp", [D, IFF], BF16)
    wdT_sp = nc.dram_tensor("wdT_sp", [IFF, D], BF16)
    maxrt = nc.dram_tensor("maxrt", [1, 128], F32)
    maxrt2 = nc.dram_tensor("maxrt2", [1, 128], F32)
    cc1i, cc1o = nc.dram_tensor("cc1i", [1, 128], F32), nc.dram_tensor("cc1o", [1, 128], F32)
    cc2i, cc2o = nc.dram_tensor("cc2i", [1, 128], F32), nc.dram_tensor("cc2o", [1, 128], F32)

    taps = {}

    def tap(name, shape, dtype):
        if name in DBG and name not in taps:
            taps[name] = nc.dram_tensor("tap_" + name, shape, dtype,
                                        kind="ExternalOutput")
        return taps.get(name)

    stack = ExitStack()
    sems, cnt = {}, {}

    def sem(name):
        if name not in sems:
            sems[name] = stack.enter_context(nc.semaphore(name))
            cnt[name] = 0
        return sems[name]

    def inc(instr, name, k=1):
        instr.then_inc(sem(name), k)
        cnt[name] += k
        return cnt[name]

    def cget(name):
        sem(name)
        return cnt[name]

    dw_recent = {"vector": [], "scalar": [], "gpsimd": []}

    class DW:
        _PASS = {"wait_ge", "sem_inc", "nop"}
        _NOTRACK = {"dma_start"}
        WINDOW = 2

        def __init__(self, eng):
            self._e = eng
            tn = type(eng).__name__.lower()
            self._k = ("vector" if "vector" in tn else
                       "scalar" if "scalar" in tn else "gpsimd")

        def drain(self):
            dw_recent[self._k].clear()
            return self._e.drain()

        def __getattr__(self, n):
            f = getattr(self._e, n)
            if n in DW._PASS or n in DW._NOTRACK or not callable(f):
                return f
            rec = dw_recent[self._k]
            def g(*a, **k):
                names = []
                for o in list(a) + list(k.values()):
                    t = getattr(o, "tensor", None)
                    if t is not None:
                        names.append(getattr(t, "name", None) or id(t))
                outs, ins = set(names[:1]), set(names[1:])
                if any((ins & ro) or (outs & ro) or (outs & ri)
                       for ri, ro in rec):
                    self._e.drain()
                    rec.clear()
                r = f(*a, **k)
                rec.append((ins, outs))
                while len(rec) > DW.WINDOW:
                    rec.pop(0)
                return r
            return g

    def chain(steps, name="ch"):
        base = cget(name)
        with nc.Block() as blk:
            for en in ENGS:
                if not any(e == en for e, _ in steps):
                    continue

                def body(eng, en=en):
                    if en in ("vector", "scalar", "gpsimd"):
                        eng = DW(eng)
                    for i, (e2, fn) in enumerate(steps):
                        if e2 != en:
                            continue
                        if i > 0:
                            eng.wait_ge(sem(name), base + i)
                        last = fn(eng)
                        assert last is not None, f"chain step {i} returned None"
                        inc(last, name)
                getattr(blk, en)(body)

    def chain2(steps, name="c2"):
        if "serial2" in DBG or ("ser:" + name) in DBG:
            steps = [(e, fn, [i - 1] if i > 0 else [])
                     for i, (e, fn, deps) in enumerate(steps)]
        ordinals = []
        count = {}
        for e, fn, deps in steps:
            ordinals.append(count.get(e, 0))
            count[e] = count.get(e, 0) + 1
        bases = {e: cget(f"{name}_{e}") for e in count}
        with nc.Block() as blk:
            for en in list(count):
                def body(eng, en=en):
                    if en in ("vector", "scalar", "gpsimd"):
                        eng = DW(eng)
                    for i, (e, fn, deps) in enumerate(steps):
                        if e != en:
                            continue
                        need = {}
                        for d in deps:
                            de = steps[d][0]
                            if de == en:
                                continue
                            need[de] = max(need.get(de, -1), ordinals[d])
                        for de, o in need.items():
                            eng.wait_ge(sem(f"{name}_{de}"), bases[de] + o + 1)
                        last = fn(eng)
                        assert last is not None, f"chain2 step {i} returned None"
                        inc(last, f"{name}_{en}")
                getattr(blk, en)(body)

    bar_tgt = [0]
    ENGS = ["sync", "tensor", "vector", "scalar", "gpsimd"]

    def barrier():
        bar = sem("bar")
        bar_tgt[0] += 5
        tgt = bar_tgt[0]
        with nc.Block() as blk:
            for en in ENGS:
                def mk(en=en):
                    @getattr(blk, en)
                    def _(eng):
                        if en != "sync":
                            eng.drain()
                        eng.sem_inc(bar, 1)
                        eng.wait_ge(bar, tgt)
                mk()
        for rec in dw_recent.values():
            rec.clear()

    def sbuf(name, shape, dtype, st=None):
        return (st or stack).enter_context(nc.sbuf_tensor(name, shape, dtype))

    def psum(name, shape, dtype=F32, st=None):
        return (st or stack).enter_context(nc.psum_tensor(name, shape, dtype))

    idbf = sbuf("idbf", [128, 128], BF16)
    utb = sbuf("utb", [128, 128], BF16)
    onesb = sbuf("onesb", [128, 128], BF16)
    onesf = sbuf("onesf", [128, 128], F32)
    iota128 = sbuf("iota128", [128, 1], F32)
    iota32 = sbuf("iota32", [1, 32], F32)

    bocol = sbuf("bocol", [128, 8], F32)
    g2col = sbuf("g2col", [128, 8], F32)
    b2col = sbuf("b2col", [128, 8], F32)
    bm2col = sbuf("bm2col", [128, 8], F32)
    bm1col = sbuf("bm1col", [128, 32], F32)

    rmaxcol = sbuf("rmaxcol", [128, 1], F32)
    rmaxcol2 = sbuf("rmaxcol2", [128, 1], F32)
    rowm = sbuf("rowm", [1, 128], F32)
    ccrow = sbuf("ccrow", [1, 128], F32)
    cc2row = sbuf("cc2row", [1, 128], F32)
    sxv = sbuf("sxv", [128, 1], F32)
    rsxv = sbuf("rsxv", [128, 1], F32)
    sdv = sbuf("sdv", [128, 1], F32)
    rsdv = sbuf("rsdv", [128, 1], F32)
    scg = sbuf("scg", [128, NIC], F32)
    scu = sbuf("scu", [128, NIC], F32)
    scd = sbuf("scd", [128, NCH], F32)
    kcol = sbuf("kcol", [128, 4], F32)

    stXN1 = ExitStack()
    xn1_fm = sbuf("xn1_fm", [128, NCH * NT], BF16, stXN1)
    stTab = ExitStack()
    cosT2 = sbuf("cosT2", [128, NTAU * 64], F32, stTab)
    sinT = sbuf("sinT", [128, NTAU * 64], F32, stTab)
    clockS = sbuf("clockS", [128, NTAU * H], F32, stTab)
    maskadd = sbuf("maskadd", [128, 4 * 512], BF16, stTab)
    brepc = sbuf("brepc", [128, H], F32, stTab)
    growB = sbuf("growB", [1, D], F32, stTab)
    st_early = ExitStack()
    brow = sbuf("brow", [1, 4096], F32, st_early)
    g1rep = sbuf("g1rep", [128, D], F32, st_early)
    b1rep = sbuf("b1rep", [128, D], F32, st_early)
    stArena = ExitStack()
    arena1 = sbuf("arena1", [128, NCH * NT], BF16, stArena)

    psA = psum("psA", [128, 1024], BF16)
    psB = psum("psB", [128, 1024], BF16)
    psP = psum("psP", [128, 512], F32)
    psQ = psum("psQ", [128, 512], F32)
    psS1 = psum("psS1", [128, 512], F32)
    psS2 = psum("psS2", [128, 512], F32)
    psO = psum("psO", [128, 512], F32)
    psR = psum("psR", [128, 512], F32)

    with nc.Block() as blk:
        @blk.sync
        def _(sy):
            for dst, src in [(idbf, c_idbf), (utb, c_ut),
                             (onesb, c_onesb), (onesf, c_onesf),
                             (iota128, c_iota128), (iota32, c_iota32)]:
                inc(sy.dma_start(out=dst[:, :], in_=src[:, :]), "dA", 16)

        @blk.gpsimd
        def _(g):
            for kc in range(NCH):
                inc(g.dma_start(
                    out=AP(w1bf, kc * 128 * 128,
                           [[128, 128], [8 * 128 * 128, NMH], [1, 128]]),
                    in_=AP(w1, kc * 128 * MH,
                           [[MH, 128], [128, NMH], [1, 128]])), "dW1c", 16)

        @blk.vector
        def _(v):
            v = DW(v)
            v.wait_ge(sem("dA"), cget("dA"))
            for s_ in range(4):
                base = s_ * 512
                if s_ > 0:
                    v.memset(maskadd[:, base:base + 128 * s_], -4000.0)
                v.tensor_scalar(maskadd[:, base + 128 * s_:
                                        base + 128 * (s_ + 1)],
                                utb[:, :], -1.0, 4000.0, ALU.add, ALU.mult)
                if s_ < 3:
                    v.memset(maskadd[:, base + 128 * (s_ + 1):base + 512], 0.0)
            v.memset(rmaxcol[:, :], 0.0)
            v.memset(rmaxcol2[:, :], 0.0)
            v.memset(ccrow[:, :], 0.0)
            v.memset(cc2row[:, :], 0.0)
            v.memset(kcol[:, 0:1], EPS)
            v.memset(kcol[:, 1:2], EPS * EPS)
            v.memset(kcol[:, 2:3], 1.0)
            v.memset(kcol[:, 3:4], 0.5)

    barrier()

    st0 = ExitStack()
    thS = sbuf("thS", [128, NTAU * 32], F32, st0)
    thC = sbuf("thC", [128, NTAU * 32], F32, st0)
    snR = sbuf("snR", [128, NTAU * 32], F32, st0)
    rpF = sbuf("rpF", [128, 32], F32, st0)
    rpB = sbuf("rpB", [128, 32], F32, st0)
    rpC = sbuf("rpC", [128, 32], F32, st0)
    rpI = sbuf("rpI", [128, 32], I32, st0)
    tvcol = sbuf("tvcol", [128, 1], F32, st0)

    def range_reduce(v, src_ap, dst_ap, shift):
        v.tensor_scalar(rpB[:, :], src_ap, float(shift), 1.0 / TWO_PI,
                        ALU.add, ALU.mult)
        v.tensor_copy(rpI[:, :], rpB[:, :])
        v.tensor_scalar(rpC[:, :], src_ap, float(shift), None, ALU.add)
        v.scalar_tensor_tensor(rpB[:, :], rpI[:, :], -TWO_PI, rpC[:, :],
                               ALU.mult, ALU.add)
        v.tensor_scalar(rpC[:, :], rpB[:, :], math.pi, -TWO_PI,
                        ALU.is_ge, ALU.mult)
        v.tensor_tensor(rpB[:, :], rpB[:, :], rpC[:, :], ALU.add)
        v.tensor_scalar(rpC[:, :], rpB[:, :], -math.pi, TWO_PI,
                        ALU.is_le, ALU.mult)
        v.tensor_tensor(rpB[:, :], rpB[:, :], rpC[:, :], ALU.add)
        return v.tensor_scalar(dst_ap, rpB[:, :], PI_CLAMP, -PI_CLAMP,
                               ALU.min, ALU.max)

    with nc.Block() as blk:
        @blk.scalar
        def _(s):
            s = DW(s)
            inc(s.activation(brow[0:1, 0:32], iota32[0:1, :], ACT.Exp,
                             bias=0.0, scale=-math.log(10000.0) / 32.0), "sA")
            s.wait_ge(sem("vA_rope"), 1)
            s.activation(snR[:, :], thS[:, :], ACT.Sin)
            inc(s.activation(thC[:, :], thC[:, :], ACT.Sin), "sA")

        @blk.tensor
        def _(t):
            t.wait_ge(sem("sA"), cget("sA") - 1)
            inc(t.matmul(psR[:, 0:32], onesf[0:1, :], brow[0:1, 0:32],
                         start=True, stop=True), "tA")

        @blk.vector
        def _(v):
            v = DW(v)
            v.wait_ge(sem("tA"), cget("tA"))
            v.tensor_copy(rpF[:, :], psR[:, 0:32])
            for tau in range(NTAU):
                pos = (tau % 16) * 128
                v.tensor_scalar(tvcol[:, :], iota128[:, :], float(pos), None, ALU.add)
                v.tensor_scalar(thS[:, tau * 32:(tau + 1) * 32], rpF[:, :],
                                tvcol[:, 0:1], None, ALU.mult)
                range_reduce(v, thS[:, tau * 32:(tau + 1) * 32],
                             thC[:, tau * 32:(tau + 1) * 32], math.pi / 2)
                last = range_reduce(v, thS[:, tau * 32:(tau + 1) * 32],
                                    thS[:, tau * 32:(tau + 1) * 32], 0.0)
            inc(last, "vA_rope")
            v.wait_ge(sem("sA"), cget("sA"))
            v.tensor_scalar(AP(sinT, 0, [[NTAU * 64, 128], [2, NTAU * 32]]),
                            snR[:, :], -1.0, None, ALU.mult)
            v.tensor_copy(AP(sinT, 1, [[NTAU * 64, 128], [2, NTAU * 32]]),
                          snR[:, :])
            v.tensor_copy(AP(cosT2, 0, [[NTAU * 64, 128], [2, NTAU * 32]]),
                          thC[:, :])
            inc(v.tensor_copy(AP(cosT2, 1, [[NTAU * 64, 128], [2, NTAU * 32]]),
                              thC[:, :]), "vA")

    barrier()
    st0.close()

    jobs = [
        ("rep", g1v, D, g1rep), ("rep", b1v, D, b1rep), ("rep", b_clk, H, brepc),
        ("col", b_out, D, bocol), ("col", g2v, D, g2col), ("col", b2v, D, b2col),
        ("col", bm2, D, bm2col), ("col", bm1, MH, bm1col),
    ]
    with nc.Block() as blk:
        @blk.sync
        def _(sy):
            inc(sy.dma_start(out=growB[0:1, :], in_=b_gate.ap()), "dB", 16)
    nvb = []
    for kind, src_, n, dst in jobs:
        nvb.append(1 if kind == "rep" else (n // 128 + 7) // 8)

    with nc.Block() as blk:
        @blk.sync
        def _(sy):
            base_vb = cget("vB")
            for j, (kind, src_, n, dst) in enumerate(jobs):
                if j > 0:
                    sy.wait_ge(sem("vB"), base_vb + sum(nvb[:j]))
                inc(sy.dma_start(out=brow[0:1, 0:n], in_=src_.ap()), "dB", 16)

        @blk.tensor
        def _(t):
            base_db = cget("dB") - 16 * len(jobs)
            base_vb = cget("vB")
            base_tb = cget("tB")
            vb_done = 0
            for j, (kind, src_, n, dst) in enumerate(jobs):
                t.wait_ge(sem("dB"), base_db + 16 * (j + 1))
                if kind == "rep":
                    for o in range(0, n, 512):
                        w_ = min(512, n - o)
                        pst = psR if o == 0 else psQ
                        mm = t.matmul(pst[:, 0:w_], onesf[0:1, :],
                                      brow[0:1, o:o + w_], start=True, stop=True)
                    inc(mm, "tB")
                    vb_done += 1
                else:
                    ncol = n // 128
                    for gidx in range(0, ncol, 8):
                        if vb_done > 0 or gidx > 0:
                            t.wait_ge(sem("vB"), base_vb + vb_done)
                        for jj in range(gidx, min(gidx + 8, ncol)):
                            mm = t.matmul(psR[:, (jj % 8):(jj % 8) + 1],
                                          brow[0:1, 128 * jj:128 * (jj + 1)],
                                          onesf[0:1, 0:1], start=True, stop=True)
                        inc(mm, "tB")
                        vb_done += 1

        @blk.vector
        def _(v):
            v = DW(v)
            base_tb = cget("tB") - sum(nvb)
            ng = 0
            for j, (kind, src_, n, dst) in enumerate(jobs):
                if kind == "rep":
                    ng += 1
                    v.wait_ge(sem("tB"), base_tb + ng)
                    for o in range(0, n, 512):
                        w_ = min(512, n - o)
                        pst = psR if o == 0 else psQ
                        cp = v.tensor_copy(dst[:, o:o + w_], pst[:, 0:w_])
                    inc(cp, "vB")
                else:
                    ncol = n // 128
                    for gidx in range(0, ncol, 8):
                        ng += 1
                        v.wait_ge(sem("tB"), base_tb + ng)
                        for jj in range(gidx, min(gidx + 8, ncol)):
                            cp = v.tensor_copy(dst[:, jj:jj + 1],
                                               psR[:, (jj % 8):(jj % 8) + 1])
                        inc(cp, "vB")

    barrier()

    stA = ExitStack()
    xa = [sbuf("xa0", [128, D], F32, stA), sbuf("xa1", [128, D], F32, stA)]
    xcf = sbuf("xcf", [128, D], F32, stA)
    xsq = sbuf("xsq", [128, D], F32, stA)
    xnb = [sbuf("xnb0", [128, D], BF16, stA), sbuf("xnb1", [128, D], BF16, stA)]
    x1b = [sbuf("x1b0", [128, D], BF16, stA), sbuf("x1b1", [128, D], BF16, stA)]
    smA = sbuf("smA", [128, 1], F32, stA)
    smB = sbuf("smB", [128, 1], F32, stA)
    smC = sbuf("smC", [128, 1], F32, stA)
    smD = sbuf("smD", [128, 1], F32, stA)
    smE = sbuf("smE", [128, 1], F32, stA)
    smF = sbuf("smF", [128, 1], F32, stA)
    smG = sbuf("smG", [128, 1], F32, stA)
    smH = sbuf("smH", [128, 1], F32, stA)
    smI = sbuf("smI", [128, 1], F32, stA)
    statT = sbuf("statT", [128, 128], F32, stA)

    with nc.Block() as blk:
        @blk.sync
        def _(sy):
            for tau in range(NTAU):
                if tau >= 2:
                    sy.wait_ge(sem("vA2"), tau - 1)
                inc(sy.dma_start(out=xa[tau % 2][:, :],
                                 in_=x_in[128 * tau:128 * (tau + 1), :]),
                    f"dX{tau % 2}", 16)

        @blk.vector
        def _(v):
            v = DW(v)
            for tau in range(NTAU):
                a = xa[tau % 2]
                if tau > 0:
                    v.wait_ge(sem("tA2"), tau)
                    v.tensor_copy(
                        AP(arena1, 128 * (tau - 1), [[NCH * NT, 128], [NT, NCH], [1, 128]]),
                        psA[:, :])
                    inc(v.tensor_copy(
                        AP(xn1_fm, 128 * (tau - 1), [[NCH * NT, 128], [NT, NCH], [1, 128]]),
                        psB[:, :]), "vE")
                v.wait_ge(sem(f"dX{tau % 2}"), 16 * (tau // 2 + 1))
                v.reduce_sum(smA[:, :], a[:, :], AX.X)
                v.tensor_scalar(smB[:, :], smA[:, :], 1.0 / D, None, ALU.mult)
                v.tensor_scalar(xcf[:, :], a[:, :], smB[:, 0:1], None, ALU.subtract)
                v.tensor_tensor(xsq[:, :], xcf[:, :], xcf[:, :], ALU.mult)
                inc(v.reduce_sum(smC[:, :], xsq[:, :], AX.X), "vA1")
                v.wait_ge(sem("sA2"), tau + 1)
                v.reciprocal(smF[:, :], smD[:, :])
                v.reciprocal(smG[:, :], smE[:, :])
                v.tensor_scalar(xnb[tau % 2][:, :], xcf[:, :], smF[:, 0:1],
                                None, ALU.mult)
                v.reduce_max(smH[:, :], xcf[:, :], AX.X, apply_absolute_value=True)
                v.tensor_tensor(smH[:, :], smH[:, :], smF[:, :], ALU.mult)
                v.tensor_tensor(rmaxcol[:, :], rmaxcol[:, :], smH[:, :], ALU.max)
                v.scalar_tensor_tensor(xsq[:, :], xcf[:, :], smG[:, 0:1],
                                       g1rep[:, :], ALU.mult, ALU.mult)
                v.tensor_copy(statT[:, tau:tau + 1], smB[:, :])
                v.tensor_copy(statT[:, 32 + tau:33 + tau], smF[:, :])
                v.tensor_copy(statT[:, 64 + tau:65 + tau], smG[:, :])
                v.tensor_copy(statT[:, 96 + tau:97 + tau], smC[:, :])
                inc(v.tensor_tensor(x1b[tau % 2][:, :], xsq[:, :], b1rep[:, :],
                                    ALU.add), "vA2")
            v.wait_ge(sem("tA2"), NTAU)
            v.tensor_copy(
                AP(arena1, 128 * (NTAU - 1), [[NCH * NT, 128], [NT, NCH], [1, 128]]),
                psA[:, :])
            inc(v.tensor_copy(
                AP(xn1_fm, 128 * (NTAU - 1), [[NCH * NT, 128], [NT, NCH], [1, 128]]),
                psB[:, :]), "vE")

        @blk.scalar
        def _(s):
            s = DW(s)
            for tau in range(NTAU):
                s.wait_ge(sem("vA1"), tau + 1)
                s.activation(smD[:, :], smC[:, :], ACT.Sqrt,
                             bias=kcol[:, 0:1], scale=1.0 / D)
                s.activation(smE[:, :], smC[:, :], ACT.Sqrt,
                             bias=kcol[:, 1:2], scale=(1.0 + EPS) / D)
                inc(s.activation(smI[:, :], smD[:, :], ACT.Copy), "sA2")

        @blk.tensor
        def _(t):
            for tau in range(NTAU):
                t.wait_ge(sem("vA2"), tau + 1)
                if tau > 0:
                    t.wait_ge(sem("vE"), tau)
                for c in range(NCH):
                    t.transpose(psA[:, 128 * c:128 * (c + 1)],
                                xnb[tau % 2][:, 128 * c:128 * (c + 1)], idbf[:, :])
                    tr = t.transpose(psB[:, 128 * c:128 * (c + 1)],
                                     x1b[tau % 2][:, 128 * c:128 * (c + 1)], idbf[:, :])
                inc(tr, "tA2")

    barrier()

    with nc.Block() as blk:
        @blk.sync
        def _(sy):
            for c in range(NCH):
                inc(sy.dma_start(out=xn_spill[128 * c:128 * (c + 1), :],
                                 in_=arena1[:, NT * c:NT * (c + 1)]), "dO", 16)
            inc(sy.dma_start(out=maxrt[0:1, :], in_=rmaxcol[:, 0:1]), "dO", 16)
            sy.wait_ge(sem("dO"), cget("dO"))
            inc(sy.dma_start(out=rowm[0:1, :], in_=maxrt[0:1, :]), "dA", 16)

        @blk.vector
        def _(v):
            v = DW(v)
            v.wait_ge(sem("dA"), cget("dA"))
            inc(v.reduce_max(ccrow[0:1, 0:1], rowm[0:1, :], AX.X), "vA")

        @blk.gpsimd
        def _(g):
            g = DW(g)
            g.wait_ge(sem("vA"), cget("vA"))
            inc(g.dma_start(out=cc1i[0:1, :], in_=ccrow[0:1, :]), "dW", 16)
            g.wait_ge(sem("dW"), cget("dW"))
            if "nocc" in DBG:
                inc(g.dma_start(out=cc1o[0:1, :], in_=cc1i[0:1, :]), "cc", 16)
            else:
                inc(g.collective_compute(
                    "AllReduce", ALU.max, replica_groups=[list(range(8))],
                    ins=[cc1i.ap().opt()], outs=[cc1o.ap().opt()]), "cc", 1)

    barrier()

    tp0 = tap("stats", [128, 128], F32)
    tp1 = tap("xn_fm", [128, NCH * NT], BF16)
    tp2 = tap("xn1_fm", [128, NCH * NT], BF16)
    tp3 = tap("gmax", [1, 128], F32)
    if any(t is not None for t in (tp0, tp1, tp2, tp3)):
        with nc.Block() as blk:
            @blk.sync
            def _(sy):
                if tp0 is not None:
                    inc(sy.dma_start(out=tp0[:, :], in_=statT[:, :]), "dO", 16)
                if tp1 is not None:
                    inc(sy.dma_start(out=tp1[:, :], in_=arena1[:, :]), "dO", 16)
                if tp2 is not None:
                    inc(sy.dma_start(out=tp2[:, :], in_=xn1_fm[:, :]), "dO", 16)
                if tp3 is not None:
                    sy.wait_ge(sem("cc"), cget("cc"))
                    inc(sy.dma_start(out=rowm[0:1, :], in_=cc1o[0:1, :]), "dA", 16)
                    sy.wait_ge(sem("dA"), cget("dA"))
                    inc(sy.dma_start(out=tp3[0:1, :], in_=rowm[0:1, :]), "dO", 16)
                sy.wait_ge(sem("dO"), cget("dO"))
        barrier()


    stA.close()
    stArena.close()
    st_early.close()

    stB = ExitStack()
    Wall = sbuf("Wall", [128, 8 * 640], BF16, stB)
    wstg = [sbuf("wstg0", [128, 640], F32, stB),
            sbuf("wstg1", [128, 640], F32, stB)]
    bufQ = sbuf("bufQ", [128, NT], BF16, stB)
    bufK = sbuf("bufK", [128, NT], BF16, stB)
    bufG = sbuf("bufG", [128, NT], BF16, stB)
    bufW1 = sbuf("bufW1", [128, NT], BF16, stB)
    bufW2 = sbuf("bufW2", [128, NT], BF16, stB)
    gcpT = sbuf("gcpT", [128, NT], BF16, stB)
    pexpT = sbuf("pexpT", [128, NT], BF16, stB)
    pcsT = sbuf("pcsT", [128, NT], BF16, stB)
    v0c = sbuf("v0c", [128, NT], BF16, stB)
    v1c = sbuf("v1c", [128, NT], BF16, stB)
    eSb01 = [sbuf("eSb0", [128, 512], BF16, stB),
             sbuf("eSb1", [128, 512], BF16, stB)]
    den2 = sbuf("den2", [128, 512], F32, stB)
    spB = sbuf("spB", [128, 1024], F32, stB)
    nrmB = sbuf("nrmB", [128, 64], F32, stB)
    rqk = sbuf("rqk", [128, 64], F32, stB)
    negpm = sbuf("negpm", [128, 2], F32, stB)
    offF = sbuf("offF", [1, 128], F32, stB)
    offB = sbuf("offB", [1, 128], BF16, stB)
    totF = sbuf("totF", [1, 128], F32, stB)
    cgf = sbuf("cgf", [128, 256], F32, stB)
    brepg = sbuf("brepg", [128, 128], F32, stB)
    clkw = sbuf("clkw", [128, 128], BF16, stB)
    clkstgF = sbuf("clkstgF", [128, 128], F32, stB)

    with nc.Block() as blk:
        @blk.sync
        def _(sy):
            for kc in range(NCH):
                inc(sy.dma_start(out=clkstgF[:, 16 * kc:16 * (kc + 1)],
                                 in_=wclk[128 * kc:128 * (kc + 1), :]), "dCLK", 16)

        @blk.vector
        def _(v):
            v = DW(v)
            v.wait_ge(sem("dCLK"), cget("dCLK"))
            inc(v.tensor_copy(clkw[:, :], clkstgF[:, :]), "vCLK")

    with nc.Block() as blk:
        @blk.tensor
        def _(t):
            t.wait_ge(sem("vCLK"), cget("vCLK"))

    steps = []
    for tau in range(NTAU):
        def t_clk(t, tau=tau):
            for kc in range(NCH):
                mm = t.matmul(psP[:, 0:16],
                              xn1_fm[:, kc * NT + 128 * tau:kc * NT + 128 * (tau + 1)],
                              clkw[:, 16 * kc:16 * (kc + 1)],
                              start=(kc == 0), stop=(kc == NCH - 1))
            return mm
        def v_clk(v, tau=tau):
            return v.tensor_tensor(clockS[:, 16 * tau:16 * (tau + 1)],
                                   psP[:, 0:16], brepc[:, :], ALU.add)
        steps.append(("tensor", t_clk))
        steps.append(("vector", v_clk))
    steps += [
        ("scalar", lambda s: s.activation(spB[:, 0:512], clockS[:, :], ACT.Abs)),
        ("scalar", lambda s: s.activation(den2[:, :], spB[:, 0:512], ACT.Exp,
                                          bias=0.0, scale=-1.0)),
        ("scalar", lambda s: s.activation(spB[:, 0:512], den2[:, :], ACT.Ln,
                                          bias=1.0, scale=1.0)),
        ("vector", lambda v: v.tensor_scalar(den2[:, :], clockS[:, :], 0.0, None,
                                             ALU.max)),
        ("vector", lambda v: v.tensor_tensor(clockS[:, :], den2[:, :],
                                             spB[:, 0:512], ALU.add)),
    ]
    chain(steps)
    barrier()

    def clock_ap(c):
        return AP(clockS, 2 * c, [[NTAU * H, 128], [H, NTAU], [1, 2], [0, 64]])

    COS_AP = AP(cosT2, 0, [[NTAU * 64, 128], [64, NTAU], [0, 2], [1, 64]])
    SIN_AP = AP(sinT, 0, [[NTAU * 64, 128], [64, NTAU], [0, 2], [1, 64]])
    RQK_AP = AP(rqk, 0, [[64, 128], [2, NTAU], [1, 2], [0, 64]])
    NRM_OUT = AP(nrmB, 0, [[64, 128], [2, NTAU], [1, 2]])

    def ev_even(buf):
        return AP(buf, 0, [[NT, 128], [128, NTAU], [64, 2], [2, 32]])

    def ev_odd(buf):
        return AP(buf, 1, [[NT, 128], [128, NTAU], [64, 2], [2, 32]])

    for c in range(NCH):
        wl = "wlB"
        with nc.Block() as blk:
            @blk.gpsimd
            def _(g):
                if c > 0:
                    g.wait_ge(sem("pp_tensor"), cget("pp_tensor"))
                if "oldwl" in DBG:
                    return
                for widx, w in enumerate([wq, wk, wg, wv, wp]):
                    inc(g.dma_start(
                        out=AP(Wall, 128 * widx,
                               [[8 * 640, 128], [640, 8], [1, 128]]),
                        in_=AP(w, 128 * c,
                               [[1024, 128], [128 * 1024, 8], [1, 128]])),
                        wl + "d", 16)

        if "oldwl" in DBG:
            with nc.Block() as blk:
                @blk.sync
                def _(sy):
                    for kc in range(NCH):
                        if kc >= 2:
                            sy.wait_ge(sem(wl + "v"), cget(wl + "v") - 1)
                        for widx, w in enumerate([wq, wk, wg, wv, wp]):
                            inc(sy.dma_start(
                                out=wstg[kc % 2][:, 128 * widx:128 * (widx + 1)],
                                in_=w[128 * kc:128 * (kc + 1),
                                      128 * c:128 * (c + 1)]),
                                wl + f"d{kc % 2}", 16)

                @blk.vector
                def _(v):
                    v = DW(v)
                    for kc in range(NCH):
                        v.wait_ge(sem(wl + f"d{kc % 2}"),
                                  cget(wl + f"d{kc % 2}") - 80 * ((NCH - 1 - kc) // 2))
                        inc(v.tensor_copy(Wall[:, 640 * kc:640 * (kc + 1)],
                                          wstg[kc % 2][:, :]), wl + "v")

        with nc.Block() as blk:
            @blk.tensor
            def _(t):
                if "oldwl" in DBG:
                    t.wait_ge(sem(wl + "v"), cget(wl + "v"))
                else:
                    t.wait_ge(sem(wl + "d"), cget(wl + "d"))
                inc(t.matmul(psR[:, 0:128], onesf[0:1, :],
                             growB[0:1, 128 * c:128 * (c + 1)],
                             start=True, stop=True), "tBG")

            @blk.vector
            def _(v):
                v = DW(v)
                v.wait_ge(sem("tBG"), cget("tBG"))
                v.tensor_copy(brepg[:, :], psR[:, 0:128])
                v.memset(v0c[:, :], 1.0)
                inc(v.memset(v1c[:, :], 1.0), "vBG")

        with nc.Block() as blk:
            @blk.tensor
            def _(t):
                t.wait_ge(sem("vBG"), cget("vBG"))

        steps1 = []
        for tau in range(NTAU):
            ps = psP if tau % 2 == 0 else psQ
            def t_proj(t, tau=tau, ps=ps):
                for kc in range(NCH):
                    mm = t.matmul(ps[:, 0:512],
                                  xn1_fm[:, kc * NT + 128 * tau:kc * NT + 128 * (tau + 1)],
                                  Wall[:, 640 * kc:640 * kc + 512],
                                  start=(kc == 0), stop=(kc == NCH - 1))
                return mm
            def v_ev(v, tau=tau, ps=ps):
                o = 128 * tau
                v.tensor_copy(bufQ[:, o:o + 128], ps[:, 0:128])
                v.tensor_copy(bufK[:, o:o + 128], ps[:, 128:256])
                v.tensor_tensor(bufG[:, o:o + 128], ps[:, 256:384], brepg[:, :],
                                ALU.add)
                v.tensor_copy(v0c[:, o:o + 64], ps[:, 384:448])
                return v.tensor_copy(v1c[:, o + 64:o + 128], ps[:, 448:512])
            ti = len(steps1)
            steps1.append(("tensor", t_proj,
                           [ti - 3] if tau >= 2 else []))
            steps1.append(("vector", v_ev, [ti]))
        chain2(steps1, name="pj")
        barrier()

        chain([
            ("scalar", lambda s: s.activation(bufW1[:, :], bufG[:, :], ACT.Abs)),
            ("scalar", lambda s: s.activation(bufW2[:, :], bufW1[:, :], ACT.Exp,
                                              bias=0.0, scale=-1.0)),
            ("scalar", lambda s: s.activation(bufW1[:, :], bufW2[:, :], ACT.Ln,
                                              bias=1.0, scale=1.0)),
            ("vector", lambda v: v.tensor_scalar(bufW2[:, :], bufG[:, :], 0.0,
                                                 None, ALU.max)),
            ("vector", lambda v: v.scalar_tensor_tensor(bufG[:, :], bufW2[:, :],
                                                        -1.0, bufW1[:, :],
                                                        ALU.mult, ALU.subtract)),
        ])

        def cumsum_chain(srcT, dstT, clip_exp, tag):
            steps = []
            v_idx = {}
            s_idx = {}
            it = 0
            for b in range(BPC):
                for tt in range(16):
                    tau = 16 * b + tt
                    ps = psP if tt % 2 == 0 else psQ
                    cg = cgf[:, 128 * (tt % 2):128 * (tt % 2 + 1)]
                    def t_cs(t, b=b, tt=tt, ps=ps):
                        for tp in range(tt + 1):
                            o = 128 * (16 * b + tp)
                            mm = t.matmul(ps[:, 0:128],
                                          utb[:, :] if tp == tt else onesb[:, :],
                                          srcT[:, o:o + 128],
                                          start=(tp == 0), stop=(tp == tt))
                        return mm
                    ti = len(steps)
                    steps.append(("tensor", t_cs,
                                  [v_idx[it - 2]] if it >= 2 else []))
                    if clip_exp:
                        def v_cs(v, ps=ps, cg=cg):
                            return v.tensor_scalar(cg, ps[:, 0:128], 50.0, -60.0,
                                                   ALU.min, ALU.max)
                        def s_cs(s, tau=tau, cg=cg):
                            o = 128 * tau
                            return s.activation(dstT[:, o:o + 128], cg, ACT.Exp)
                        vi = len(steps)
                        steps.append(("vector", v_cs,
                                      [ti] + ([s_idx[it - 2]] if it >= 2 else [])))
                        s_idx[it] = len(steps)
                        steps.append(("scalar", s_cs, [vi]))
                        v_idx[it] = vi
                    else:
                        def v_cs(v, tau=tau, ps=ps):
                            o = 128 * tau
                            return v.tensor_copy(dstT[:, o:o + 128], ps[:, 0:128])
                        v_idx[it] = len(steps)
                        steps.append(("vector", v_cs, [ti]))
                    it += 1
            chain2(steps, name=tag)
            barrier()

        barrier()
        cumsum_chain(bufG, gcpT, True, "cg")

        steps = []
        vproj = {}
        for m in range(8):
            ps = psP if m % 2 == 0 else psQ
            def t_p(t, m=m, ps=ps):
                for kc in range(NCH):
                    mm = t.matmul(ps[:, 0:512],
                                  Wall[:, 640 * kc + 512:640 * (kc + 1)],
                                  xn1_fm[:, kc * NT + 512 * m:kc * NT + 512 * (m + 1)],
                                  start=(kc == 0), stop=(kc == NCH - 1))
                return mm
            def v_p(v, m=m, ps=ps):
                return v.tensor_copy(bufW1[:, 512 * m:512 * (m + 1)], ps[:, 0:512])
            ti = len(steps)
            steps.append(("tensor", t_p, [vproj[m - 2]] if m >= 2 else []))
            steps.append(("vector", v_p, [ti]))
            vproj[m] = ti + 1
        spe = {}
        for b in range(BPC):
            def v_pm(v, b=b):
                return v.reduce_max(negpm[:, b:b + 1], bufW1[:, T * b:T * (b + 1)],
                                    AX.X, negate=True)
            def s_pe(s, b=b):
                return s.activation(bufW2[:, T * b:T * (b + 1)],
                                    bufW1[:, T * b:T * (b + 1)], ACT.Exp,
                                    bias=negpm[:, b:b + 1], scale=1.0)
            vi = len(steps)
            steps.append(("vector", v_pm, [vproj[4 * b + 3]]))
            steps.append(("scalar", s_pe, [vi]))
            spe[b] = vi + 1
        vtr = {}
        for g in range(8):
            bank = psA if g % 2 == 0 else psB
            def t_tr(t, g=g, bank=bank):
                for u in range(4):
                    tau = 4 * g + u
                    tr = t.transpose(bank[:, 128 * u:128 * (u + 1)],
                                     bufW2[:, 128 * tau:128 * (tau + 1)], idbf[:, :])
                return tr
            def v_tr(v, g=g, bank=bank):
                return v.tensor_copy(bufW1[:, 512 * g:512 * (g + 1)],
                                     bank[:, 0:512])
            ti = len(steps)
            deps = [spe[g // 4]]
            if g >= 2:
                deps.append(vtr[g - 2])
            steps.append(("tensor", t_tr, deps))
            steps.append(("vector", v_tr, [ti]))
            vtr[g] = ti + 1
        steps.append(("vector", lambda v: v.tensor_tensor(
            pexpT[:, :], bufW1[:, :], clock_ap(c), ALU.mult)), )
        steps[-1] = ("vector", steps[-1][0] if False else steps[-1][1], [vtr[7]])
        chain2(steps, name="pp")
        barrier()

        cumsum_chain(pexpT, pcsT, False, "cp")

        def qk_prep(bufX, wnum, wden):
            chain([
                ("vector", lambda v: v.tensor_tensor(bufW1[:, :], bufX[:, :],
                                                     bufX[:, :], ALU.mult)),
                ("vector", lambda v: v.reduce_sum(
                    NRM_OUT,
                    AP(bufW1, 0, [[NT, 128], [128, NTAU], [64, 2], [1, 64]]),
                    AX.X)),
                ("vector", lambda v: v.tensor_scalar(nrmB[:, :], nrmB[:, :],
                                                     1e-24, None, ALU.max)),
                ("scalar", lambda s: s.activation(den2[:, 0:64], nrmB[:, :],
                                                  ACT.Ln)),
                ("scalar", lambda s: s.activation(rqk[:, :], den2[:, 0:64],
                                                  ACT.Exp, bias=0.0, scale=-0.5)),
                ("vector", lambda v: v.tensor_tensor(bufW2[:, :], bufX[:, :],
                                                     RQK_AP, ALU.mult)),
                ("vector", lambda v: v.tensor_copy(ev_even(bufW1), ev_odd(bufW2))),
                ("vector", lambda v: v.tensor_copy(ev_odd(bufW1), ev_even(bufW2))),
                ("vector", lambda v: v.tensor_tensor(bufX[:, :], bufW1[:, :],
                                                     SIN_AP, ALU.mult)),
                ("vector", lambda v: v.tensor_tensor(bufW1[:, :], bufW2[:, :],
                                                     COS_AP, ALU.mult)),
                ("vector", lambda v: v.tensor_tensor(bufW2[:, :], bufX[:, :],
                                                     bufW1[:, :], ALU.add)),
                ("vector", lambda v: v.tensor_scalar(bufW1[:, :], wden[:, :],
                                                     1e-8, None, ALU.add)),
            ] + [
                it for m in range(4) for it in [
                    ("scalar", (lambda m: (lambda s: (
                        s.activation(spB[:, :], bufW1[:, 1024 * m:1024 * (m + 1)],
                                     ACT.Ln),
                        s.activation(den2[:, :],
                                     spB[:, 0:512], ACT.Exp, bias=0.0,
                                     scale=-1.0))[-1]))(m)),
                    ("scalar", (lambda m: (lambda s: s.activation(
                        spB[:, 0:512], spB[:, 512:1024], ACT.Exp, bias=0.0,
                        scale=-1.0))(m))),
                    ("vector", (lambda m: (lambda v: (
                        v.tensor_tensor(bufW1[:, 1024 * m:1024 * m + 512],
                                        wnum[:, 1024 * m:1024 * m + 512],
                                        den2[:, :], ALU.mult),
                        v.tensor_tensor(bufW1[:, 1024 * m + 512:1024 * (m + 1)],
                                        wnum[:, 1024 * m + 512:1024 * (m + 1)],
                                        spB[:, 0:512], ALU.mult))[-1]))(m)),
                ]
            ] + [
                ("vector", lambda v: v.tensor_tensor(bufX[:, :], bufW2[:, :],
                                                     bufW1[:, :], ALU.mult)),
            ])

        qk_prep(bufQ, gcpT, pcsT)
        barrier()
        qk_prep(bufK, pexpT, gcpT)
        barrier()

        steps = []
        vtr2 = {}
        for part in range(2):
            srcb = bufQ if part == 0 else bufK
            dstb = bufW1 if part == 0 else bufW2
            for g in range(8):
                gg = 8 * part + g
                bank = psA if gg % 2 == 0 else psB
                def t_t2(t, g=g, srcb=srcb, bank=bank):
                    for u in range(4):
                        tau = 4 * g + u
                        tr = t.transpose(bank[:, 128 * u:128 * (u + 1)],
                                         srcb[:, 128 * tau:128 * (tau + 1)],
                                         idbf[:, :])
                    return tr
                def v_t2(v, g=g, dstb=dstb, bank=bank):
                    return v.tensor_copy(dstb[:, 512 * g:512 * (g + 1)],
                                         bank[:, 0:512])
                ti = len(steps)
                steps.append(("tensor", t_t2, [vtr2[gg - 2]] if gg >= 2 else []))
                steps.append(("vector", v_t2, [ti]))
                vtr2[gg] = ti + 1
        chain2(steps, name="t2")
        barrier()

        qfm, kfm, outfm = bufW1, bufW2, bufG
        psS = [psS1, psS2]
        steps = []

        def add(eng, fn, deps):
            steps.append((eng, fn, [d for d in deps if d is not None]))
            return len(steps) - 1

        last_psS_rd = [None, None]
        last_pv_eS = [None, None]
        last_spB_rd = [None, None]
        last_vout = [None]
        last_vout_bank = [None, None]
        pending_den = [None]
        psOx = [psO, psP]
        mcount = 0

        for b in range(BPC):
            for hl in range(2):
                vtm = v0c if hl == 0 else v1c
                for m in range(4):
                    ob = mcount % 2
                    psOb = psOx[ob]
                    mcount += 1
                    qc0 = T * b + 512 * m
                    nj = 4 * m + 4
                    iex = {}

                    def mk_pv(j, b=b, nj=nj, vtm=vtm, psOb=psOb, ob=ob):
                        eS = eSb01[j % 2]
                        def t_pv(t, j=j, eS=eS):
                            return t.matmul(
                                psOb[:, 0:512],
                                vtm[:, 128 * (16 * b + j):128 * (16 * b + j + 1)],
                                eS[:, :],
                                start=(j == 0), stop=(j == nj - 1))
                        deps = [iex[j]]
                        if j == 0:
                            deps.append(last_vout_bank[ob])
                        ipv = add("tensor", t_pv, deps)
                        last_pv_eS[j % 2] = ipv
                        return ipv

                    for j in range(nj):
                        p = j % 2
                        hp = slice(512 * p, 512 * (p + 1))
                        eS = eSb01[p]

                        def t_qk(t, b=b, hl=hl, qc0=qc0, j=j, p=p):
                            return t.matmul(
                                psS[p][:, 0:512],
                                kfm[64 * hl:64 * hl + 64,
                                    T * b + 128 * j:T * b + 128 * (j + 1)],
                                qfm[64 * hl:64 * hl + 64, qc0:qc0 + 512],
                                start=True, stop=True)
                        iqk = add("tensor", t_qk, [last_psS_rd[p]])

                        if j >= 4 * m:
                            def v_cm(v, p=p, hp=hp, mv=j - 4 * m):
                                return v.scalar_tensor_tensor(
                                    spB[:, hp], psS[p][:, 0:512], 500.0,
                                    maskadd[:, 512 * mv:512 * (mv + 1)],
                                    ALU.min, ALU.add)
                            icm = add("vector", v_cm, [iqk, last_spB_rd[p]])
                            last_psS_rd[p] = icm

                            def s_ex(s, eS=eS, hp=hp):
                                return s.activation(eS[:, :], spB[:, hp],
                                                    ACT.Exp, bias=0.0,
                                                    scale=0.125)
                            iex[j] = add("scalar", s_ex,
                                         [icm, last_pv_eS[p]])
                            last_spB_rd[p] = iex[j]
                        else:
                            def s_ex(s, eS=eS, p=p):
                                return s.activation(eS[:, :], psS[p][:, 0:512],
                                                    ACT.Exp, bias=0.0,
                                                    scale=0.125)
                            iex[j] = add("scalar", s_ex,
                                         [iqk, last_pv_eS[p]])
                            last_psS_rd[p] = iex[j]

                        if j == 1 and pending_den[0] is not None:
                            pending_den[0]()
                            pending_den[0] = None
                        if j >= 1:
                            ipv = mk_pv(j - 1)
                    ipv_last = mk_pv(nj - 1)

                    def den_emit(hl=hl, qc0=qc0, ipv_last=ipv_last,
                                 psOb=psOb, ob=ob):
                        global_last = last_vout[0]
                        if hl == 0:
                            def s_rc(s):
                                s.activation(den2[64:65, :], psOb[64:65, 0:512],
                                             ACT.Ln)
                                return s.activation(den2[64:65, :],
                                                    den2[64:65, :],
                                                    ACT.Exp, bias=0.0,
                                                    scale=-1.0)
                            def t_bc(t):
                                return t.matmul(psR[0:64, :],
                                                onesf[64:65, 0:64],
                                                den2[64:65, :], start=True,
                                                stop=True,
                                                tile_position=(64, 0))
                            def v_out(v, qc0=qc0, psOb=psOb):
                                v.tensor_copy(den2[0:64, 0:512], psR[0:64, :])
                                return v.tensor_tensor(
                                    outfm[0:64, qc0:qc0 + 512],
                                    psOb[0:64, 0:512],
                                    den2[0:64, 0:512], ALU.mult)
                        else:
                            def s_rc(s):
                                s.activation(den2[0:1, :], psOb[0:1, 0:512],
                                             ACT.Ln)
                                return s.activation(den2[0:1, :], den2[0:1, :],
                                                    ACT.Exp, bias=0.0,
                                                    scale=-1.0)
                            def t_bc(t):
                                return t.matmul(psR[64:128, :],
                                                onesf[0:1, 0:64],
                                                den2[0:1, :], start=True,
                                                stop=True,
                                                tile_position=(0, 64))
                            def v_out(v, qc0=qc0, psOb=psOb):
                                v.tensor_copy(den2[64:128, 0:512],
                                              psR[64:128, :])
                                return v.tensor_tensor(
                                    outfm[64:128, qc0:qc0 + 512],
                                    psOb[64:128, 0:512],
                                    den2[64:128, 0:512], ALU.mult)
                        irc = add("scalar", s_rc, [ipv_last])
                        itb = add("tensor", t_bc, [irc, global_last])
                        ivo = add("vector", v_out, [itb])
                        last_vout[0] = ivo
                        last_vout_bank[ob] = ivo
                        return ivo

                    pending_den[0] = den_emit
        if pending_den[0] is not None:
            pending_den[0]()
            pending_den[0] = None
        chain2(steps, name="at")

        with nc.Block() as blk:
            @blk.sync
            def _(sy):
                sy.wait_ge(sem("at_vector"), cget("at_vector"))
                inc(sy.dma_start(out=out_spill[128 * c:128 * (c + 1), :],
                                 in_=outfm[:, :]), "dO", 16)
                sy.wait_ge(sem("dO"), cget("dO"))

        if c == 0:
            tb = {nm: tap(nm, [128, NT], BF16)
                  for nm in ("qfm0", "kfm0", "gcp0", "pexp0", "pcs0", "outfm0",
                             "v0c0")}
            srcs = {"qfm0": bufW1, "kfm0": bufW2, "gcp0": gcpT, "pexp0": pexpT,
                    "pcs0": pcsT, "outfm0": bufG, "v0c0": v0c}
            if any(t_ is not None for t_ in tb.values()):
                with nc.Block() as blk:
                    @blk.sync
                    def _(sy):
                        sy.wait_ge(sem("at_vector"), cget("at_vector"))
                        for nm, t_ in tb.items():
                            if t_ is not None:
                                inc(sy.dma_start(out=t_[:, :], in_=srcs[nm][:, :]),
                                    "dO", 16)
                        sy.wait_ge(sem("dO"), cget("dO"))
        barrier()

    stB.close()



    if "stopB" in DBG:
        stZ2 = ExitStack()
        zb2 = sbuf("zb2", [128, D], F32, stZ2)
        with nc.Block() as blk:
            @blk.vector
            def _(v):
                inc(v.memset(zb2[:, :], 0.0), "vZ2")

            @blk.sync
            def _(sy):
                sy.wait_ge(sem("vZ2"), cget("vZ2"))
                for tau in range(NTAU):
                    inc(sy.dma_start(out=y_out[128 * tau:128 * (tau + 1), :],
                                     in_=zb2[:, :]), "dO", 16)
                sy.wait_ge(sem("dO"), cget("dO"))
        stZ2.close()
        stB.close()
        stTab.close()
        stXN1.close()
        stack.close()
        return nc, taps

    stP = ExitStack()
    stgF = [sbuf("stgF0", [128, MH], F32, stP), sbuf("stgF1", [128, IFF], F32, stP)]
    sA_ = sbuf("sA_", [128, IFF], F32, stP)
    sB_ = sbuf("sB_", [128, IFF], F32, stP)
    sI_ = sbuf("sI_", [128, IFF], I32, stP)
    sG_ = sbuf("sG_", [128, IFF], BF16, stP)
    wI_ = sbuf("wI_", [128, IFF], BF16, stP)
    wT_ = sbuf("wT_", [128, IFF], BF16, stP)
    rmx = sbuf("rmx", [128, 1], F32, stP)

    barrier()

    wT2_ = sbuf("wT2_", [128, IFF], BF16, stP)
    wTd = [wT_, wT2_]

    qw_specs = ([(wgate, rc, D, scg, wgT_sp) for rc in range(NIC)] +
                [(wup, rc, D, scu, wuT_sp) for rc in range(NIC)] +
                [(wdown, rc, IFF, scd, wdT_sp) for rc in range(NCH)])
    steps = []

    def addq(eng, fn, deps):
        steps.append((eng, fn, [d for d in deps if d is not None]))
        return len(steps) - 1

    i_in, i_sc, i_ab, i_rd, i_out = {}, {}, {}, {}, {}
    i_ttr, i_vtr = {}, {}
    last_vtr_bank = [None, None]
    gctr = 0
    for it, (wsrc, rc, ncol, sccol, dst_sp) in enumerate(qw_specs):
        stg = stgF[it % 2]
        wT = wTd[it % 2]
        ncc = ncol // 128

        def g_in(g, wsrc=wsrc, rc=rc, ncol=ncol, stg=stg):
            inc(g.dma_start(out=stg[:, 0:ncol],
                            in_=wsrc[128 * rc:128 * (rc + 1), :]), "dQW", 16)
            g.wait_ge(sem("dQW"), cget("dQW"))
            return g.nop()
        i_in[it] = addq("gpsimd", g_in, [i_sc.get(it - 2)])

        def v_scale(v, rc=rc, ncol=ncol, sccol=sccol, stg=stg):
            v.reduce_max(rmx[:, :], stg[:, 0:ncol], AX.X,
                         apply_absolute_value=True)
            v.tensor_scalar(sccol[:, rc:rc + 1], rmx[:, :], 1.0 / 127.0,
                            None, ALU.mult)
            v.reciprocal(rmx[:, :], rmx[:, :])
            v.tensor_scalar(rmx[:, :], rmx[:, :], 127.0, None, ALU.mult)
            return v.tensor_scalar(sA_[:, 0:ncol], stg[:, 0:ncol],
                                   rmx[:, 0:1], 128.5, ALU.mult, ALU.add)
        i_sc[it] = addq("vector", v_scale, [i_in[it]])

        def v_round(v, ncol=ncol):
            v.tensor_copy(sI_[:, 0:ncol], sA_[:, 0:ncol])
            v.scalar_tensor_tensor(sB_[:, 0:ncol], sI_[:, 0:ncol], 1.0,
                                   sA_[:, 0:ncol], ALU.mult, ALU.is_le)
            v.tensor_tensor(sA_[:, 0:ncol], sI_[:, 0:ncol], sB_[:, 0:ncol],
                            ALU.add)
            return v.tensor_scalar(wI_[:, 0:ncol], sA_[:, 0:ncol], -129.0,
                                   None, ALU.add)
        i_rd[it] = addq("vector", v_round,
                        [i_sc[it], i_ttr.get((it - 1, -1))])

        ngrp = (ncc + 3) // 4
        for gi, g0 in enumerate(range(0, ncc, 4)):
            gw = min(4, ncc - g0)
            bank = psA if gctr % 2 == 0 else psB

            def t_tr(t, g0=g0, gw=gw, bank=bank):
                for u in range(gw):
                    tr = t.transpose(bank[:, 128 * u:128 * (u + 1)],
                                     wI_[:, 128 * (g0 + u):128 * (g0 + u + 1)],
                                     idbf[:, :])
                return tr
            i_ttr[(it, gi)] = addq("tensor", t_tr,
                                   [i_rd[it], last_vtr_bank[gctr % 2]])

            def v_tr(s, g0=g0, gw=gw, bank=bank, wT=wT):
                return s.activation(wT[:, 128 * g0:128 * (g0 + gw)],
                                    bank[:, 0:128 * gw], ACT.Copy)
            deps = [i_ttr[(it, gi)]]
            if gi == 0:
                deps.append(i_out.get(it - 2))
            i_vtr[(it, gi)] = addq("scalar", v_tr, deps)
            last_vtr_bank[gctr % 2] = i_vtr[(it, gi)]
            gctr += 1
        i_ttr[(it, -1)] = i_ttr[(it, ngrp - 1)]

        def sy_out(sy, rc=rc, ncc=ncc, dst_sp=dst_sp, wT=wT):
            for cc in range(ncc):
                inc(sy.dma_start(
                    out=dst_sp[128 * cc:128 * (cc + 1),
                               128 * rc:128 * (rc + 1)],
                    in_=wT[:, 128 * cc:128 * (cc + 1)]), "dQWo", 16)
            sy.wait_ge(sem("dQWo"), cget("dQWo"))
            return sy.nop()
        i_out[it] = addq("sync", sy_out, [i_vtr[(it, ngrp - 1)]])
    chain2(steps, name="qw")
    barrier()
    stP.close()
    stTab.close()
    stXN1.close()

    stC1 = ExitStack()
    x1fm = sbuf("x1fm", [128, NCH * NT], BF16, stC1)
    stC2 = ExitStack()
    outR = sbuf("outR", [128, NCH * NT], BF16, stC2)
    woS = sbuf("woS", [128, NCH * 1024], BF16, stC2)
    wstgC = sbuf("wstgC", [128, 1024], F32, stC2)
    xnstg = sbuf("xnstg", [128, NT], BF16, stC2)
    spC = sbuf("spC", [128, 512], F32, stC2)

    with nc.Block() as blk:
        @blk.sync
        def _(sy):
            for cc in range(NCH):
                inc(sy.dma_start(out=outR[:, NT * cc:NT * (cc + 1)],
                                 in_=out_spill[128 * cc:128 * (cc + 1), :]),
                    "dC1", 16)

        @blk.gpsimd
        def _(g):
            for kc in range(NCH):
                inc(g.dma_start(out=woS[:, 1024 * kc:1024 * (kc + 1)],
                                in_=wo[128 * kc:128 * (kc + 1), :]), "dC2", 16)

    with nc.Block() as blk:
        @blk.sync
        def _(sy):
            sy.wait_ge(sem("dC1"), cget("dC1"))
            for oc in range(NCH):
                if oc >= 1:
                    sy.wait_ge(sem("vC2"), oc)
                inc(sy.dma_start(out=xnstg[:, :],
                                 in_=xn_spill[128 * oc:128 * (oc + 1), :]),
                    "dC3", 16)

        @blk.tensor
        def _(t):
            t.wait_ge(sem("dC2"), cget("dC2"))
            t.wait_ge(sem("dC1"), cget("dC1"))
            for oc in range(NCH):
                for m in range(8):
                    idx = 8 * oc + m
                    if idx >= 2:
                        t.wait_ge(sem("vC2x"), idx - 1)
                    ps = psP if m % 2 == 0 else psQ
                    for kc in range(NCH):
                        mm = t.matmul(
                            ps[:, 0:512],
                            woS[:, 1024 * kc + 128 * oc:1024 * kc + 128 * (oc + 1)],
                            outR[:, NT * kc + 512 * m:NT * kc + 512 * (m + 1)],
                            start=(kc == 0), stop=(kc == NCH - 1))
                    inc(mm, "tC1")

        @blk.vector
        def _(v):
            v = DW(v)
            for oc in range(NCH):
                v.wait_ge(sem("dC3"), 16 * (oc + 1))
                for m in range(8):
                    idx = 8 * oc + m
                    v.wait_ge(sem("tC1"), idx + 1)
                    ps = psP if m % 2 == 0 else psQ
                    v.tensor_scalar(spC[:, 0:512], ps[:, 0:512],
                                    bocol[:, oc:oc + 1], None, ALU.add)
                    last = v.tensor_tensor(
                        x1fm[:, NT * oc + 512 * m:NT * oc + 512 * (m + 1)],
                        spC[:, 0:512], xnstg[:, 512 * m:512 * (m + 1)], ALU.add)
                    inc(last, "vC2x")
                inc(v.memset(spC[0:1, 0:1], 0.0), "vC2")
    barrier()
    stC2.close()

    stC3 = ExitStack()
    w2S = sbuf("w2S", [128, NMH * 1024], BF16, stC3)
    w1blk = sbuf("w1blk", [128, 2048], BF16, stC3)
    hact = sbuf("hact", [128, NMH * 512], BF16, stC3)
    h0m = sbuf("h0m", [128, NCH * 512], BF16, stC3)
    sqm = sbuf("sqm", [128, NCH * 512], BF16, stC3)
    sm1 = sbuf("sm1", [128, 512], F32, stC3)
    sm2 = sbuf("sm2", [128, 512], F32, stC3)
    sm3 = sbuf("sm3", [128, 512], F32, stC3)
    wstgC2 = sbuf("wstgC2", [128, 1024], F32, stC3)
    sOut = sbuf("sOut", [128, 1024], BF16, stC3)

    with nc.Block() as blk:
        @blk.sync
        def _(sy):
            for hc in range(NMH):
                if hc >= 1:
                    sy.wait_ge(sem("vC3"), hc)
                inc(sy.dma_start(out=wstgC2[:, :],
                                 in_=w2[128 * hc:128 * (hc + 1), :]), "dC4", 16)

        @blk.vector
        def _(v):
            v = DW(v)
            for hc in range(NMH):
                v.wait_ge(sem("dC4"), 16 * (hc + 1))
                inc(v.tensor_copy(w2S[:, 1024 * hc:1024 * (hc + 1)],
                                  wstgC2[:, :]), "vC3")

    for m in range(8):
        steps = []

        def add(eng, fn, deps):
            steps.append((eng, fn, [d for d in deps if d is not None]))
            return len(steps) - 1

        def t_mu(t, m=m):
            for kc in range(NCH):
                mm = t.matmul(psQ[0:1, 0:512], onesb[:, 0:1],
                              x1fm[:, NT * kc + 512 * m:NT * kc + 512 * (m + 1)],
                              start=(kc == 0), stop=(kc == NCH - 1))
            return mm
        i_tmu = add("tensor", t_mu, [])

        def v_sq(v, m=m):
            for kc in range(NCH):
                cp = v.tensor_tensor(
                    sqm[:, 512 * kc:512 * (kc + 1)],
                    x1fm[:, NT * kc + 512 * m:NT * kc + 512 * (m + 1)],
                    x1fm[:, NT * kc + 512 * m:NT * kc + 512 * (m + 1)], ALU.mult)
            return cp
        i_vsq = add("vector", v_sq, [])

        def t_sq(t, m=m):
            for kc in range(NCH):
                mm = t.matmul(psR[0:1, 0:512], onesb[:, 0:1],
                              sqm[:, 512 * kc:512 * (kc + 1)],
                              start=(kc == 0), stop=(kc == NCH - 1))
            return mm
        i_tsq = add("tensor", t_sq, [i_vsq])

        def v_stats(v, m=m):
            v.tensor_scalar(sm1[0:1, :], psQ[0:1, 0:512], 1.0 / D, None, ALU.mult)
            v.tensor_scalar(sm2[0:1, :], psR[0:1, 0:512], 1.0 / D, None, ALU.mult)
            v.tensor_tensor(sm3[0:1, :], sm1[0:1, :], sm1[0:1, :], ALU.mult)
            return v.tensor_tensor(sm2[0:1, :], sm2[0:1, :], sm3[0:1, :],
                                   ALU.subtract)
        i_vst = add("vector", v_stats, [i_tmu, i_tsq])
        i_srs = add("scalar",
                    lambda s: s.activation(sm3[0:1, :], sm2[0:1, :], ACT.Sqrt,
                                           bias=kcol[0:1, 0:1], scale=1.0),
                    [i_vst])
        i_vrs = add("vector", lambda v: v.reciprocal(sm2[0:1, :], sm3[0:1, :]),
                    [i_srs])

        def t_bc(t, m=m):
            t.matmul(psQ[:, 0:512], onesf[0:1, :], sm1[0:1, :],
                     start=True, stop=True)
            return t.matmul(psR[:, 0:512], onesf[0:1, :], sm2[0:1, :],
                            start=True, stop=True)
        i_tbc = add("tensor", t_bc, [i_vrs])

        def v_bc(v, m=m):
            v.tensor_copy(sm1[:, :], psQ[:, 0:512])
            return v.tensor_copy(sm2[:, :], psR[:, 0:512])
        i_vbc = add("vector", v_bc, [i_tbc])

        def v_nrm(v, m=m):
            for kc in range(NCH):
                v.tensor_tensor(sm3[:, :],
                                x1fm[:, NT * kc + 512 * m:NT * kc + 512 * (m + 1)],
                                sm1[:, :], ALU.subtract)
                v.tensor_tensor(sm3[:, :], sm3[:, :], sm2[:, :], ALU.mult)
                cp = v.tensor_scalar(h0m[:, 512 * kc:512 * (kc + 1)], sm3[:, :],
                                     g2col[:, kc:kc + 1], b2col[:, kc:kc + 1],
                                     ALU.mult, ALU.add)
            return cp
        i_vnrm = add("vector", v_nrm, [i_vbc])

        i_sy = {}
        i_t1 = {}
        i_s1 = {}
        for hc in range(NMH):
            wslc = slice(1024 * (hc % 2), 1024 * (hc % 2 + 1))

            def sy_w1(sy, m=m, hc=hc, wslc=wslc):
                if hc == 0:
                    sy.wait_ge(sem("dW1c"), 128)
                inc(sy.dma_start(
                    out=w1blk[:, wslc],
                    in_=AP(w1bf, hc * NCH * 128 * 128,
                           [[128, 128], [128 * 128, NCH], [1, 128]])), "dW1b", 16)
                sy.wait_ge(sem("dW1b"), cget("dW1b"))
                return sy.nop()
            i_sy[hc] = add("sync", sy_w1, [i_t1.get(hc - 2)])

            def t_m1(t, m=m, hc=hc, wslc=wslc):
                ps = psP if hc % 2 == 0 else psQ
                for kc in range(NCH):
                    mm = t.matmul(ps[:, 0:512],
                                  w1blk[:, wslc][:, 128 * kc:128 * (kc + 1)],
                                  h0m[:, 512 * kc:512 * (kc + 1)],
                                  start=(kc == 0), stop=(kc == NCH - 1))
                return mm
            i_t1[hc] = add("tensor", t_m1,
                           [i_sy[hc], i_vnrm, i_s1.get(hc - 2)])

            def s_m1(s, m=m, hc=hc):
                ps = psP if hc % 2 == 0 else psQ
                return s.activation(hact[:, 512 * hc:512 * (hc + 1)],
                                    ps[:, 0:512], ACT.Gelu,
                                    bias=bm1col[:, hc:hc + 1], scale=1.0)
            i_s1[hc] = add("scalar", s_m1, [i_t1[hc]])

        i_t2 = {}
        i_v2 = {}
        i_sp = {}
        for oc in range(NCH):
            oslc = slice(512 * (oc % 2), 512 * (oc % 2 + 1))

            def t_m2(t, m=m, oc=oc):
                ps = psS1 if oc % 2 == 0 else psS2
                for hc in range(NMH):
                    mm = t.matmul(
                        ps[:, 0:512],
                        w2S[:, 1024 * hc + 128 * oc:1024 * hc + 128 * (oc + 1)],
                        hact[:, 512 * hc:512 * (hc + 1)],
                        start=(hc == 0), stop=(hc == NMH - 1))
                return mm
            i_t2[oc] = add("tensor", t_m2,
                           [i_s1[NMH - 1], i_v2.get(oc - 2)])

            def v_m2(v, m=m, oc=oc, oslc=oslc):
                ps = psS1 if oc % 2 == 0 else psS2
                v.tensor_scalar(sm3[:, :], ps[:, 0:512], bm2col[:, oc:oc + 1],
                                None, ALU.add)
                return v.tensor_tensor(
                    sOut[:, oslc], sm3[:, :],
                    x1fm[:, NT * oc + 512 * m:NT * oc + 512 * (m + 1)], ALU.add)
            i_v2[oc] = add("vector", v_m2, [i_t2[oc], i_sp.get(oc - 2)])

            def sy_sp(sy, m=m, oc=oc, oslc=oslc):
                inc(sy.dma_start(out=s_spill[128 * oc:128 * (oc + 1),
                                             512 * m:512 * (m + 1)],
                                 in_=sOut[:, oslc]), "dSo", 16)
                sy.wait_ge(sem("dSo"), cget("dSo"))
                return sy.nop()
            i_sp[oc] = add("sync", sy_sp, [i_v2[oc]])
        chain2(steps, name="c2m")
        barrier()
    barrier()
    stC3.close()

    tcx = tap("x1fm", [128, NCH * NT], BF16)
    if tcx is not None:
        with nc.Block() as blk:
            @blk.sync
            def _(sy):
                inc(sy.dma_start(out=tcx[:, :], in_=x1fm[:, :]), "dO", 16)
                sy.wait_ge(sem("dO"), cget("dO"))
    stC1.close()

    stD = ExitStack()
    xqA = sbuf("xqA", [128, NCH * NT], BF16, stD)
    stD1 = ExitStack()
    wgS = sbuf("wgS", [128, NCH * IFF], BF16, stD1)
    wuS = sbuf("wuS", [128, NCH * IFF], BF16, stD1)
    xnstg2 = sbuf("xnstg2", [128, 2 * NT], BF16, stD1)
    dq1 = sbuf("dq1", [128, 512], F32, stD1)
    dq2 = sbuf("dq2", [128, 512], F32, stD1)
    dqI = sbuf("dqI", [128, 512], I32, stD1)
    dqS = sbuf("dqS", [128, 512], BF16, stD1)
    svg = sbuf("svg", [128, NIC], F32, stD1)
    svu = sbuf("svu", [128, NIC], F32, stD1)
    clpS = sbuf("clpS", [1, 2], F32, stD1)
    midT = sbuf("midT", [128, 1024], BF16, stD1)
    sgT = sbuf("sgT", [128, 1024], BF16, stD1)
    dmx = sbuf("dmx", [128, 1], F32, stD1)

    with nc.Block() as blk:
        @blk.sync
        def _(sy):
            sy.wait_ge(sem("cc"), cget("cc"))
            inc(sy.dma_start(out=rowm[0:1, :], in_=cc1o[0:1, :]), "dD0", 16)
            inc(sy.dma_start(out=clpS[0:1, 0:1], in_=clipx.ap()), "dD0", 16)
            inc(sy.dma_start(out=clpS[0:1, 1:2], in_=clipd.ap()), "dD0", 16)

        @blk.scalar
        def _(s):
            s = DW(s)
            s.wait_ge(sem("dD0"), cget("dD0"))
            inc(s.activation(clpS[0:1, 0:1], clpS[0:1, 0:1], ACT.Exp,
                             bias=0.0, scale=-1.0), "sD0")

        @blk.vector
        def _(v):
            v = DW(v)
            v.wait_ge(sem("sD0"), cget("sD0"))
            v.tensor_scalar(clpS[0:1, 0:1], clpS[0:1, 0:1], 1.0, None, ALU.add)
            v.reciprocal(clpS[0:1, 0:1], clpS[0:1, 0:1])
            v.tensor_tensor(ccrow[0:1, 1:2], clpS[0:1, 0:1], rowm[0:1, 0:1],
                            ALU.mult)
            v.tensor_scalar(ccrow[0:1, 1:2], ccrow[0:1, 1:2], 1.0 / 127.0, None,
                            ALU.mult)
            inc(v.reciprocal(ccrow[0:1, 2:3], ccrow[0:1, 1:2]), "vD0")

        @blk.tensor
        def _(t):
            t.wait_ge(sem("vD0"), cget("vD0"))
            t.matmul(psR[:, 0:1], onesf[0:1, :], ccrow[0:1, 1:2],
                     start=True, stop=True)
            inc(t.matmul(psR[:, 1:2], onesf[0:1, :], ccrow[0:1, 2:3],
                         start=True, stop=True), "tD0")

    with nc.Block() as blk:
        @blk.vector
        def _(v):
            v = DW(v)
            v.wait_ge(sem("tD0"), cget("tD0"))
            v.tensor_copy(sxv[:, :], psR[:, 0:1])
            v.tensor_copy(rsxv[:, :], psR[:, 1:2])
            v.tensor_scalar(svg[:, :], scg[:, :], sxv[:, 0:1], None, ALU.mult)
            inc(v.tensor_scalar(svu[:, :], scu[:, :], sxv[:, 0:1], None,
                                ALU.mult), "vD1")

        @blk.gpsimd
        def _(g):
            for kc in range(NCH):
                inc(g.dma_start(out=wgS[:, IFF * kc:IFF * (kc + 1)],
                                in_=wgT_sp[128 * kc:128 * (kc + 1), :]), "dDg", 16)
                inc(g.dma_start(out=wuS[:, IFF * kc:IFF * (kc + 1)],
                                in_=wuT_sp[128 * kc:128 * (kc + 1), :]), "dDg", 16)

    def quant_tile(v, srcap, dstap, rvec, do_clip):
        v.tensor_scalar(dq1[:, :], srcap, rvec, None, ALU.mult)
        return None

    steps = []
    i_d1in, i_d1v = {}, {}

    def add1(eng, fn, deps):
        steps.append((eng, fn, [d for d in deps if d is not None]))
        return len(steps) - 1

    for cc in range(NCH):
        st0 = NT * (cc % 2)

        def sy_xn(sy, cc=cc, st0=st0):
            inc(sy.dma_start(out=xnstg2[:, st0:st0 + NT],
                             in_=xn_spill[128 * cc:128 * (cc + 1), :]),
                "dD1", 16)
            sy.wait_ge(sem("dD1"), cget("dD1"))
            return sy.nop()
        i_d1in[cc] = add1("sync", sy_xn, [i_d1v.get(cc - 2)])

        def v_q(v, cc=cc, st0=st0):
            for mm_ in range(8):
                v.tensor_scalar(dq1[:, :],
                                xnstg2[:, st0 + 512 * mm_:st0 + 512 * (mm_ + 1)],
                                rsxv[:, 0:1], 128.5, ALU.mult, ALU.add)
                v.tensor_scalar(dq1[:, :], dq1[:, :], 0.0, None, ALU.max)
                v.tensor_copy(dqI[:, :], dq1[:, :])
                v.scalar_tensor_tensor(dq2[:, :], dqI[:, :], 1.0, dq1[:, :],
                                       ALU.mult, ALU.is_le)
                v.tensor_tensor(dq1[:, :], dqI[:, :], dq2[:, :], ALU.add)
                last = v.tensor_scalar(
                    xqA[:, NT * cc + 512 * mm_:NT * cc + 512 * (mm_ + 1)],
                    dq1[:, :], -129.0, 127.0, ALU.add, ALU.min)
            return last
        i_d1v[cc] = add1("vector", v_q, [i_d1in[cc]])
    with nc.Block() as blk:
        @blk.vector
        def _(v):
            v.wait_ge(sem("vD1"), cget("vD1"))
        @blk.sync
        def _(sy):
            sy.wait_ge(sem("dDg"), cget("dDg"))
    chain2(steps, name="d1")

    with nc.Block() as blk:
        @blk.tensor
        def _(t):
            t.wait_ge(sem("dDg"), cget("dDg"))
    barrier()

    steps = []

    def addd(eng, fn, deps):
        steps.append((eng, fn, [d for d in deps if d is not None]))
        return len(steps) - 1

    i_t = {}
    i_s = {}
    i_v = {}
    i_y = {}
    it = 0
    for m in range(8):
        for ic in range(NIC):
            pg = psP if it % 2 == 0 else psS1
            pu = psQ if it % 2 == 0 else psS2
            hp = slice(512 * (it % 2), 512 * (it % 2 + 1))

            def t_gu(t, m=m, ic=ic, pg=pg, pu=pu):
                for kc in range(NCH):
                    t.matmul(pg[:, 0:512],
                             wgS[:, IFF * kc + 128 * ic:IFF * kc + 128 * (ic + 1)],
                             xqA[:, NT * kc + 512 * m:NT * kc + 512 * (m + 1)],
                             start=(kc == 0), stop=(kc == NCH - 1))
                for kc in range(NCH):
                    mm = t.matmul(
                        pu[:, 0:512],
                        wuS[:, IFF * kc + 128 * ic:IFF * kc + 128 * (ic + 1)],
                        xqA[:, NT * kc + 512 * m:NT * kc + 512 * (m + 1)],
                        start=(kc == 0), stop=(kc == NCH - 1))
                return mm
            i_t[it] = addd("tensor", t_gu, [i_s.get(it - 2), i_v.get(it - 2)])

            def s_gu(s, m=m, ic=ic, pg=pg, hp=hp):
                return s.activation(sgT[:, hp], pg[:, 0:512], ACT.Silu,
                                    bias=0.0, scale=svg[:, ic:ic + 1])
            i_s[it] = addd("scalar", s_gu, [i_t[it], i_v.get(it - 2)])

            def v_gu(v, m=m, ic=ic, pu=pu, hp=hp):
                v.scalar_tensor_tensor(midT[:, hp], pu[:, 0:512],
                                       svu[:, ic:ic + 1], sgT[:, hp],
                                       ALU.mult, ALU.mult)
                v.reduce_max(dmx[:, :], midT[:, hp], AX.X,
                             apply_absolute_value=True)
                return v.tensor_tensor(rmaxcol2[:, :], rmaxcol2[:, :], dmx[:, :],
                                       ALU.max)
            i_v[it] = addd("vector", v_gu, [i_s[it], i_y.get(it - 2)])

            def sy_gu(sy, m=m, ic=ic, hp=hp):
                inc(sy.dma_start(out=mid_spill[128 * ic:128 * (ic + 1),
                                               512 * m:512 * (m + 1)],
                                 in_=midT[:, hp]), "dMo", 16)
                sy.wait_ge(sem("dMo"), cget("dMo"))
                return sy.nop()
            i_y[it] = addd("sync", sy_gu, [i_v[it]])
            it += 1
    chain2(steps, name="d2")
    barrier()
    stD1.close()

    stD2 = ExitStack()
    wdS = sbuf("wdS", [128, NIC * 1024], BF16, stD2)
    mqA = sbuf("mqA", [128, 2 * NIC * 512], BF16, stD2)
    mres = sbuf("mres", [128, 2 * NCH * 512], BF16, stD2)
    sStg = sbuf("sStg", [128, 1024], BF16, stD2)
    xStg = sbuf("xStg", [128, 2048], F32, stD2)
    ysb = sbuf("ysb", [128, 2048], F32, stD2)
    dq1b = sbuf("dq1b", [128, 512], F32, stD2)
    dq2b = sbuf("dq2b", [128, 512], F32, stD2)
    dqIb = sbuf("dqIb", [128, 512], I32, stD2)
    dqSb = sbuf("dqSb", [128, 512], BF16, stD2)
    sgT2 = sbuf("sgT2", [128, 1024], BF16, stD2)
    svdb = sbuf("svdb", [128, NCH], F32, stD2)
    clp2 = sbuf("clp2", [1, 2], F32, stD2)

    with nc.Block() as blk:
        @blk.sync
        def _(sy):
            inc(sy.dma_start(out=maxrt2[0:1, :], in_=rmaxcol2[:, 0:1]), "dD2", 16)
            sy.wait_ge(sem("dD2"), cget("dD2"))
            inc(sy.dma_start(out=rowm[0:1, :], in_=maxrt2[0:1, :]), "dD3", 16)

        @blk.vector
        def _(v):
            v = DW(v)
            v.wait_ge(sem("dD3"), cget("dD3"))
            inc(v.reduce_max(cc2row[0:1, 0:1], rowm[0:1, :], AX.X), "vD3")

        @blk.gpsimd
        def _(g):
            g.wait_ge(sem("vD3"), cget("vD3"))
            inc(g.dma_start(out=cc2i[0:1, :], in_=cc2row[0:1, :]), "dW", 16)
            g.wait_ge(sem("dW"), cget("dW"))
            if "nocc" in DBG:
                inc(g.dma_start(out=cc2o[0:1, :], in_=cc2i[0:1, :]), "cc2", 16)
            else:
                inc(g.collective_compute(
                    "AllReduce", ALU.max, replica_groups=[list(range(8))],
                    ins=[cc2i.ap().opt()], outs=[cc2o.ap().opt()]), "cc2", 1)

    with nc.Block() as blk:
        @blk.sync
        def _(sy):
            sy.wait_ge(sem("cc2"), cget("cc2"))
            inc(sy.dma_start(out=rowm[0:1, :], in_=cc2o[0:1, :]), "dD4", 16)
            inc(sy.dma_start(out=clp2[0:1, 1:2], in_=clipd.ap()), "dD4", 16)
            for ic in range(NIC):
                inc(sy.dma_start(out=wdS[:, 1024 * ic:1024 * (ic + 1)],
                                 in_=wdT_sp[128 * ic:128 * (ic + 1), :]),
                    "dD5", 16)

        @blk.scalar
        def _(s):
            s = DW(s)
            s.wait_ge(sem("dD4"), cget("dD4"))
            inc(s.activation(clp2[0:1, 1:2], clp2[0:1, 1:2], ACT.Exp,
                             bias=0.0, scale=-1.0), "sD1")

        @blk.vector
        def _(v):
            v = DW(v)
            v.wait_ge(sem("sD1"), cget("sD1"))
            v.tensor_scalar(clp2[0:1, 1:2], clp2[0:1, 1:2], 1.0, None, ALU.add)
            v.reciprocal(clp2[0:1, 1:2], clp2[0:1, 1:2])
            v.tensor_tensor(ccrow[0:1, 5:6], clp2[0:1, 1:2], rowm[0:1, 0:1],
                            ALU.mult)
            v.tensor_scalar(ccrow[0:1, 5:6], ccrow[0:1, 5:6], 1.0 / 127.0, None,
                            ALU.mult)
            inc(v.reciprocal(ccrow[0:1, 6:7], ccrow[0:1, 5:6]), "vD4")

        @blk.tensor
        def _(t):
            t.wait_ge(sem("vD4"), cget("vD4"))
            t.matmul(psR[:, 2:3], onesf[0:1, :], ccrow[0:1, 5:6],
                     start=True, stop=True)
            inc(t.matmul(psR[:, 3:4], onesf[0:1, :], ccrow[0:1, 6:7],
                         start=True, stop=True), "tD1")

    with nc.Block() as blk:
        @blk.vector
        def _(v):
            v = DW(v)
            v.wait_ge(sem("tD1"), cget("tD1"))
            v.tensor_copy(sdv[:, :], psR[:, 2:3])
            v.tensor_copy(rsdv[:, :], psR[:, 3:4])
            inc(v.tensor_scalar(svdb[:, :], scd[:, :], sdv[:, 0:1], None,
                                ALU.mult), "vD5")

    steps = []

    def addf(eng, fn, deps):
        steps.append((eng, fn, [d for d in deps if d is not None]))
        return len(steps) - 1

    i_mid = {}
    i_q = {}
    i_tdn = {}
    i_sdn = {}
    i_vdn = {}
    i_tf = {}
    i_vf = {}
    i_yy = {}
    for m in range(8):
        mq0 = NIC * 512 * (m % 2)

        def sy_mid(sy, m=m, mq0=mq0):
            for ic in range(NIC):
                inc(sy.dma_start(out=mqA[:, mq0 + 512 * ic:mq0 + 512 * (ic + 1)],
                                 in_=mid_spill[128 * ic:128 * (ic + 1),
                                               512 * m:512 * (m + 1)]),
                    "dD6", 16)
            sy.wait_ge(sem("dD6"), cget("dD6"))
            return sy.nop()
        i_mid[m] = addf("sync", sy_mid, [i_tdn.get((m - 2, NCH - 1))])

        for ic in range(NIC):
            def v_mq(v, m=m, ic=ic, mq0=mq0):
                v.tensor_scalar(dq1b[:, :],
                                mqA[:, mq0 + 512 * ic:mq0 + 512 * (ic + 1)],
                                rsdv[:, 0:1], 128.5, ALU.mult, ALU.add)
                v.tensor_scalar(dq1b[:, :], dq1b[:, :], 0.0, None, ALU.max)
                v.tensor_copy(dqIb[:, :], dq1b[:, :])
                v.scalar_tensor_tensor(dq2b[:, :], dqIb[:, :], 1.0, dq1b[:, :],
                                       ALU.mult, ALU.is_le)
                v.tensor_tensor(dq1b[:, :], dqIb[:, :], dq2b[:, :], ALU.add)
                return v.tensor_scalar(mqA[:, mq0 + 512 * ic:mq0 + 512 * (ic + 1)],
                                       dq1b[:, :], -129.0, 127.0, ALU.add,
                                       ALU.min)
            i_q[(m, ic)] = addf("vector", v_mq, [i_mid[m]])

        for oc in range(NCH):
            it2 = 8 * m + oc
            pd = psP if it2 % 2 == 0 else psQ
            hp = slice(512 * (it2 % 2), 512 * (it2 % 2 + 1))

            def t_dn(t, m=m, oc=oc, pd=pd, mq0=mq0):
                for ic in range(NIC):
                    mm = t.matmul(
                        pd[:, 0:512],
                        wdS[:, 1024 * ic + 128 * oc:1024 * ic + 128 * (oc + 1)],
                        mqA[:, mq0 + 512 * ic:mq0 + 512 * (ic + 1)],
                        start=(ic == 0), stop=(ic == NIC - 1))
                return mm
            i_tdn[(m, oc)] = addf("tensor", t_dn,
                                  [i_q[(m, NIC - 1)], i_sdn.get((m, oc - 2)),
                                   i_tf.get((m - 1, 3))])

            def s_dn(s, m=m, oc=oc, pd=pd, hp=hp):
                return s.activation(sgT2[:, hp], pd[:, 0:512], ACT.Silu,
                                    bias=0.0, scale=svdb[:, oc:oc + 1])
            i_sdn[(m, oc)] = addf("scalar", s_dn,
                                  [i_tdn[(m, oc)], i_vdn.get((m, oc - 2))])

            def sy_s(sy, m=m, oc=oc, hp=hp):
                inc(sy.dma_start(out=sStg[:, hp],
                                 in_=s_spill[128 * oc:128 * (oc + 1),
                                             512 * m:512 * (m + 1)]), "dD7", 16)
                sy.wait_ge(sem("dD7"), cget("dD7"))
                return sy.nop()
            isys = addf("sync", sy_s, [i_vdn.get((m, oc - 2))])

            def v_dn(g, m=m, oc=oc, hp=hp):
                return g.tensor_tensor(
                    mres[:, NCH * 512 * (m % 2) + 512 * oc:
                         NCH * 512 * (m % 2) + 512 * (oc + 1)],
                    sgT2[:, hp], sStg[:, hp], ALU.mult)
            i_vdn[(m, oc)] = addf("gpsimd", v_dn,
                                  [i_sdn[(m, oc)], isys,
                                   i_tf.get((m - 1, 3))])

        for u in range(4):
            tau = 4 * m + u
            bank = psA if u % 2 == 0 else psB
            xh = slice(1024 * (u % 2), 1024 * (u % 2 + 1))

            def t_fin(t, m=m, u=u, bank=bank):
                for oc in range(NCH):
                    tr = t.transpose(
                        bank[:, 128 * oc:128 * (oc + 1)],
                        mres[:, NCH * 512 * (m % 2) + 512 * oc + 128 * u:
                             NCH * 512 * (m % 2) + 512 * oc + 128 * (u + 1)],
                        idbf[:, :])
                return tr
            i_tf[(m, u)] = addf("tensor", t_fin,
                                [i_vdn[(m, NCH - 1)], i_vf.get((m, u - 2))])

            def sy_x(sy, tau=tau, xh=xh):
                inc(sy.dma_start(out=xStg[:, xh],
                                 in_=x_in[128 * tau:128 * (tau + 1), :]), "dD8", 16)
                sy.wait_ge(sem("dD8"), cget("dD8"))
                return sy.nop()
            isyx = addf("sync", sy_x, [i_vf.get((m, u - 2))])

            def v_fin(v, tau=tau, bank=bank, xh=xh):
                return v.tensor_tensor(ysb[:, xh], bank[:, 0:1024], xStg[:, xh],
                                       ALU.add)
            i_vf[(m, u)] = addf("vector", v_fin,
                                [i_tf[(m, u)], isyx, i_yy.get((m, u - 2))])

            def sy_y(sy, tau=tau, xh=xh):
                inc(sy.dma_start(out=y_out[128 * tau:128 * (tau + 1), :],
                                 in_=ysb[:, xh]), "dY", 16)
                sy.wait_ge(sem("dY"), cget("dY"))
                return sy.nop()
            i_yy[(m, u)] = addf("sync", sy_y, [i_vf[(m, u)]])
    with nc.Block() as blk:
        @blk.tensor
        def _(t):
            t.wait_ge(sem("vD5"), cget("vD5"))
        @blk.sync
        def _(sy):
            sy.wait_ge(sem("dD5"), cget("dD5"))
    barrier()
    chain2(steps, name="d4")
    barrier()
    stD2.close()
    stD.close()


    stack.close()
    return nc, taps


def _consts():
    one = np.ones((128, 128), dtype=np.float32)
    return {
        "c_idbf": np.eye(128, dtype=ml_dtypes.bfloat16),
        "c_ut": np.triu(one).astype(ml_dtypes.bfloat16),
        "c_sut": np.triu(one, 1).astype(ml_dtypes.bfloat16),
        "c_onesb": one.astype(ml_dtypes.bfloat16),
        "c_onesf": one,
        "c_iota128": np.arange(128, dtype=np.float32).reshape(128, 1),
        "c_iota32": np.arange(32, dtype=np.float32).reshape(1, 32),
    }


def _in_maps(inputs):
    x = np.ascontiguousarray(np.asarray(inputs["L0_attn_x"], dtype=np.float32))
    shared = {
        "wq": inputs["L0_attn_Wq"], "wk": inputs["L0_attn_Wk"],
        "wv": inputs["L0_attn_Wv"], "wgt": inputs["L0_attn_gate_w"],
        "wp": inputs["L0_attn_p_w"], "wo": inputs["L0_attn_out_w"],
        "wclk": inputs["L0_attn_clock_w"],
        "b_gate": inputs["L0_attn_gate_b"], "b_out": inputs["L0_attn_out_b"],
        "b_clk": inputs["L0_attn_clock_b"],
        "g1": inputs["L0_attn_ln1_g"], "b1": inputs["L0_attn_ln1_b"],
        "g2": inputs["L0_attn_ln2_g"], "b2": inputs["L0_attn_ln2_b"],
        "w1": inputs["L0_attn_mlp_w1"], "bm1": inputs["L0_attn_mlp_b1"],
        "w2": inputs["L0_attn_mlp_w2"], "bm2": inputs["L0_attn_mlp_b2"],
        "clipx": inputs["L0_ffn_clip_x"], "clipd": inputs["L0_ffn_clip_down"],
        "wgate": inputs["L0_ffn_w_gate"], "wup": inputs["L0_ffn_w_up"],
        "wdown": inputs["L0_ffn_w_down"],
    }
    shared = {k: np.ascontiguousarray(np.asarray(v, dtype=np.float32))
              for k, v in shared.items()}
    shared.update(_consts())
    maps = []
    for core in range(8):
        m = dict(shared)
        m["x"] = np.ascontiguousarray(
            x[core * BPC:(core + 1) * BPC].reshape(NT, D))
        maps.append(m)
    return maps


_RES = {}


def kernel(**inputs):
    nc, taps = build_program()
    maps = _in_maps(inputs)
    res = run_bass_kernel_spmd(nc, maps, core_ids=list(range(8)))
    _RES["results"] = res.results
    ys = [np.asarray(res.results[c]["y"]).reshape(BPC, T, D) for c in range(8)]
    return np.concatenate(ys, axis=0)


# version 18
# speedup vs baseline: 1.0549x; 1.0195x over previous
import math
import numpy as np
import ml_dtypes
from contextlib import ExitStack

import concourse.bass as bass
import concourse.mybir as mybir
from concourse.bass_utils import run_bass_kernel_spmd

F32 = mybir.dt.float32
BF16 = mybir.dt.bfloat16
I32 = mybir.dt.int32
ACT = mybir.ActivationFunctionType
ALU = mybir.AluOpType
AX = mybir.AxisListType
AP = bass.AP

D, T, BPC = 1024, 2048, 2
NT = BPC * T
H, DH = 16, 64
NCH, NTAU = 8, 32
IFF, NIC = 2816, 22
MH, NMH = 4096, 32
EPS = 1e-5
TWO_PI = 2.0 * math.pi
PI_CLAMP = 3.1415920
RND_CAP = 128.49

DBG = []


def build_program():
    nc = bass.Bass("TRN2", target_bir_lowering=False)

    def din(name, shape):
        return nc.dram_tensor(name, shape, F32, kind="ExternalInput")

    x_in = din("x", [NT, D])
    wq, wk, wv, wg, wp, wo = (din(n, [D, D]) for n in
                              ("wq", "wk", "wv", "wgt", "wp", "wo"))
    wclk = din("wclk", [D, H])
    b_gate, b_out, b_clk = din("b_gate", [D]), din("b_out", [D]), din("b_clk", [H])
    g1v, b1v = din("g1", [D]), din("b1", [D])
    g2v, b2v = din("g2", [D]), din("b2", [D])
    w1, bm1, w2, bm2 = din("w1", [D, MH]), din("bm1", [MH]), din("w2", [MH, D]), din("bm2", [D])
    clipx, clipd = din("clipx", [1]), din("clipd", [1])
    wgate, wup, wdown = din("wgate", [IFF, D]), din("wup", [IFF, D]), din("wdown", [D, IFF])
    c_idbf = nc.dram_tensor("c_idbf", [128, 128], BF16, kind="ExternalInput")
    c_ut = nc.dram_tensor("c_ut", [128, 128], BF16, kind="ExternalInput")
    c_sut = nc.dram_tensor("c_sut", [128, 128], BF16, kind="ExternalInput")
    c_onesb = nc.dram_tensor("c_onesb", [128, 128], BF16, kind="ExternalInput")
    c_onesf = nc.dram_tensor("c_onesf", [128, 128], F32, kind="ExternalInput")
    c_iota128 = nc.dram_tensor("c_iota128", [128, 1], F32, kind="ExternalInput")
    c_iota32 = nc.dram_tensor("c_iota32", [1, 32], F32, kind="ExternalInput")

    y_out = nc.dram_tensor("y", [NT, D], F32, kind="ExternalOutput")

    xn_spill = nc.dram_tensor("xn_spill", [D, NT], BF16)
    out_spill = nc.dram_tensor("out_spill", [D, NT], BF16)
    s_spill = nc.dram_tensor("s_spill", [D, NT], BF16)
    mid_spill = nc.dram_tensor("mid_spill", [IFF, NT], BF16)
    w1bf = nc.dram_tensor("w1bf", [NMH, NCH, 128, 128], BF16)
    wgT_sp = nc.dram_tensor("wgT_sp", [D, IFF], BF16)
    wuT_sp = nc.dram_tensor("wuT_sp", [D, IFF], BF16)
    wdT_sp = nc.dram_tensor("wdT_sp", [IFF, D], BF16)
    maxrt = nc.dram_tensor("maxrt", [1, 128], F32)
    maxrt2 = nc.dram_tensor("maxrt2", [1, 128], F32)
    cc1i, cc1o = nc.dram_tensor("cc1i", [1, 128], F32), nc.dram_tensor("cc1o", [1, 128], F32)
    cc2i, cc2o = nc.dram_tensor("cc2i", [1, 128], F32), nc.dram_tensor("cc2o", [1, 128], F32)

    taps = {}

    def tap(name, shape, dtype):
        if name in DBG and name not in taps:
            taps[name] = nc.dram_tensor("tap_" + name, shape, dtype,
                                        kind="ExternalOutput")
        return taps.get(name)

    stack = ExitStack()
    sems, cnt = {}, {}

    def sem(name):
        if name not in sems:
            sems[name] = stack.enter_context(nc.semaphore(name))
            cnt[name] = 0
        return sems[name]

    def inc(instr, name, k=1):
        instr.then_inc(sem(name), k)
        cnt[name] += k
        return cnt[name]

    def cget(name):
        sem(name)
        return cnt[name]

    dw_recent = {"vector": [], "scalar": [], "gpsimd": []}

    class DW:
        _PASS = {"wait_ge", "sem_inc", "nop"}
        _NOTRACK = {"dma_start"}
        WINDOW = 2

        def __init__(self, eng):
            self._e = eng
            tn = type(eng).__name__.lower()
            self._k = ("vector" if "vector" in tn else
                       "scalar" if "scalar" in tn else "gpsimd")

        def drain(self):
            dw_recent[self._k].clear()
            return self._e.drain()

        def __getattr__(self, n):
            f = getattr(self._e, n)
            if n in DW._PASS or n in DW._NOTRACK or not callable(f):
                return f
            rec = dw_recent[self._k]
            def g(*a, **k):
                names = []
                for o in list(a) + list(k.values()):
                    t = getattr(o, "tensor", None)
                    if t is not None:
                        names.append(getattr(t, "name", None) or id(t))
                outs, ins = set(names[:1]), set(names[1:])
                if any((ins & ro) or (outs & ro) or (outs & ri)
                       for ri, ro in rec):
                    self._e.drain()
                    rec.clear()
                r = f(*a, **k)
                rec.append((ins, outs))
                while len(rec) > DW.WINDOW:
                    rec.pop(0)
                return r
            return g

    def chain(steps, name="ch"):
        base = cget(name)
        with nc.Block() as blk:
            for en in ENGS:
                if not any(e == en for e, _ in steps):
                    continue

                def body(eng, en=en):
                    if en in ("vector", "scalar", "gpsimd"):
                        eng = DW(eng)
                    for i, (e2, fn) in enumerate(steps):
                        if e2 != en:
                            continue
                        if i > 0:
                            eng.wait_ge(sem(name), base + i)
                        last = fn(eng)
                        assert last is not None, f"chain step {i} returned None"
                        inc(last, name)
                getattr(blk, en)(body)

    def chain2(steps, name="c2"):
        if "serial2" in DBG or ("ser:" + name) in DBG:
            steps = [(e, fn, [i - 1] if i > 0 else [])
                     for i, (e, fn, deps) in enumerate(steps)]
        ordinals = []
        count = {}
        for e, fn, deps in steps:
            ordinals.append(count.get(e, 0))
            count[e] = count.get(e, 0) + 1
        bases = {e: cget(f"{name}_{e}") for e in count}
        with nc.Block() as blk:
            for en in list(count):
                def body(eng, en=en):
                    if en in ("vector", "scalar", "gpsimd"):
                        eng = DW(eng)
                    for i, (e, fn, deps) in enumerate(steps):
                        if e != en:
                            continue
                        need = {}
                        for d in deps:
                            de = steps[d][0]
                            if de == en:
                                continue
                            need[de] = max(need.get(de, -1), ordinals[d])
                        for de, o in need.items():
                            eng.wait_ge(sem(f"{name}_{de}"), bases[de] + o + 1)
                        last = fn(eng)
                        assert last is not None, f"chain2 step {i} returned None"
                        inc(last, f"{name}_{en}")
                getattr(blk, en)(body)

    bar_tgt = [0]
    ENGS = ["sync", "tensor", "vector", "scalar", "gpsimd"]

    def barrier():
        bar = sem("bar")
        bar_tgt[0] += 5
        tgt = bar_tgt[0]
        with nc.Block() as blk:
            for en in ENGS:
                def mk(en=en):
                    @getattr(blk, en)
                    def _(eng):
                        if en != "sync":
                            eng.drain()
                        eng.sem_inc(bar, 1)
                        eng.wait_ge(bar, tgt)
                mk()
        for rec in dw_recent.values():
            rec.clear()

    def sbuf(name, shape, dtype, st=None):
        return (st or stack).enter_context(nc.sbuf_tensor(name, shape, dtype))

    def psum(name, shape, dtype=F32, st=None):
        return (st or stack).enter_context(nc.psum_tensor(name, shape, dtype))

    idbf = sbuf("idbf", [128, 128], BF16)
    utb = sbuf("utb", [128, 128], BF16)
    onesb = sbuf("onesb", [128, 128], BF16)
    onesf = sbuf("onesf", [128, 128], F32)
    iota128 = sbuf("iota128", [128, 1], F32)
    iota32 = sbuf("iota32", [1, 32], F32)

    bocol = sbuf("bocol", [128, 8], F32)
    g2col = sbuf("g2col", [128, 8], F32)
    b2col = sbuf("b2col", [128, 8], F32)
    bm2col = sbuf("bm2col", [128, 8], F32)
    bm1col = sbuf("bm1col", [128, 32], F32)

    rmaxcol = sbuf("rmaxcol", [128, 1], F32)
    rmaxcol2 = sbuf("rmaxcol2", [128, 1], F32)
    rowm = sbuf("rowm", [1, 128], F32)
    ccrow = sbuf("ccrow", [1, 128], F32)
    cc2row = sbuf("cc2row", [1, 128], F32)
    sxv = sbuf("sxv", [128, 1], F32)
    rsxv = sbuf("rsxv", [128, 1], F32)
    sdv = sbuf("sdv", [128, 1], F32)
    rsdv = sbuf("rsdv", [128, 1], F32)
    scg = sbuf("scg", [128, NIC], F32)
    scu = sbuf("scu", [128, NIC], F32)
    scd = sbuf("scd", [128, NCH], F32)
    kcol = sbuf("kcol", [128, 4], F32)

    stXN1 = ExitStack()
    xn1_fm = sbuf("xn1_fm", [128, NCH * NT], BF16, stXN1)
    stTab = ExitStack()
    cosT2 = sbuf("cosT2", [128, NTAU * 64], F32, stTab)
    sinT = sbuf("sinT", [128, NTAU * 64], F32, stTab)
    clockS = sbuf("clockS", [128, NTAU * H], F32, stTab)
    maskadd = sbuf("maskadd", [128, 4 * 512], BF16, stTab)
    brepc = sbuf("brepc", [128, H], F32, stTab)
    growB = sbuf("growB", [1, D], F32, stTab)
    st_early = ExitStack()
    brow = sbuf("brow", [1, 4096], F32, st_early)
    g1rep = sbuf("g1rep", [128, D], F32, st_early)
    b1rep = sbuf("b1rep", [128, D], F32, st_early)
    stArena = ExitStack()
    arena1 = sbuf("arena1", [128, NCH * NT], BF16, stArena)

    psA = psum("psA", [128, 1024], BF16)
    psB = psum("psB", [128, 1024], BF16)
    psP = psum("psP", [128, 512], F32)
    psQ = psum("psQ", [128, 512], F32)
    psS1 = psum("psS1", [128, 512], F32)
    psS2 = psum("psS2", [128, 512], F32)
    psO = psum("psO", [128, 512], F32)
    psR = psum("psR", [128, 512], F32)

    with nc.Block() as blk:
        @blk.sync
        def _(sy):
            for dst, src in [(idbf, c_idbf), (utb, c_ut),
                             (onesb, c_onesb), (onesf, c_onesf),
                             (iota128, c_iota128), (iota32, c_iota32)]:
                inc(sy.dma_start(out=dst[:, :], in_=src[:, :]), "dA", 16)

        @blk.gpsimd
        def _(g):
            for kc in range(NCH):
                inc(g.dma_start(
                    out=AP(w1bf, kc * 128 * 128,
                           [[128, 128], [8 * 128 * 128, NMH], [1, 128]]),
                    in_=AP(w1, kc * 128 * MH,
                           [[MH, 128], [128, NMH], [1, 128]])), "dW1c", 16)

        @blk.vector
        def _(v):
            v = DW(v)
            v.wait_ge(sem("dA"), cget("dA"))
            for s_ in range(4):
                base = s_ * 512
                if s_ > 0:
                    v.memset(maskadd[:, base:base + 128 * s_], -4000.0)
                v.tensor_scalar(maskadd[:, base + 128 * s_:
                                        base + 128 * (s_ + 1)],
                                utb[:, :], -1.0, 4000.0, ALU.add, ALU.mult)
                if s_ < 3:
                    v.memset(maskadd[:, base + 128 * (s_ + 1):base + 512], 0.0)
            v.memset(rmaxcol[:, :], 0.0)
            v.memset(rmaxcol2[:, :], 0.0)
            v.memset(ccrow[:, :], 0.0)
            v.memset(cc2row[:, :], 0.0)
            v.memset(kcol[:, 0:1], EPS)
            v.memset(kcol[:, 1:2], EPS * EPS)
            v.memset(kcol[:, 2:3], 1.0)
            v.memset(kcol[:, 3:4], 0.5)

    barrier()

    st0 = ExitStack()
    thS = sbuf("thS", [128, NTAU * 32], F32, st0)
    thC = sbuf("thC", [128, NTAU * 32], F32, st0)
    snR = sbuf("snR", [128, NTAU * 32], F32, st0)
    rpF = sbuf("rpF", [128, 32], F32, st0)
    rpB = sbuf("rpB", [128, 32], F32, st0)
    rpC = sbuf("rpC", [128, 32], F32, st0)
    rpI = sbuf("rpI", [128, 32], I32, st0)
    tvcol = sbuf("tvcol", [128, 1], F32, st0)

    def range_reduce(v, src_ap, dst_ap, shift):
        v.tensor_scalar(rpB[:, :], src_ap, float(shift), 1.0 / TWO_PI,
                        ALU.add, ALU.mult)
        v.tensor_copy(rpI[:, :], rpB[:, :])
        v.tensor_scalar(rpC[:, :], src_ap, float(shift), None, ALU.add)
        v.scalar_tensor_tensor(rpB[:, :], rpI[:, :], -TWO_PI, rpC[:, :],
                               ALU.mult, ALU.add)
        v.tensor_scalar(rpC[:, :], rpB[:, :], math.pi, -TWO_PI,
                        ALU.is_ge, ALU.mult)
        v.tensor_tensor(rpB[:, :], rpB[:, :], rpC[:, :], ALU.add)
        v.tensor_scalar(rpC[:, :], rpB[:, :], -math.pi, TWO_PI,
                        ALU.is_le, ALU.mult)
        v.tensor_tensor(rpB[:, :], rpB[:, :], rpC[:, :], ALU.add)
        return v.tensor_scalar(dst_ap, rpB[:, :], PI_CLAMP, -PI_CLAMP,
                               ALU.min, ALU.max)

    with nc.Block() as blk:
        @blk.scalar
        def _(s):
            s = DW(s)
            inc(s.activation(brow[0:1, 0:32], iota32[0:1, :], ACT.Exp,
                             bias=0.0, scale=-math.log(10000.0) / 32.0), "sA")
            s.wait_ge(sem("vA_rope"), 1)
            s.activation(snR[:, :], thS[:, :], ACT.Sin)
            inc(s.activation(thC[:, :], thC[:, :], ACT.Sin), "sA")

        @blk.tensor
        def _(t):
            t.wait_ge(sem("sA"), cget("sA") - 1)
            inc(t.matmul(psR[:, 0:32], onesf[0:1, :], brow[0:1, 0:32],
                         start=True, stop=True), "tA")

        @blk.vector
        def _(v):
            v = DW(v)
            v.wait_ge(sem("tA"), cget("tA"))
            v.tensor_copy(rpF[:, :], psR[:, 0:32])
            for tau in range(NTAU):
                pos = (tau % 16) * 128
                v.tensor_scalar(tvcol[:, :], iota128[:, :], float(pos), None, ALU.add)
                v.tensor_scalar(thS[:, tau * 32:(tau + 1) * 32], rpF[:, :],
                                tvcol[:, 0:1], None, ALU.mult)
                range_reduce(v, thS[:, tau * 32:(tau + 1) * 32],
                             thC[:, tau * 32:(tau + 1) * 32], math.pi / 2)
                last = range_reduce(v, thS[:, tau * 32:(tau + 1) * 32],
                                    thS[:, tau * 32:(tau + 1) * 32], 0.0)
            inc(last, "vA_rope")
            v.wait_ge(sem("sA"), cget("sA"))
            v.tensor_scalar(AP(sinT, 0, [[NTAU * 64, 128], [2, NTAU * 32]]),
                            snR[:, :], -1.0, None, ALU.mult)
            v.tensor_copy(AP(sinT, 1, [[NTAU * 64, 128], [2, NTAU * 32]]),
                          snR[:, :])
            v.tensor_copy(AP(cosT2, 0, [[NTAU * 64, 128], [2, NTAU * 32]]),
                          thC[:, :])
            inc(v.tensor_copy(AP(cosT2, 1, [[NTAU * 64, 128], [2, NTAU * 32]]),
                              thC[:, :]), "vA")

    barrier()
    st0.close()

    jobs = [
        ("rep", g1v, D, g1rep), ("rep", b1v, D, b1rep), ("rep", b_clk, H, brepc),
        ("col", b_out, D, bocol), ("col", g2v, D, g2col), ("col", b2v, D, b2col),
        ("col", bm2, D, bm2col), ("col", bm1, MH, bm1col),
    ]
    with nc.Block() as blk:
        @blk.sync
        def _(sy):
            inc(sy.dma_start(out=growB[0:1, :], in_=b_gate.ap()), "dB", 16)
    nvb = []
    for kind, src_, n, dst in jobs:
        nvb.append(1 if kind == "rep" else (n // 128 + 7) // 8)

    with nc.Block() as blk:
        @blk.sync
        def _(sy):
            base_vb = cget("vB")
            for j, (kind, src_, n, dst) in enumerate(jobs):
                if j > 0:
                    sy.wait_ge(sem("vB"), base_vb + sum(nvb[:j]))
                inc(sy.dma_start(out=brow[0:1, 0:n], in_=src_.ap()), "dB", 16)

        @blk.tensor
        def _(t):
            base_db = cget("dB") - 16 * len(jobs)
            base_vb = cget("vB")
            base_tb = cget("tB")
            vb_done = 0
            for j, (kind, src_, n, dst) in enumerate(jobs):
                t.wait_ge(sem("dB"), base_db + 16 * (j + 1))
                if kind == "rep":
                    for o in range(0, n, 512):
                        w_ = min(512, n - o)
                        pst = psR if o == 0 else psQ
                        mm = t.matmul(pst[:, 0:w_], onesf[0:1, :],
                                      brow[0:1, o:o + w_], start=True, stop=True)
                    inc(mm, "tB")
                    vb_done += 1
                else:
                    ncol = n // 128
                    for gidx in range(0, ncol, 8):
                        if vb_done > 0 or gidx > 0:
                            t.wait_ge(sem("vB"), base_vb + vb_done)
                        for jj in range(gidx, min(gidx + 8, ncol)):
                            mm = t.matmul(psR[:, (jj % 8):(jj % 8) + 1],
                                          brow[0:1, 128 * jj:128 * (jj + 1)],
                                          onesf[0:1, 0:1], start=True, stop=True)
                        inc(mm, "tB")
                        vb_done += 1

        @blk.vector
        def _(v):
            v = DW(v)
            base_tb = cget("tB") - sum(nvb)
            ng = 0
            for j, (kind, src_, n, dst) in enumerate(jobs):
                if kind == "rep":
                    ng += 1
                    v.wait_ge(sem("tB"), base_tb + ng)
                    for o in range(0, n, 512):
                        w_ = min(512, n - o)
                        pst = psR if o == 0 else psQ
                        cp = v.tensor_copy(dst[:, o:o + w_], pst[:, 0:w_])
                    inc(cp, "vB")
                else:
                    ncol = n // 128
                    for gidx in range(0, ncol, 8):
                        ng += 1
                        v.wait_ge(sem("tB"), base_tb + ng)
                        for jj in range(gidx, min(gidx + 8, ncol)):
                            cp = v.tensor_copy(dst[:, jj:jj + 1],
                                               psR[:, (jj % 8):(jj % 8) + 1])
                        inc(cp, "vB")

    barrier()

    stA = ExitStack()
    xa = [sbuf("xa0", [128, D], F32, stA), sbuf("xa1", [128, D], F32, stA)]
    xcf = sbuf("xcf", [128, D], F32, stA)
    xsq = sbuf("xsq", [128, D], F32, stA)
    xnb = [sbuf("xnb0", [128, D], BF16, stA), sbuf("xnb1", [128, D], BF16, stA)]
    x1b = [sbuf("x1b0", [128, D], BF16, stA), sbuf("x1b1", [128, D], BF16, stA)]
    smA = sbuf("smA", [128, 1], F32, stA)
    smB = sbuf("smB", [128, 1], F32, stA)
    smC = sbuf("smC", [128, 1], F32, stA)
    smD = sbuf("smD", [128, 1], F32, stA)
    smE = sbuf("smE", [128, 1], F32, stA)
    smF = sbuf("smF", [128, 1], F32, stA)
    smG = sbuf("smG", [128, 1], F32, stA)
    smH = sbuf("smH", [128, 1], F32, stA)
    smI = sbuf("smI", [128, 1], F32, stA)
    statT = sbuf("statT", [128, 128], F32, stA)

    with nc.Block() as blk:
        @blk.sync
        def _(sy):
            for tau in range(NTAU):
                if tau >= 2:
                    sy.wait_ge(sem("vA2"), tau - 1)
                inc(sy.dma_start(out=xa[tau % 2][:, :],
                                 in_=x_in[128 * tau:128 * (tau + 1), :]),
                    f"dX{tau % 2}", 16)

        @blk.vector
        def _(v):
            v = DW(v)
            for tau in range(NTAU):
                a = xa[tau % 2]
                if tau > 0:
                    v.wait_ge(sem("tA2"), tau)
                    v.tensor_copy(
                        AP(arena1, 128 * (tau - 1), [[NCH * NT, 128], [NT, NCH], [1, 128]]),
                        psA[:, :])
                    inc(v.tensor_copy(
                        AP(xn1_fm, 128 * (tau - 1), [[NCH * NT, 128], [NT, NCH], [1, 128]]),
                        psB[:, :]), "vE")
                v.wait_ge(sem(f"dX{tau % 2}"), 16 * (tau // 2 + 1))
                v.reduce_sum(smA[:, :], a[:, :], AX.X)
                v.tensor_scalar(smB[:, :], smA[:, :], 1.0 / D, None, ALU.mult)
                v.tensor_scalar(xcf[:, :], a[:, :], smB[:, 0:1], None, ALU.subtract)
                v.tensor_tensor(xsq[:, :], xcf[:, :], xcf[:, :], ALU.mult)
                inc(v.reduce_sum(smC[:, :], xsq[:, :], AX.X), "vA1")
                v.wait_ge(sem("sA2"), tau + 1)
                v.reciprocal(smF[:, :], smD[:, :])
                v.reciprocal(smG[:, :], smE[:, :])
                v.tensor_scalar(xnb[tau % 2][:, :], xcf[:, :], smF[:, 0:1],
                                None, ALU.mult)
                v.reduce_max(smH[:, :], xcf[:, :], AX.X, apply_absolute_value=True)
                v.tensor_tensor(smH[:, :], smH[:, :], smF[:, :], ALU.mult)
                v.tensor_tensor(rmaxcol[:, :], rmaxcol[:, :], smH[:, :], ALU.max)
                v.scalar_tensor_tensor(xsq[:, :], xcf[:, :], smG[:, 0:1],
                                       g1rep[:, :], ALU.mult, ALU.mult)
                v.tensor_copy(statT[:, tau:tau + 1], smB[:, :])
                v.tensor_copy(statT[:, 32 + tau:33 + tau], smF[:, :])
                v.tensor_copy(statT[:, 64 + tau:65 + tau], smG[:, :])
                v.tensor_copy(statT[:, 96 + tau:97 + tau], smC[:, :])
                inc(v.tensor_tensor(x1b[tau % 2][:, :], xsq[:, :], b1rep[:, :],
                                    ALU.add), "vA2")
            v.wait_ge(sem("tA2"), NTAU)
            v.tensor_copy(
                AP(arena1, 128 * (NTAU - 1), [[NCH * NT, 128], [NT, NCH], [1, 128]]),
                psA[:, :])
            inc(v.tensor_copy(
                AP(xn1_fm, 128 * (NTAU - 1), [[NCH * NT, 128], [NT, NCH], [1, 128]]),
                psB[:, :]), "vE")

        @blk.scalar
        def _(s):
            s = DW(s)
            for tau in range(NTAU):
                s.wait_ge(sem("vA1"), tau + 1)
                s.activation(smD[:, :], smC[:, :], ACT.Sqrt,
                             bias=kcol[:, 0:1], scale=1.0 / D)
                s.activation(smE[:, :], smC[:, :], ACT.Sqrt,
                             bias=kcol[:, 1:2], scale=(1.0 + EPS) / D)
                inc(s.activation(smI[:, :], smD[:, :], ACT.Copy), "sA2")

        @blk.tensor
        def _(t):
            for tau in range(NTAU):
                t.wait_ge(sem("vA2"), tau + 1)
                if tau > 0:
                    t.wait_ge(sem("vE"), tau)
                for c in range(NCH):
                    t.transpose(psA[:, 128 * c:128 * (c + 1)],
                                xnb[tau % 2][:, 128 * c:128 * (c + 1)], idbf[:, :])
                    tr = t.transpose(psB[:, 128 * c:128 * (c + 1)],
                                     x1b[tau % 2][:, 128 * c:128 * (c + 1)], idbf[:, :])
                inc(tr, "tA2")

    barrier()

    with nc.Block() as blk:
        @blk.sync
        def _(sy):
            for c in range(NCH):
                inc(sy.dma_start(out=xn_spill[128 * c:128 * (c + 1), :],
                                 in_=arena1[:, NT * c:NT * (c + 1)]), "dO", 16)
            inc(sy.dma_start(out=maxrt[0:1, :], in_=rmaxcol[:, 0:1]), "dO", 16)
            sy.wait_ge(sem("dO"), cget("dO"))
            inc(sy.dma_start(out=rowm[0:1, :], in_=maxrt[0:1, :]), "dA", 16)

        @blk.vector
        def _(v):
            v = DW(v)
            v.wait_ge(sem("dA"), cget("dA"))
            inc(v.reduce_max(ccrow[0:1, 0:1], rowm[0:1, :], AX.X), "vA")

        @blk.gpsimd
        def _(g):
            g = DW(g)
            g.wait_ge(sem("vA"), cget("vA"))
            inc(g.dma_start(out=cc1i[0:1, :], in_=ccrow[0:1, :]), "dW", 16)
            g.wait_ge(sem("dW"), cget("dW"))
            if "nocc" in DBG:
                inc(g.dma_start(out=cc1o[0:1, :], in_=cc1i[0:1, :]), "cc", 16)
            else:
                inc(g.collective_compute(
                    "AllReduce", ALU.max, replica_groups=[list(range(8))],
                    ins=[cc1i.ap().opt()], outs=[cc1o.ap().opt()]), "cc", 1)

    barrier()

    tp0 = tap("stats", [128, 128], F32)
    tp1 = tap("xn_fm", [128, NCH * NT], BF16)
    tp2 = tap("xn1_fm", [128, NCH * NT], BF16)
    tp3 = tap("gmax", [1, 128], F32)
    if any(t is not None for t in (tp0, tp1, tp2, tp3)):
        with nc.Block() as blk:
            @blk.sync
            def _(sy):
                if tp0 is not None:
                    inc(sy.dma_start(out=tp0[:, :], in_=statT[:, :]), "dO", 16)
                if tp1 is not None:
                    inc(sy.dma_start(out=tp1[:, :], in_=arena1[:, :]), "dO", 16)
                if tp2 is not None:
                    inc(sy.dma_start(out=tp2[:, :], in_=xn1_fm[:, :]), "dO", 16)
                if tp3 is not None:
                    sy.wait_ge(sem("cc"), cget("cc"))
                    inc(sy.dma_start(out=rowm[0:1, :], in_=cc1o[0:1, :]), "dA", 16)
                    sy.wait_ge(sem("dA"), cget("dA"))
                    inc(sy.dma_start(out=tp3[0:1, :], in_=rowm[0:1, :]), "dO", 16)
                sy.wait_ge(sem("dO"), cget("dO"))
        barrier()


    stA.close()
    stArena.close()
    st_early.close()

    stB = ExitStack()
    Wall = sbuf("Wall", [128, 8 * 640], BF16, stB)
    wstg = [sbuf("wstg0", [128, 640], F32, stB),
            sbuf("wstg1", [128, 640], F32, stB)]
    bufQ = sbuf("bufQ", [128, NT], BF16, stB)
    bufK = sbuf("bufK", [128, NT], BF16, stB)
    bufG = sbuf("bufG", [128, NT], BF16, stB)
    bufW1 = sbuf("bufW1", [128, NT], BF16, stB)
    bufW2 = sbuf("bufW2", [128, NT], BF16, stB)
    gcpT = sbuf("gcpT", [128, NT], BF16, stB)
    pexpT = sbuf("pexpT", [128, NT], BF16, stB)
    pcsT = sbuf("pcsT", [128, NT], BF16, stB)
    v0c = sbuf("v0c", [128, NT], BF16, stB)
    v1c = sbuf("v1c", [128, NT], BF16, stB)
    eSb01 = [sbuf("eSb0", [128, 512], BF16, stB),
             sbuf("eSb1", [128, 512], BF16, stB)]
    den2 = sbuf("den2", [128, 512], F32, stB)
    spB = sbuf("spB", [128, 1024], F32, stB)
    nrmB = sbuf("nrmB", [128, 64], F32, stB)
    rqk = sbuf("rqk", [128, 64], F32, stB)
    negpm = sbuf("negpm", [128, 2], F32, stB)
    offF = sbuf("offF", [1, 128], F32, stB)
    offB = sbuf("offB", [1, 128], BF16, stB)
    totF = sbuf("totF", [1, 128], F32, stB)
    cgf = sbuf("cgf", [128, 256], F32, stB)
    brepg = sbuf("brepg", [128, 128], F32, stB)
    clkw = sbuf("clkw", [128, 128], BF16, stB)
    clkstgF = sbuf("clkstgF", [128, 128], F32, stB)

    with nc.Block() as blk:
        @blk.sync
        def _(sy):
            for kc in range(NCH):
                inc(sy.dma_start(out=clkstgF[:, 16 * kc:16 * (kc + 1)],
                                 in_=wclk[128 * kc:128 * (kc + 1), :]), "dCLK", 16)

        @blk.vector
        def _(v):
            v = DW(v)
            v.wait_ge(sem("dCLK"), cget("dCLK"))
            inc(v.tensor_copy(clkw[:, :], clkstgF[:, :]), "vCLK")

    with nc.Block() as blk:
        @blk.tensor
        def _(t):
            t.wait_ge(sem("vCLK"), cget("vCLK"))

    steps = []
    for tau in range(NTAU):
        def t_clk(t, tau=tau):
            for kc in range(NCH):
                mm = t.matmul(psP[:, 0:16],
                              xn1_fm[:, kc * NT + 128 * tau:kc * NT + 128 * (tau + 1)],
                              clkw[:, 16 * kc:16 * (kc + 1)],
                              start=(kc == 0), stop=(kc == NCH - 1))
            return mm
        def v_clk(v, tau=tau):
            return v.tensor_tensor(clockS[:, 16 * tau:16 * (tau + 1)],
                                   psP[:, 0:16], brepc[:, :], ALU.add)
        steps.append(("tensor", t_clk))
        steps.append(("vector", v_clk))
    steps += [
        ("scalar", lambda s: s.activation(spB[:, 0:512], clockS[:, :], ACT.Abs)),
        ("scalar", lambda s: s.activation(den2[:, :], spB[:, 0:512], ACT.Exp,
                                          bias=0.0, scale=-1.0)),
        ("scalar", lambda s: s.activation(spB[:, 0:512], den2[:, :], ACT.Ln,
                                          bias=1.0, scale=1.0)),
        ("vector", lambda v: v.tensor_scalar(den2[:, :], clockS[:, :], 0.0, None,
                                             ALU.max)),
        ("vector", lambda v: v.tensor_tensor(clockS[:, :], den2[:, :],
                                             spB[:, 0:512], ALU.add)),
    ]
    chain(steps)
    barrier()

    def clock_ap(c):
        return AP(clockS, 2 * c, [[NTAU * H, 128], [H, NTAU], [1, 2], [0, 64]])

    COS_AP = AP(cosT2, 0, [[NTAU * 64, 128], [64, NTAU], [0, 2], [1, 64]])
    SIN_AP = AP(sinT, 0, [[NTAU * 64, 128], [64, NTAU], [0, 2], [1, 64]])
    RQK_AP = AP(rqk, 0, [[64, 128], [2, NTAU], [1, 2], [0, 64]])
    NRM_OUT = AP(nrmB, 0, [[64, 128], [2, NTAU], [1, 2]])

    def ev_even(buf):
        return AP(buf, 0, [[NT, 128], [128, NTAU], [64, 2], [2, 32]])

    def ev_odd(buf):
        return AP(buf, 1, [[NT, 128], [128, NTAU], [64, 2], [2, 32]])

    for c in range(NCH):
        wl = "wlB"
        with nc.Block() as blk:
            @blk.gpsimd
            def _(g):
                if c > 0:
                    g.wait_ge(sem("pp_tensor"), cget("pp_tensor"))
                if "oldwl" in DBG:
                    return
                for widx, w in enumerate([wq, wk, wg, wv, wp]):
                    inc(g.dma_start(
                        out=AP(Wall, 128 * widx,
                               [[8 * 640, 128], [640, 8], [1, 128]]),
                        in_=AP(w, 128 * c,
                               [[1024, 128], [128 * 1024, 8], [1, 128]])),
                        wl + "d", 16)

        if "oldwl" in DBG:
            with nc.Block() as blk:
                @blk.sync
                def _(sy):
                    for kc in range(NCH):
                        if kc >= 2:
                            sy.wait_ge(sem(wl + "v"), cget(wl + "v") - 1)
                        for widx, w in enumerate([wq, wk, wg, wv, wp]):
                            inc(sy.dma_start(
                                out=wstg[kc % 2][:, 128 * widx:128 * (widx + 1)],
                                in_=w[128 * kc:128 * (kc + 1),
                                      128 * c:128 * (c + 1)]),
                                wl + f"d{kc % 2}", 16)

                @blk.vector
                def _(v):
                    v = DW(v)
                    for kc in range(NCH):
                        v.wait_ge(sem(wl + f"d{kc % 2}"),
                                  cget(wl + f"d{kc % 2}") - 80 * ((NCH - 1 - kc) // 2))
                        inc(v.tensor_copy(Wall[:, 640 * kc:640 * (kc + 1)],
                                          wstg[kc % 2][:, :]), wl + "v")

        with nc.Block() as blk:
            @blk.tensor
            def _(t):
                if "oldwl" in DBG:
                    t.wait_ge(sem(wl + "v"), cget(wl + "v"))
                else:
                    t.wait_ge(sem(wl + "d"), cget(wl + "d"))
                inc(t.matmul(psR[:, 0:128], onesf[0:1, :],
                             growB[0:1, 128 * c:128 * (c + 1)],
                             start=True, stop=True), "tBG")

            @blk.vector
            def _(v):
                v = DW(v)
                v.wait_ge(sem("tBG"), cget("tBG"))
                v.tensor_copy(brepg[:, :], psR[:, 0:128])
                v.memset(v0c[:, :], 1.0)
                inc(v.memset(v1c[:, :], 1.0), "vBG")

        with nc.Block() as blk:
            @blk.tensor
            def _(t):
                t.wait_ge(sem("vBG"), cget("vBG"))

        steps1 = []
        for tau in range(NTAU):
            ps = psP if tau % 2 == 0 else psQ
            def t_proj(t, tau=tau, ps=ps):
                for kc in range(NCH):
                    mm = t.matmul(ps[:, 0:512],
                                  xn1_fm[:, kc * NT + 128 * tau:kc * NT + 128 * (tau + 1)],
                                  Wall[:, 640 * kc:640 * kc + 512],
                                  start=(kc == 0), stop=(kc == NCH - 1))
                return mm
            def v_ev(v, tau=tau, ps=ps):
                o = 128 * tau
                v.tensor_copy(bufQ[:, o:o + 128], ps[:, 0:128])
                v.tensor_copy(bufK[:, o:o + 128], ps[:, 128:256])
                v.tensor_tensor(bufG[:, o:o + 128], ps[:, 256:384], brepg[:, :],
                                ALU.add)
                v.tensor_copy(v0c[:, o:o + 64], ps[:, 384:448])
                return v.tensor_copy(v1c[:, o + 64:o + 128], ps[:, 448:512])
            ti = len(steps1)
            steps1.append(("tensor", t_proj,
                           [ti - 3] if tau >= 2 else []))
            steps1.append(("vector", v_ev, [ti]))
        chain2(steps1, name="pj")
        barrier()

        chain([
            ("scalar", lambda s: s.activation(bufW1[:, :], bufG[:, :], ACT.Abs)),
            ("scalar", lambda s: s.activation(bufW2[:, :], bufW1[:, :], ACT.Exp,
                                              bias=0.0, scale=-1.0)),
            ("scalar", lambda s: s.activation(bufW1[:, :], bufW2[:, :], ACT.Ln,
                                              bias=1.0, scale=1.0)),
            ("vector", lambda v: v.tensor_scalar(bufW2[:, :], bufG[:, :], 0.0,
                                                 None, ALU.max)),
            ("vector", lambda v: v.scalar_tensor_tensor(bufG[:, :], bufW2[:, :],
                                                        -1.0, bufW1[:, :],
                                                        ALU.mult, ALU.subtract)),
        ])

        def cumsum_chain(srcT, dstT, clip_exp, tag):
            steps = []
            v_idx = {}
            s_idx = {}
            it = 0
            for b in range(BPC):
                for tt in range(16):
                    tau = 16 * b + tt
                    ps = psP if tt % 2 == 0 else psQ
                    cg = cgf[:, 128 * (tt % 2):128 * (tt % 2 + 1)]
                    def t_cs(t, b=b, tt=tt, ps=ps):
                        for tp in range(tt + 1):
                            o = 128 * (16 * b + tp)
                            mm = t.matmul(ps[:, 0:128],
                                          utb[:, :] if tp == tt else onesb[:, :],
                                          srcT[:, o:o + 128],
                                          start=(tp == 0), stop=(tp == tt))
                        return mm
                    ti = len(steps)
                    steps.append(("tensor", t_cs,
                                  [v_idx[it - 2]] if it >= 2 else []))
                    if clip_exp:
                        def v_cs(v, ps=ps, cg=cg):
                            return v.tensor_scalar(cg, ps[:, 0:128], 50.0, -60.0,
                                                   ALU.min, ALU.max)
                        def s_cs(s, tau=tau, cg=cg):
                            o = 128 * tau
                            return s.activation(dstT[:, o:o + 128], cg, ACT.Exp)
                        vi = len(steps)
                        steps.append(("vector", v_cs,
                                      [ti] + ([s_idx[it - 2]] if it >= 2 else [])))
                        s_idx[it] = len(steps)
                        steps.append(("scalar", s_cs, [vi]))
                        v_idx[it] = vi
                    else:
                        def v_cs(v, tau=tau, ps=ps):
                            o = 128 * tau
                            return v.tensor_copy(dstT[:, o:o + 128], ps[:, 0:128])
                        v_idx[it] = len(steps)
                        steps.append(("vector", v_cs, [ti]))
                    it += 1
            chain2(steps, name=tag)
            barrier()

        barrier()
        cumsum_chain(bufG, gcpT, True, "cg")

        steps = []
        vproj = {}
        for m in range(8):
            ps = psP if m % 2 == 0 else psQ
            def t_p(t, m=m, ps=ps):
                for kc in range(NCH):
                    mm = t.matmul(ps[:, 0:512],
                                  Wall[:, 640 * kc + 512:640 * (kc + 1)],
                                  xn1_fm[:, kc * NT + 512 * m:kc * NT + 512 * (m + 1)],
                                  start=(kc == 0), stop=(kc == NCH - 1))
                return mm
            def v_p(v, m=m, ps=ps):
                return v.tensor_copy(bufW1[:, 512 * m:512 * (m + 1)], ps[:, 0:512])
            ti = len(steps)
            steps.append(("tensor", t_p, [vproj[m - 2]] if m >= 2 else []))
            steps.append(("vector", v_p, [ti]))
            vproj[m] = ti + 1
        spe = {}
        for b in range(BPC):
            def v_pm(v, b=b):
                return v.reduce_max(negpm[:, b:b + 1], bufW1[:, T * b:T * (b + 1)],
                                    AX.X, negate=True)
            def s_pe(s, b=b):
                return s.activation(bufW2[:, T * b:T * (b + 1)],
                                    bufW1[:, T * b:T * (b + 1)], ACT.Exp,
                                    bias=negpm[:, b:b + 1], scale=1.0)
            vi = len(steps)
            steps.append(("vector", v_pm, [vproj[4 * b + 3]]))
            steps.append(("scalar", s_pe, [vi]))
            spe[b] = vi + 1
        vtr = {}
        for g in range(8):
            bank = psA if g % 2 == 0 else psB
            def t_tr(t, g=g, bank=bank):
                for u in range(4):
                    tau = 4 * g + u
                    tr = t.transpose(bank[:, 128 * u:128 * (u + 1)],
                                     bufW2[:, 128 * tau:128 * (tau + 1)], idbf[:, :])
                return tr
            def v_tr(v, g=g, bank=bank):
                return v.tensor_copy(bufW1[:, 512 * g:512 * (g + 1)],
                                     bank[:, 0:512])
            ti = len(steps)
            deps = [spe[g // 4]]
            if g >= 2:
                deps.append(vtr[g - 2])
            steps.append(("tensor", t_tr, deps))
            steps.append(("vector", v_tr, [ti]))
            vtr[g] = ti + 1
        steps.append(("vector", lambda v: v.tensor_tensor(
            pexpT[:, :], bufW1[:, :], clock_ap(c), ALU.mult)), )
        steps[-1] = ("vector", steps[-1][0] if False else steps[-1][1], [vtr[7]])
        chain2(steps, name="pp")
        barrier()

        cumsum_chain(pexpT, pcsT, False, "cp")

        def qk_prep(bufX, wnum, wden):
            chain([
                ("vector", lambda v: v.tensor_tensor(bufW1[:, :], bufX[:, :],
                                                     bufX[:, :], ALU.mult)),
                ("vector", lambda v: v.reduce_sum(
                    NRM_OUT,
                    AP(bufW1, 0, [[NT, 128], [128, NTAU], [64, 2], [1, 64]]),
                    AX.X)),
                ("vector", lambda v: v.tensor_scalar(nrmB[:, :], nrmB[:, :],
                                                     1e-24, None, ALU.max)),
                ("scalar", lambda s: s.activation(den2[:, 0:64], nrmB[:, :],
                                                  ACT.Ln)),
                ("scalar", lambda s: s.activation(rqk[:, :], den2[:, 0:64],
                                                  ACT.Exp, bias=0.0, scale=-0.5)),
                ("vector", lambda v: v.tensor_tensor(bufW2[:, :], bufX[:, :],
                                                     RQK_AP, ALU.mult)),
                ("vector", lambda v: v.tensor_copy(ev_even(bufW1), ev_odd(bufW2))),
                ("vector", lambda v: v.tensor_copy(ev_odd(bufW1), ev_even(bufW2))),
                ("vector", lambda v: v.tensor_tensor(bufX[:, :], bufW1[:, :],
                                                     SIN_AP, ALU.mult)),
                ("vector", lambda v: v.tensor_tensor(bufW1[:, :], bufW2[:, :],
                                                     COS_AP, ALU.mult)),
                ("vector", lambda v: v.tensor_tensor(bufW2[:, :], bufX[:, :],
                                                     bufW1[:, :], ALU.add)),
                ("vector", lambda v: v.tensor_scalar(bufW1[:, :], wden[:, :],
                                                     1e-8, None, ALU.add)),
            ] + [
                it for m in range(4) for it in [
                    ("scalar", (lambda m: (lambda s: (
                        s.activation(spB[:, :], bufW1[:, 1024 * m:1024 * (m + 1)],
                                     ACT.Ln),
                        s.activation(den2[:, :],
                                     spB[:, 0:512], ACT.Exp, bias=0.0,
                                     scale=-1.0))[-1]))(m)),
                    ("scalar", (lambda m: (lambda s: s.activation(
                        spB[:, 0:512], spB[:, 512:1024], ACT.Exp, bias=0.0,
                        scale=-1.0))(m))),
                    ("vector", (lambda m: (lambda v: (
                        v.tensor_tensor(bufW1[:, 1024 * m:1024 * m + 512],
                                        wnum[:, 1024 * m:1024 * m + 512],
                                        den2[:, :], ALU.mult),
                        v.tensor_tensor(bufW1[:, 1024 * m + 512:1024 * (m + 1)],
                                        wnum[:, 1024 * m + 512:1024 * (m + 1)],
                                        spB[:, 0:512], ALU.mult))[-1]))(m)),
                ]
            ] + [
                ("vector", lambda v: v.tensor_tensor(bufX[:, :], bufW2[:, :],
                                                     bufW1[:, :], ALU.mult)),
            ])

        qk_prep(bufQ, gcpT, pcsT)
        barrier()
        qk_prep(bufK, pexpT, gcpT)
        barrier()

        steps = []
        vtr2 = {}
        for part in range(2):
            srcb = bufQ if part == 0 else bufK
            dstb = bufW1 if part == 0 else bufW2
            for g in range(8):
                gg = 8 * part + g
                bank = psA if gg % 2 == 0 else psB
                def t_t2(t, g=g, srcb=srcb, bank=bank):
                    for u in range(4):
                        tau = 4 * g + u
                        tr = t.transpose(bank[:, 128 * u:128 * (u + 1)],
                                         srcb[:, 128 * tau:128 * (tau + 1)],
                                         idbf[:, :])
                    return tr
                def v_t2(v, g=g, dstb=dstb, bank=bank):
                    return v.tensor_copy(dstb[:, 512 * g:512 * (g + 1)],
                                         bank[:, 0:512])
                ti = len(steps)
                steps.append(("tensor", t_t2, [vtr2[gg - 2]] if gg >= 2 else []))
                steps.append(("vector", v_t2, [ti]))
                vtr2[gg] = ti + 1
        chain2(steps, name="t2")
        barrier()

        qfm, kfm, outfm = bufW1, bufW2, bufG
        psS = [psS1, psS2]
        steps = []

        def add(eng, fn, deps):
            steps.append((eng, fn, [d for d in deps if d is not None]))
            return len(steps) - 1

        last_psS_rd = [None, None]
        last_pv_eS = [None, None]
        last_spB_rd = [None, None]
        last_vout = [None]
        last_vout_bank = [None, None]
        pending_den = [None]
        psOx = [psO, psP]
        mcount = 0

        for b in range(BPC):
            for hl in range(2):
                vtm = v0c if hl == 0 else v1c
                for m in range(4):
                    ob = mcount % 2
                    psOb = psOx[ob]
                    mcount += 1
                    qc0 = T * b + 512 * m
                    nj = 4 * m + 4
                    iex = {}

                    def mk_pv(j, b=b, nj=nj, vtm=vtm, psOb=psOb, ob=ob):
                        eS = eSb01[j % 2]
                        def t_pv(t, j=j, eS=eS):
                            return t.matmul(
                                psOb[:, 0:512],
                                vtm[:, 128 * (16 * b + j):128 * (16 * b + j + 1)],
                                eS[:, :],
                                start=(j == 0), stop=(j == nj - 1))
                        deps = [iex[j]]
                        if j == 0:
                            deps.append(last_vout_bank[ob])
                        ipv = add("tensor", t_pv, deps)
                        last_pv_eS[j % 2] = ipv
                        return ipv

                    for j in range(nj):
                        p = j % 2
                        hp = slice(512 * p, 512 * (p + 1))
                        eS = eSb01[p]

                        def t_qk(t, b=b, hl=hl, qc0=qc0, j=j, p=p):
                            return t.matmul(
                                psS[p][:, 0:512],
                                kfm[64 * hl:64 * hl + 64,
                                    T * b + 128 * j:T * b + 128 * (j + 1)],
                                qfm[64 * hl:64 * hl + 64, qc0:qc0 + 512],
                                start=True, stop=True)
                        iqk = add("tensor", t_qk, [last_psS_rd[p]])

                        if j >= 4 * m:
                            def v_cm(v, p=p, hp=hp, mv=j - 4 * m):
                                return v.scalar_tensor_tensor(
                                    spB[:, hp], psS[p][:, 0:512], 500.0,
                                    maskadd[:, 512 * mv:512 * (mv + 1)],
                                    ALU.min, ALU.add)
                            icm = add("vector", v_cm, [iqk, last_spB_rd[p]])
                            last_psS_rd[p] = icm

                            def s_ex(s, eS=eS, hp=hp):
                                return s.activation(eS[:, :], spB[:, hp],
                                                    ACT.Exp, bias=0.0,
                                                    scale=0.125)
                            iex[j] = add("scalar", s_ex,
                                         [icm, last_pv_eS[p]])
                            last_spB_rd[p] = iex[j]
                        else:
                            def s_ex(s, eS=eS, p=p):
                                return s.activation(eS[:, :], psS[p][:, 0:512],
                                                    ACT.Exp, bias=0.0,
                                                    scale=0.125)
                            iex[j] = add("scalar", s_ex,
                                         [iqk, last_pv_eS[p]])
                            last_psS_rd[p] = iex[j]

                        if j == 1 and pending_den[0] is not None:
                            pending_den[0]()
                            pending_den[0] = None
                        if j >= 1:
                            ipv = mk_pv(j - 1)
                    ipv_last = mk_pv(nj - 1)

                    def den_emit(hl=hl, qc0=qc0, ipv_last=ipv_last,
                                 psOb=psOb, ob=ob):
                        global_last = last_vout[0]
                        if hl == 0:
                            def s_rc(s):
                                s.activation(den2[64:65, :], psOb[64:65, 0:512],
                                             ACT.Ln)
                                return s.activation(den2[64:65, :],
                                                    den2[64:65, :],
                                                    ACT.Exp, bias=0.0,
                                                    scale=-1.0)
                            def t_bc(t):
                                return t.matmul(psR[0:64, :],
                                                onesf[64:65, 0:64],
                                                den2[64:65, :], start=True,
                                                stop=True,
                                                tile_position=(64, 0))
                            def v_out(v, qc0=qc0, psOb=psOb):
                                v.tensor_copy(den2[0:64, 0:512], psR[0:64, :])
                                return v.tensor_tensor(
                                    outfm[0:64, qc0:qc0 + 512],
                                    psOb[0:64, 0:512],
                                    den2[0:64, 0:512], ALU.mult)
                        else:
                            def s_rc(s):
                                s.activation(den2[0:1, :], psOb[0:1, 0:512],
                                             ACT.Ln)
                                return s.activation(den2[0:1, :], den2[0:1, :],
                                                    ACT.Exp, bias=0.0,
                                                    scale=-1.0)
                            def t_bc(t):
                                return t.matmul(psR[64:128, :],
                                                onesf[0:1, 0:64],
                                                den2[0:1, :], start=True,
                                                stop=True,
                                                tile_position=(0, 64))
                            def v_out(v, qc0=qc0, psOb=psOb):
                                v.tensor_copy(den2[64:128, 0:512],
                                              psR[64:128, :])
                                return v.tensor_tensor(
                                    outfm[64:128, qc0:qc0 + 512],
                                    psOb[64:128, 0:512],
                                    den2[64:128, 0:512], ALU.mult)
                        irc = add("scalar", s_rc, [ipv_last])
                        itb = add("tensor", t_bc, [irc, global_last])
                        ivo = add("vector", v_out, [itb])
                        last_vout[0] = ivo
                        last_vout_bank[ob] = ivo
                        return ivo

                    pending_den[0] = den_emit
        if pending_den[0] is not None:
            pending_den[0]()
            pending_den[0] = None
        chain2(steps, name="at")

        with nc.Block() as blk:
            @blk.sync
            def _(sy):
                sy.wait_ge(sem("at_vector"), cget("at_vector"))
                inc(sy.dma_start(out=out_spill[128 * c:128 * (c + 1), :],
                                 in_=outfm[:, :]), "dO", 16)
                sy.wait_ge(sem("dO"), cget("dO"))

        if c == 0:
            tb = {nm: tap(nm, [128, NT], BF16)
                  for nm in ("qfm0", "kfm0", "gcp0", "pexp0", "pcs0", "outfm0",
                             "v0c0")}
            srcs = {"qfm0": bufW1, "kfm0": bufW2, "gcp0": gcpT, "pexp0": pexpT,
                    "pcs0": pcsT, "outfm0": bufG, "v0c0": v0c}
            if any(t_ is not None for t_ in tb.values()):
                with nc.Block() as blk:
                    @blk.sync
                    def _(sy):
                        sy.wait_ge(sem("at_vector"), cget("at_vector"))
                        for nm, t_ in tb.items():
                            if t_ is not None:
                                inc(sy.dma_start(out=t_[:, :], in_=srcs[nm][:, :]),
                                    "dO", 16)
                        sy.wait_ge(sem("dO"), cget("dO"))
        barrier()

    stB.close()



    if "stopB" in DBG:
        stZ2 = ExitStack()
        zb2 = sbuf("zb2", [128, D], F32, stZ2)
        with nc.Block() as blk:
            @blk.vector
            def _(v):
                inc(v.memset(zb2[:, :], 0.0), "vZ2")

            @blk.sync
            def _(sy):
                sy.wait_ge(sem("vZ2"), cget("vZ2"))
                for tau in range(NTAU):
                    inc(sy.dma_start(out=y_out[128 * tau:128 * (tau + 1), :],
                                     in_=zb2[:, :]), "dO", 16)
                sy.wait_ge(sem("dO"), cget("dO"))
        stZ2.close()
        stB.close()
        stTab.close()
        stXN1.close()
        stack.close()
        return nc, taps

    stP = ExitStack()
    stgF = [sbuf("stgF0", [128, MH], F32, stP), sbuf("stgF1", [128, IFF], F32, stP)]
    sA_ = sbuf("sA_", [128, IFF], F32, stP)
    sB_ = sbuf("sB_", [128, IFF], F32, stP)
    sI_ = sbuf("sI_", [128, IFF], I32, stP)
    sG_ = sbuf("sG_", [128, IFF], BF16, stP)
    wI_ = sbuf("wI_", [128, IFF], BF16, stP)
    wT_ = sbuf("wT_", [128, IFF], BF16, stP)
    rmx = sbuf("rmx", [128, 1], F32, stP)

    barrier()

    wT2_ = sbuf("wT2_", [128, IFF], BF16, stP)
    wTd = [wT_, wT2_]

    qw_specs = ([(wgate, rc, D, scg, wgT_sp) for rc in range(NIC)] +
                [(wup, rc, D, scu, wuT_sp) for rc in range(NIC)] +
                [(wdown, rc, IFF, scd, wdT_sp) for rc in range(NCH)])
    steps = []

    def addq(eng, fn, deps):
        steps.append((eng, fn, [d for d in deps if d is not None]))
        return len(steps) - 1

    i_in, i_sc, i_ab, i_rd, i_out = {}, {}, {}, {}, {}
    i_ttr, i_vtr = {}, {}
    last_vtr_bank = [None, None]
    gctr = 0
    for it, (wsrc, rc, ncol, sccol, dst_sp) in enumerate(qw_specs):
        stg = stgF[it % 2]
        wT = wTd[it % 2]
        ncc = ncol // 128

        def g_in(g, wsrc=wsrc, rc=rc, ncol=ncol, stg=stg):
            inc(g.dma_start(out=stg[:, 0:ncol],
                            in_=wsrc[128 * rc:128 * (rc + 1), :]), "dQW", 16)
            g.wait_ge(sem("dQW"), cget("dQW"))
            return g.nop()
        i_in[it] = addq("gpsimd", g_in, [i_sc.get(it - 2)])

        def v_scale(v, rc=rc, ncol=ncol, sccol=sccol, stg=stg):
            v.reduce_max(rmx[:, :], stg[:, 0:ncol], AX.X,
                         apply_absolute_value=True)
            v.tensor_scalar(sccol[:, rc:rc + 1], rmx[:, :], 1.0 / 127.0,
                            None, ALU.mult)
            v.reciprocal(rmx[:, :], rmx[:, :])
            v.tensor_scalar(rmx[:, :], rmx[:, :], 127.0, None, ALU.mult)
            return v.tensor_scalar(sA_[:, 0:ncol], stg[:, 0:ncol],
                                   rmx[:, 0:1], 128.5, ALU.mult, ALU.add)
        i_sc[it] = addq("vector", v_scale, [i_in[it]])

        def v_round(v, ncol=ncol):
            v.tensor_copy(sI_[:, 0:ncol], sA_[:, 0:ncol])
            v.scalar_tensor_tensor(sB_[:, 0:ncol], sI_[:, 0:ncol], 1.0,
                                   sA_[:, 0:ncol], ALU.mult, ALU.is_le)
            v.tensor_tensor(sA_[:, 0:ncol], sI_[:, 0:ncol], sB_[:, 0:ncol],
                            ALU.add)
            return v.tensor_scalar(wI_[:, 0:ncol], sA_[:, 0:ncol], -129.0,
                                   None, ALU.add)
        i_rd[it] = addq("vector", v_round,
                        [i_sc[it], i_ttr.get((it - 1, -1))])

        ngrp = (ncc + 3) // 4
        for gi, g0 in enumerate(range(0, ncc, 4)):
            gw = min(4, ncc - g0)
            bank = psA if gctr % 2 == 0 else psB

            def t_tr(t, g0=g0, gw=gw, bank=bank):
                for u in range(gw):
                    tr = t.transpose(bank[:, 128 * u:128 * (u + 1)],
                                     wI_[:, 128 * (g0 + u):128 * (g0 + u + 1)],
                                     idbf[:, :])
                return tr
            i_ttr[(it, gi)] = addq("tensor", t_tr,
                                   [i_rd[it], last_vtr_bank[gctr % 2]])

            def v_tr(s, g0=g0, gw=gw, bank=bank, wT=wT):
                return s.activation(wT[:, 128 * g0:128 * (g0 + gw)],
                                    bank[:, 0:128 * gw], ACT.Copy)
            deps = [i_ttr[(it, gi)]]
            if gi == 0:
                deps.append(i_out.get(it - 2))
            i_vtr[(it, gi)] = addq("scalar", v_tr, deps)
            last_vtr_bank[gctr % 2] = i_vtr[(it, gi)]
            gctr += 1
        i_ttr[(it, -1)] = i_ttr[(it, ngrp - 1)]

        def sy_out(sy, rc=rc, ncc=ncc, dst_sp=dst_sp, wT=wT):
            for cc in range(ncc):
                inc(sy.dma_start(
                    out=dst_sp[128 * cc:128 * (cc + 1),
                               128 * rc:128 * (rc + 1)],
                    in_=wT[:, 128 * cc:128 * (cc + 1)]), "dQWo", 16)
            sy.wait_ge(sem("dQWo"), cget("dQWo"))
            return sy.nop()
        i_out[it] = addq("sync", sy_out, [i_vtr[(it, ngrp - 1)]])
    chain2(steps, name="qw")
    barrier()
    stP.close()
    stTab.close()
    stXN1.close()

    stC1 = ExitStack()
    x1fm = sbuf("x1fm", [128, NCH * NT], BF16, stC1)
    stC2 = ExitStack()
    outR = sbuf("outR", [128, NCH * NT], BF16, stC2)
    woS = sbuf("woS", [128, NCH * 1024], BF16, stC2)
    wstgC = sbuf("wstgC", [128, 1024], F32, stC2)
    xnstg = sbuf("xnstg", [128, NT], BF16, stC2)
    spC = sbuf("spC", [128, 512], F32, stC2)

    with nc.Block() as blk:
        @blk.sync
        def _(sy):
            for cc in range(NCH):
                inc(sy.dma_start(out=outR[:, NT * cc:NT * (cc + 1)],
                                 in_=out_spill[128 * cc:128 * (cc + 1), :]),
                    "dC1", 16)

        @blk.gpsimd
        def _(g):
            for kc in range(NCH):
                inc(g.dma_start(out=woS[:, 1024 * kc:1024 * (kc + 1)],
                                in_=wo[128 * kc:128 * (kc + 1), :]), "dC2", 16)

    with nc.Block() as blk:
        @blk.sync
        def _(sy):
            sy.wait_ge(sem("dC1"), cget("dC1"))
            for oc in range(NCH):
                if oc >= 1:
                    sy.wait_ge(sem("vC2"), oc)
                inc(sy.dma_start(out=xnstg[:, :],
                                 in_=xn_spill[128 * oc:128 * (oc + 1), :]),
                    "dC3", 16)

        @blk.tensor
        def _(t):
            t.wait_ge(sem("dC2"), cget("dC2"))
            t.wait_ge(sem("dC1"), cget("dC1"))
            for oc in range(NCH):
                for m in range(8):
                    idx = 8 * oc + m
                    if idx >= 2:
                        t.wait_ge(sem("vC2x"), idx - 1)
                    ps = psP if m % 2 == 0 else psQ
                    for kc in range(NCH):
                        mm = t.matmul(
                            ps[:, 0:512],
                            woS[:, 1024 * kc + 128 * oc:1024 * kc + 128 * (oc + 1)],
                            outR[:, NT * kc + 512 * m:NT * kc + 512 * (m + 1)],
                            start=(kc == 0), stop=(kc == NCH - 1))
                    inc(mm, "tC1")

        @blk.vector
        def _(v):
            v = DW(v)
            for oc in range(NCH):
                v.wait_ge(sem("dC3"), 16 * (oc + 1))
                for m in range(8):
                    idx = 8 * oc + m
                    v.wait_ge(sem("tC1"), idx + 1)
                    ps = psP if m % 2 == 0 else psQ
                    v.tensor_scalar(spC[:, 0:512], ps[:, 0:512],
                                    bocol[:, oc:oc + 1], None, ALU.add)
                    last = v.tensor_tensor(
                        x1fm[:, NT * oc + 512 * m:NT * oc + 512 * (m + 1)],
                        spC[:, 0:512], xnstg[:, 512 * m:512 * (m + 1)], ALU.add)
                    inc(last, "vC2x")
                inc(v.memset(spC[0:1, 0:1], 0.0), "vC2")
    barrier()
    stC2.close()

    stC3 = ExitStack()
    w2S = sbuf("w2S", [128, NMH * 1024], BF16, stC3)
    w1blk = sbuf("w1blk", [128, 2048], BF16, stC3)
    hact = sbuf("hact", [128, NMH * 512], BF16, stC3)
    h0m = sbuf("h0m", [128, NCH * 512], BF16, stC3)
    sqm = sbuf("sqm", [128, NCH * 512], BF16, stC3)
    sm1 = sbuf("sm1", [128, 512], F32, stC3)
    sm2 = sbuf("sm2", [128, 512], F32, stC3)
    sm3 = sbuf("sm3", [128, 512], F32, stC3)
    wstgC2 = sbuf("wstgC2", [128, 1024], F32, stC3)
    sOut = sbuf("sOut", [128, 1024], BF16, stC3)

    with nc.Block() as blk:
        @blk.sync
        def _(sy):
            for hc in range(NMH):
                if hc >= 1:
                    sy.wait_ge(sem("vC3"), hc)
                inc(sy.dma_start(out=wstgC2[:, :],
                                 in_=w2[128 * hc:128 * (hc + 1), :]), "dC4", 16)

        @blk.vector
        def _(v):
            v = DW(v)
            for hc in range(NMH):
                v.wait_ge(sem("dC4"), 16 * (hc + 1))
                inc(v.tensor_copy(w2S[:, 1024 * hc:1024 * (hc + 1)],
                                  wstgC2[:, :]), "vC3")

    for m in range(8):
        steps = []

        def add(eng, fn, deps):
            steps.append((eng, fn, [d for d in deps if d is not None]))
            return len(steps) - 1

        def t_mu(t, m=m):
            for kc in range(NCH):
                mm = t.matmul(psQ[0:1, 0:512], onesb[:, 0:1],
                              x1fm[:, NT * kc + 512 * m:NT * kc + 512 * (m + 1)],
                              start=(kc == 0), stop=(kc == NCH - 1))
            return mm
        i_tmu = add("tensor", t_mu, [])

        def v_sq(v, m=m):
            for kc in range(NCH):
                cp = v.tensor_tensor(
                    sqm[:, 512 * kc:512 * (kc + 1)],
                    x1fm[:, NT * kc + 512 * m:NT * kc + 512 * (m + 1)],
                    x1fm[:, NT * kc + 512 * m:NT * kc + 512 * (m + 1)], ALU.mult)
            return cp
        i_vsq = add("vector", v_sq, [])

        def t_sq(t, m=m):
            for kc in range(NCH):
                mm = t.matmul(psR[0:1, 0:512], onesb[:, 0:1],
                              sqm[:, 512 * kc:512 * (kc + 1)],
                              start=(kc == 0), stop=(kc == NCH - 1))
            return mm
        i_tsq = add("tensor", t_sq, [i_vsq])

        def v_stats(v, m=m):
            v.tensor_scalar(sm1[0:1, :], psQ[0:1, 0:512], 1.0 / D, None, ALU.mult)
            v.tensor_scalar(sm2[0:1, :], psR[0:1, 0:512], 1.0 / D, None, ALU.mult)
            v.tensor_tensor(sm3[0:1, :], sm1[0:1, :], sm1[0:1, :], ALU.mult)
            return v.tensor_tensor(sm2[0:1, :], sm2[0:1, :], sm3[0:1, :],
                                   ALU.subtract)
        i_vst = add("vector", v_stats, [i_tmu, i_tsq])
        i_srs = add("scalar",
                    lambda s: s.activation(sm3[0:1, :], sm2[0:1, :], ACT.Sqrt,
                                           bias=kcol[0:1, 0:1], scale=1.0),
                    [i_vst])
        i_vrs = add("vector", lambda v: v.reciprocal(sm2[0:1, :], sm3[0:1, :]),
                    [i_srs])

        def t_bc(t, m=m):
            t.matmul(psQ[:, 0:512], onesf[0:1, :], sm1[0:1, :],
                     start=True, stop=True)
            return t.matmul(psR[:, 0:512], onesf[0:1, :], sm2[0:1, :],
                            start=True, stop=True)
        i_tbc = add("tensor", t_bc, [i_vrs])

        def v_bc(v, m=m):
            v.tensor_copy(sm1[:, :], psQ[:, 0:512])
            return v.tensor_copy(sm2[:, :], psR[:, 0:512])
        i_vbc = add("vector", v_bc, [i_tbc])

        def v_nrm(v, m=m):
            for kc in range(NCH):
                v.tensor_tensor(sm3[:, :],
                                x1fm[:, NT * kc + 512 * m:NT * kc + 512 * (m + 1)],
                                sm1[:, :], ALU.subtract)
                v.tensor_tensor(sm3[:, :], sm3[:, :], sm2[:, :], ALU.mult)
                cp = v.tensor_scalar(h0m[:, 512 * kc:512 * (kc + 1)], sm3[:, :],
                                     g2col[:, kc:kc + 1], b2col[:, kc:kc + 1],
                                     ALU.mult, ALU.add)
            return cp
        i_vnrm = add("vector", v_nrm, [i_vbc])

        i_sy = {}
        i_t1 = {}
        i_s1 = {}
        for hc in range(NMH):
            wslc = slice(1024 * (hc % 2), 1024 * (hc % 2 + 1))

            def sy_w1(sy, m=m, hc=hc, wslc=wslc):
                if hc == 0:
                    sy.wait_ge(sem("dW1c"), 128)
                inc(sy.dma_start(
                    out=w1blk[:, wslc],
                    in_=AP(w1bf, hc * NCH * 128 * 128,
                           [[128, 128], [128 * 128, NCH], [1, 128]])), "dW1b", 16)
                sy.wait_ge(sem("dW1b"), cget("dW1b"))
                return sy.nop()
            i_sy[hc] = add("sync", sy_w1, [i_t1.get(hc - 2)])

            def t_m1(t, m=m, hc=hc, wslc=wslc):
                ps = psP if hc % 2 == 0 else psQ
                for kc in range(NCH):
                    mm = t.matmul(ps[:, 0:512],
                                  w1blk[:, wslc][:, 128 * kc:128 * (kc + 1)],
                                  h0m[:, 512 * kc:512 * (kc + 1)],
                                  start=(kc == 0), stop=(kc == NCH - 1))
                return mm
            i_t1[hc] = add("tensor", t_m1,
                           [i_sy[hc], i_vnrm, i_s1.get(hc - 2)])

            def s_m1(s, m=m, hc=hc):
                ps = psP if hc % 2 == 0 else psQ
                return s.activation(hact[:, 512 * hc:512 * (hc + 1)],
                                    ps[:, 0:512], ACT.Gelu,
                                    bias=bm1col[:, hc:hc + 1], scale=1.0)
            i_s1[hc] = add("scalar", s_m1, [i_t1[hc]])

        i_t2 = {}
        i_v2 = {}
        i_sp = {}
        for oc in range(NCH):
            oslc = slice(512 * (oc % 2), 512 * (oc % 2 + 1))

            def t_m2(t, m=m, oc=oc):
                ps = psS1 if oc % 2 == 0 else psS2
                for hc in range(NMH):
                    mm = t.matmul(
                        ps[:, 0:512],
                        w2S[:, 1024 * hc + 128 * oc:1024 * hc + 128 * (oc + 1)],
                        hact[:, 512 * hc:512 * (hc + 1)],
                        start=(hc == 0), stop=(hc == NMH - 1))
                return mm
            i_t2[oc] = add("tensor", t_m2,
                           [i_s1[NMH - 1], i_v2.get(oc - 2)])

            def v_m2(v, m=m, oc=oc, oslc=oslc):
                ps = psS1 if oc % 2 == 0 else psS2
                v.tensor_scalar(sm3[:, :], ps[:, 0:512], bm2col[:, oc:oc + 1],
                                None, ALU.add)
                return v.tensor_tensor(
                    sOut[:, oslc], sm3[:, :],
                    x1fm[:, NT * oc + 512 * m:NT * oc + 512 * (m + 1)], ALU.add)
            i_v2[oc] = add("vector", v_m2, [i_t2[oc], i_sp.get(oc - 2)])

            def sy_sp(sy, m=m, oc=oc, oslc=oslc):
                inc(sy.dma_start(out=s_spill[128 * oc:128 * (oc + 1),
                                             512 * m:512 * (m + 1)],
                                 in_=sOut[:, oslc]), "dSo", 16)
                sy.wait_ge(sem("dSo"), cget("dSo"))
                return sy.nop()
            i_sp[oc] = add("sync", sy_sp, [i_v2[oc]])
        chain2(steps, name="c2m")
        barrier()
    barrier()
    stC3.close()

    tcx = tap("x1fm", [128, NCH * NT], BF16)
    if tcx is not None:
        with nc.Block() as blk:
            @blk.sync
            def _(sy):
                inc(sy.dma_start(out=tcx[:, :], in_=x1fm[:, :]), "dO", 16)
                sy.wait_ge(sem("dO"), cget("dO"))
    stC1.close()

    stD = ExitStack()
    xqA = sbuf("xqA", [128, NCH * NT], BF16, stD)
    stD1 = ExitStack()
    wgS = sbuf("wgS", [128, NCH * IFF], BF16, stD1)
    wuS = sbuf("wuS", [128, NCH * IFF], BF16, stD1)
    xnstg2 = sbuf("xnstg2", [128, 2 * NT], BF16, stD1)
    dq1 = sbuf("dq1", [128, 1024], F32, stD1)
    dq2 = sbuf("dq2", [128, 1024], F32, stD1)
    dqI = sbuf("dqI", [128, 1024], I32, stD1)
    dqS = sbuf("dqS", [128, 512], BF16, stD1)
    svg = sbuf("svg", [128, NIC], F32, stD1)
    svu = sbuf("svu", [128, NIC], F32, stD1)
    clpS = sbuf("clpS", [1, 2], F32, stD1)
    midT = sbuf("midT", [128, 1024], BF16, stD1)
    sgT = sbuf("sgT", [128, 1024], BF16, stD1)
    dmx = sbuf("dmx", [128, 1], F32, stD1)

    with nc.Block() as blk:
        @blk.sync
        def _(sy):
            sy.wait_ge(sem("cc"), cget("cc"))
            inc(sy.dma_start(out=rowm[0:1, :], in_=cc1o[0:1, :]), "dD0", 16)
            inc(sy.dma_start(out=clpS[0:1, 0:1], in_=clipx.ap()), "dD0", 16)
            inc(sy.dma_start(out=clpS[0:1, 1:2], in_=clipd.ap()), "dD0", 16)

        @blk.scalar
        def _(s):
            s = DW(s)
            s.wait_ge(sem("dD0"), cget("dD0"))
            inc(s.activation(clpS[0:1, 0:1], clpS[0:1, 0:1], ACT.Exp,
                             bias=0.0, scale=-1.0), "sD0")

        @blk.vector
        def _(v):
            v = DW(v)
            v.wait_ge(sem("sD0"), cget("sD0"))
            v.tensor_scalar(clpS[0:1, 0:1], clpS[0:1, 0:1], 1.0, None, ALU.add)
            v.reciprocal(clpS[0:1, 0:1], clpS[0:1, 0:1])
            v.tensor_tensor(ccrow[0:1, 1:2], clpS[0:1, 0:1], rowm[0:1, 0:1],
                            ALU.mult)
            v.tensor_scalar(ccrow[0:1, 1:2], ccrow[0:1, 1:2], 1.0 / 127.0, None,
                            ALU.mult)
            inc(v.reciprocal(ccrow[0:1, 2:3], ccrow[0:1, 1:2]), "vD0")

        @blk.tensor
        def _(t):
            t.wait_ge(sem("vD0"), cget("vD0"))
            t.matmul(psR[:, 0:1], onesf[0:1, :], ccrow[0:1, 1:2],
                     start=True, stop=True)
            inc(t.matmul(psR[:, 1:2], onesf[0:1, :], ccrow[0:1, 2:3],
                         start=True, stop=True), "tD0")

    with nc.Block() as blk:
        @blk.vector
        def _(v):
            v = DW(v)
            v.wait_ge(sem("tD0"), cget("tD0"))
            v.tensor_copy(sxv[:, :], psR[:, 0:1])
            v.tensor_copy(rsxv[:, :], psR[:, 1:2])
            v.tensor_scalar(svg[:, :], scg[:, :], sxv[:, 0:1], None, ALU.mult)
            inc(v.tensor_scalar(svu[:, :], scu[:, :], sxv[:, 0:1], None,
                                ALU.mult), "vD1")

        @blk.gpsimd
        def _(g):
            for kc in range(NCH):
                inc(g.dma_start(out=wgS[:, IFF * kc:IFF * (kc + 1)],
                                in_=wgT_sp[128 * kc:128 * (kc + 1), :]), "dDg", 16)
                inc(g.dma_start(out=wuS[:, IFF * kc:IFF * (kc + 1)],
                                in_=wuT_sp[128 * kc:128 * (kc + 1), :]), "dDg", 16)

    def quant_tile(v, srcap, dstap, rvec, do_clip):
        v.tensor_scalar(dq1[:, :], srcap, rvec, None, ALU.mult)
        return None

    steps = []
    i_d1in, i_d1v = {}, {}

    def add1(eng, fn, deps):
        steps.append((eng, fn, [d for d in deps if d is not None]))
        return len(steps) - 1

    for cc in range(NCH):
        st0 = NT * (cc % 2)

        def sy_xn(sy, cc=cc, st0=st0):
            inc(sy.dma_start(out=xnstg2[:, st0:st0 + NT],
                             in_=xn_spill[128 * cc:128 * (cc + 1), :]),
                "dD1", 16)
            sy.wait_ge(sem("dD1"), cget("dD1"))
            return sy.nop()
        i_d1in[cc] = add1("sync", sy_xn, [i_d1v.get(cc - 2)])

        def v_q(v, cc=cc, st0=st0):
            for mp in range(4):
                v.tensor_scalar(dq1[:, :],
                                xnstg2[:, st0 + 1024 * mp:st0 + 1024 * (mp + 1)],
                                rsxv[:, 0:1], 128.5, ALU.mult, ALU.add)
                v.tensor_scalar(dq1[:, :], dq1[:, :], 0.0, None, ALU.max)
                v.tensor_copy(dqI[:, :], dq1[:, :])
                v.scalar_tensor_tensor(dq2[:, :], dqI[:, :], 1.0, dq1[:, :],
                                       ALU.mult, ALU.is_le)
                v.tensor_tensor(dq1[:, :], dqI[:, :], dq2[:, :], ALU.add)
                last = v.tensor_scalar(
                    xqA[:, NT * cc + 1024 * mp:NT * cc + 1024 * (mp + 1)],
                    dq1[:, :], -129.0, 127.0, ALU.add, ALU.min)
            return last
        i_d1v[cc] = add1("vector", v_q, [i_d1in[cc]])
    with nc.Block() as blk:
        @blk.vector
        def _(v):
            v.wait_ge(sem("vD1"), cget("vD1"))
        @blk.sync
        def _(sy):
            sy.wait_ge(sem("dDg"), cget("dDg"))
    chain2(steps, name="d1")

    with nc.Block() as blk:
        @blk.tensor
        def _(t):
            t.wait_ge(sem("dDg"), cget("dDg"))
    barrier()

    steps = []

    def addd(eng, fn, deps):
        steps.append((eng, fn, [d for d in deps if d is not None]))
        return len(steps) - 1

    i_t = {}
    i_s = {}
    i_v = {}
    i_y = {}
    it = 0
    for m in range(8):
        for ic in range(NIC):
            pg = psP if it % 2 == 0 else psS1
            pu = psQ if it % 2 == 0 else psS2
            hp = slice(512 * (it % 2), 512 * (it % 2 + 1))

            def t_gu(t, m=m, ic=ic, pg=pg, pu=pu):
                for kc in range(NCH):
                    t.matmul(pg[:, 0:512],
                             wgS[:, IFF * kc + 128 * ic:IFF * kc + 128 * (ic + 1)],
                             xqA[:, NT * kc + 512 * m:NT * kc + 512 * (m + 1)],
                             start=(kc == 0), stop=(kc == NCH - 1))
                for kc in range(NCH):
                    mm = t.matmul(
                        pu[:, 0:512],
                        wuS[:, IFF * kc + 128 * ic:IFF * kc + 128 * (ic + 1)],
                        xqA[:, NT * kc + 512 * m:NT * kc + 512 * (m + 1)],
                        start=(kc == 0), stop=(kc == NCH - 1))
                return mm
            i_t[it] = addd("tensor", t_gu, [i_s.get(it - 2), i_v.get(it - 2)])

            def s_gu(s, m=m, ic=ic, pg=pg, hp=hp):
                return s.activation(sgT[:, hp], pg[:, 0:512], ACT.Silu,
                                    bias=0.0, scale=svg[:, ic:ic + 1])
            i_s[it] = addd("scalar", s_gu, [i_t[it], i_v.get(it - 2)])

            def v_gu(v, m=m, ic=ic, pu=pu, hp=hp):
                v.scalar_tensor_tensor(midT[:, hp], pu[:, 0:512],
                                       svu[:, ic:ic + 1], sgT[:, hp],
                                       ALU.mult, ALU.mult)
                v.reduce_max(dmx[:, :], midT[:, hp], AX.X,
                             apply_absolute_value=True)
                return v.tensor_tensor(rmaxcol2[:, :], rmaxcol2[:, :], dmx[:, :],
                                       ALU.max)
            i_v[it] = addd("vector", v_gu, [i_s[it], i_y.get(it - 2)])

            def sy_gu(sy, m=m, ic=ic, hp=hp):
                inc(sy.dma_start(out=mid_spill[128 * ic:128 * (ic + 1),
                                               512 * m:512 * (m + 1)],
                                 in_=midT[:, hp]), "dMo", 16)
                sy.wait_ge(sem("dMo"), cget("dMo"))
                return sy.nop()
            i_y[it] = addd("sync", sy_gu, [i_v[it]])
            it += 1
    chain2(steps, name="d2")
    barrier()
    stD1.close()

    stD2 = ExitStack()
    wdS = sbuf("wdS", [128, NIC * 1024], BF16, stD2)
    mqA = sbuf("mqA", [128, 2 * NIC * 512], BF16, stD2)
    mres = sbuf("mres", [128, 2 * NCH * 512], BF16, stD2)
    sStg = sbuf("sStg", [128, 1024], BF16, stD2)
    xStg = sbuf("xStg", [128, 2048], F32, stD2)
    ysb = sbuf("ysb", [128, 2048], F32, stD2)
    dq1b = sbuf("dq1b", [128, 1024], F32, stD2)
    dq2b = sbuf("dq2b", [128, 1024], F32, stD2)
    dqIb = sbuf("dqIb", [128, 1024], I32, stD2)
    dqSb = sbuf("dqSb", [128, 512], BF16, stD2)
    sgT2 = sbuf("sgT2", [128, 1024], BF16, stD2)
    svdb = sbuf("svdb", [128, NCH], F32, stD2)
    clp2 = sbuf("clp2", [1, 2], F32, stD2)

    with nc.Block() as blk:
        @blk.sync
        def _(sy):
            inc(sy.dma_start(out=maxrt2[0:1, :], in_=rmaxcol2[:, 0:1]), "dD2", 16)
            sy.wait_ge(sem("dD2"), cget("dD2"))
            inc(sy.dma_start(out=rowm[0:1, :], in_=maxrt2[0:1, :]), "dD3", 16)

        @blk.vector
        def _(v):
            v = DW(v)
            v.wait_ge(sem("dD3"), cget("dD3"))
            inc(v.reduce_max(cc2row[0:1, 0:1], rowm[0:1, :], AX.X), "vD3")

        @blk.gpsimd
        def _(g):
            g.wait_ge(sem("vD3"), cget("vD3"))
            inc(g.dma_start(out=cc2i[0:1, :], in_=cc2row[0:1, :]), "dW", 16)
            g.wait_ge(sem("dW"), cget("dW"))
            if "nocc" in DBG:
                inc(g.dma_start(out=cc2o[0:1, :], in_=cc2i[0:1, :]), "cc2", 16)
            else:
                inc(g.collective_compute(
                    "AllReduce", ALU.max, replica_groups=[list(range(8))],
                    ins=[cc2i.ap().opt()], outs=[cc2o.ap().opt()]), "cc2", 1)

    with nc.Block() as blk:
        @blk.sync
        def _(sy):
            sy.wait_ge(sem("cc2"), cget("cc2"))
            inc(sy.dma_start(out=rowm[0:1, :], in_=cc2o[0:1, :]), "dD4", 16)
            inc(sy.dma_start(out=clp2[0:1, 1:2], in_=clipd.ap()), "dD4", 16)
            for ic in range(NIC):
                inc(sy.dma_start(out=wdS[:, 1024 * ic:1024 * (ic + 1)],
                                 in_=wdT_sp[128 * ic:128 * (ic + 1), :]),
                    "dD5", 16)

        @blk.scalar
        def _(s):
            s = DW(s)
            s.wait_ge(sem("dD4"), cget("dD4"))
            inc(s.activation(clp2[0:1, 1:2], clp2[0:1, 1:2], ACT.Exp,
                             bias=0.0, scale=-1.0), "sD1")

        @blk.vector
        def _(v):
            v = DW(v)
            v.wait_ge(sem("sD1"), cget("sD1"))
            v.tensor_scalar(clp2[0:1, 1:2], clp2[0:1, 1:2], 1.0, None, ALU.add)
            v.reciprocal(clp2[0:1, 1:2], clp2[0:1, 1:2])
            v.tensor_tensor(ccrow[0:1, 5:6], clp2[0:1, 1:2], rowm[0:1, 0:1],
                            ALU.mult)
            v.tensor_scalar(ccrow[0:1, 5:6], ccrow[0:1, 5:6], 1.0 / 127.0, None,
                            ALU.mult)
            inc(v.reciprocal(ccrow[0:1, 6:7], ccrow[0:1, 5:6]), "vD4")

        @blk.tensor
        def _(t):
            t.wait_ge(sem("vD4"), cget("vD4"))
            t.matmul(psR[:, 2:3], onesf[0:1, :], ccrow[0:1, 5:6],
                     start=True, stop=True)
            inc(t.matmul(psR[:, 3:4], onesf[0:1, :], ccrow[0:1, 6:7],
                         start=True, stop=True), "tD1")

    with nc.Block() as blk:
        @blk.vector
        def _(v):
            v = DW(v)
            v.wait_ge(sem("tD1"), cget("tD1"))
            v.tensor_copy(sdv[:, :], psR[:, 2:3])
            v.tensor_copy(rsdv[:, :], psR[:, 3:4])
            inc(v.tensor_scalar(svdb[:, :], scd[:, :], sdv[:, 0:1], None,
                                ALU.mult), "vD5")

    steps = []

    def addf(eng, fn, deps):
        steps.append((eng, fn, [d for d in deps if d is not None]))
        return len(steps) - 1

    i_mid = {}
    i_q = {}
    i_tdn = {}
    i_sdn = {}
    i_vdn = {}
    i_tf = {}
    i_vf = {}
    i_yy = {}
    for m in range(8):
        mq0 = NIC * 512 * (m % 2)

        def sy_mid(sy, m=m, mq0=mq0):
            for ic in range(NIC):
                inc(sy.dma_start(out=mqA[:, mq0 + 512 * ic:mq0 + 512 * (ic + 1)],
                                 in_=mid_spill[128 * ic:128 * (ic + 1),
                                               512 * m:512 * (m + 1)]),
                    "dD6", 16)
            sy.wait_ge(sem("dD6"), cget("dD6"))
            return sy.nop()
        i_mid[m] = addf("sync", sy_mid, [i_tdn.get((m - 2, NCH - 1))])

        for ip in range(NIC // 2):
            def v_mq(v, m=m, ip=ip, mq0=mq0):
                v.tensor_scalar(dq1b[:, :],
                                mqA[:, mq0 + 1024 * ip:mq0 + 1024 * (ip + 1)],
                                rsdv[:, 0:1], 128.5, ALU.mult, ALU.add)
                v.tensor_scalar(dq1b[:, :], dq1b[:, :], 0.0, None, ALU.max)
                v.tensor_copy(dqIb[:, :], dq1b[:, :])
                v.scalar_tensor_tensor(dq2b[:, :], dqIb[:, :], 1.0, dq1b[:, :],
                                       ALU.mult, ALU.is_le)
                v.tensor_tensor(dq1b[:, :], dqIb[:, :], dq2b[:, :], ALU.add)
                return v.tensor_scalar(mqA[:, mq0 + 1024 * ip:mq0 + 1024 * (ip + 1)],
                                       dq1b[:, :], -129.0, 127.0, ALU.add,
                                       ALU.min)
            i_q[(m, 2 * ip + 1)] = addf("vector", v_mq, [i_mid[m]])

        for oc in range(NCH):
            it2 = 8 * m + oc
            pd = psP if it2 % 2 == 0 else psQ
            hp = slice(512 * (it2 % 2), 512 * (it2 % 2 + 1))

            def t_dn(t, m=m, oc=oc, pd=pd, mq0=mq0):
                for ic in range(NIC):
                    mm = t.matmul(
                        pd[:, 0:512],
                        wdS[:, 1024 * ic + 128 * oc:1024 * ic + 128 * (oc + 1)],
                        mqA[:, mq0 + 512 * ic:mq0 + 512 * (ic + 1)],
                        start=(ic == 0), stop=(ic == NIC - 1))
                return mm
            i_tdn[(m, oc)] = addf("tensor", t_dn,
                                  [i_q[(m, NIC - 1)], i_sdn.get((m, oc - 2)),
                                   i_tf.get((m - 1, 3))])

            def s_dn(s, m=m, oc=oc, pd=pd, hp=hp):
                return s.activation(sgT2[:, hp], pd[:, 0:512], ACT.Silu,
                                    bias=0.0, scale=svdb[:, oc:oc + 1])
            i_sdn[(m, oc)] = addf("scalar", s_dn,
                                  [i_tdn[(m, oc)], i_vdn.get((m, oc - 2))])

            def sy_s(sy, m=m, oc=oc, hp=hp):
                inc(sy.dma_start(out=sStg[:, hp],
                                 in_=s_spill[128 * oc:128 * (oc + 1),
                                             512 * m:512 * (m + 1)]), "dD7", 16)
                sy.wait_ge(sem("dD7"), cget("dD7"))
                return sy.nop()
            isys = addf("sync", sy_s, [i_vdn.get((m, oc - 2))])

            def v_dn(g, m=m, oc=oc, hp=hp):
                return g.tensor_tensor(
                    mres[:, NCH * 512 * (m % 2) + 512 * oc:
                         NCH * 512 * (m % 2) + 512 * (oc + 1)],
                    sgT2[:, hp], sStg[:, hp], ALU.mult)
            i_vdn[(m, oc)] = addf("gpsimd", v_dn,
                                  [i_sdn[(m, oc)], isys,
                                   i_tf.get((m - 1, 3))])

        for u in range(4):
            tau = 4 * m + u
            bank = psA if u % 2 == 0 else psB
            xh = slice(1024 * (u % 2), 1024 * (u % 2 + 1))

            def t_fin(t, m=m, u=u, bank=bank):
                for oc in range(NCH):
                    tr = t.transpose(
                        bank[:, 128 * oc:128 * (oc + 1)],
                        mres[:, NCH * 512 * (m % 2) + 512 * oc + 128 * u:
                             NCH * 512 * (m % 2) + 512 * oc + 128 * (u + 1)],
                        idbf[:, :])
                return tr
            i_tf[(m, u)] = addf("tensor", t_fin,
                                [i_vdn[(m, NCH - 1)], i_vf.get((m, u - 2))])

            def sy_x(sy, tau=tau, xh=xh):
                inc(sy.dma_start(out=xStg[:, xh],
                                 in_=x_in[128 * tau:128 * (tau + 1), :]), "dD8", 16)
                sy.wait_ge(sem("dD8"), cget("dD8"))
                return sy.nop()
            isyx = addf("sync", sy_x, [i_vf.get((m, u - 2))])

            def v_fin(v, tau=tau, bank=bank, xh=xh):
                return v.tensor_tensor(ysb[:, xh], bank[:, 0:1024], xStg[:, xh],
                                       ALU.add)
            i_vf[(m, u)] = addf("vector", v_fin,
                                [i_tf[(m, u)], isyx, i_yy.get((m, u - 2))])

            def sy_y(sy, tau=tau, xh=xh):
                inc(sy.dma_start(out=y_out[128 * tau:128 * (tau + 1), :],
                                 in_=ysb[:, xh]), "dY", 16)
                sy.wait_ge(sem("dY"), cget("dY"))
                return sy.nop()
            i_yy[(m, u)] = addf("sync", sy_y, [i_vf[(m, u)]])
    with nc.Block() as blk:
        @blk.tensor
        def _(t):
            t.wait_ge(sem("vD5"), cget("vD5"))
        @blk.sync
        def _(sy):
            sy.wait_ge(sem("dD5"), cget("dD5"))
    barrier()
    chain2(steps, name="d4")
    barrier()
    stD2.close()
    stD.close()


    stack.close()
    return nc, taps


def _consts():
    one = np.ones((128, 128), dtype=np.float32)
    return {
        "c_idbf": np.eye(128, dtype=ml_dtypes.bfloat16),
        "c_ut": np.triu(one).astype(ml_dtypes.bfloat16),
        "c_sut": np.triu(one, 1).astype(ml_dtypes.bfloat16),
        "c_onesb": one.astype(ml_dtypes.bfloat16),
        "c_onesf": one,
        "c_iota128": np.arange(128, dtype=np.float32).reshape(128, 1),
        "c_iota32": np.arange(32, dtype=np.float32).reshape(1, 32),
    }


def _in_maps(inputs):
    x = np.ascontiguousarray(np.asarray(inputs["L0_attn_x"], dtype=np.float32))
    shared = {
        "wq": inputs["L0_attn_Wq"], "wk": inputs["L0_attn_Wk"],
        "wv": inputs["L0_attn_Wv"], "wgt": inputs["L0_attn_gate_w"],
        "wp": inputs["L0_attn_p_w"], "wo": inputs["L0_attn_out_w"],
        "wclk": inputs["L0_attn_clock_w"],
        "b_gate": inputs["L0_attn_gate_b"], "b_out": inputs["L0_attn_out_b"],
        "b_clk": inputs["L0_attn_clock_b"],
        "g1": inputs["L0_attn_ln1_g"], "b1": inputs["L0_attn_ln1_b"],
        "g2": inputs["L0_attn_ln2_g"], "b2": inputs["L0_attn_ln2_b"],
        "w1": inputs["L0_attn_mlp_w1"], "bm1": inputs["L0_attn_mlp_b1"],
        "w2": inputs["L0_attn_mlp_w2"], "bm2": inputs["L0_attn_mlp_b2"],
        "clipx": inputs["L0_ffn_clip_x"], "clipd": inputs["L0_ffn_clip_down"],
        "wgate": inputs["L0_ffn_w_gate"], "wup": inputs["L0_ffn_w_up"],
        "wdown": inputs["L0_ffn_w_down"],
    }
    shared = {k: np.ascontiguousarray(np.asarray(v, dtype=np.float32))
              for k, v in shared.items()}
    shared.update(_consts())
    maps = []
    for core in range(8):
        m = dict(shared)
        m["x"] = np.ascontiguousarray(
            x[core * BPC:(core + 1) * BPC].reshape(NT, D))
        maps.append(m)
    return maps


_RES = {}


def kernel(**inputs):
    nc, taps = build_program()
    maps = _in_maps(inputs)
    res = run_bass_kernel_spmd(nc, maps, core_ids=list(range(8)))
    _RES["results"] = res.results
    ys = [np.asarray(res.results[c]["y"]).reshape(BPC, T, D) for c in range(8)]
    return np.concatenate(ys, axis=0)


# version 19
# speedup vs baseline: 1.0728x; 1.0170x over previous
import math
import numpy as np
import ml_dtypes
from contextlib import ExitStack

import concourse.bass as bass
import concourse.mybir as mybir
from concourse.bass_utils import run_bass_kernel_spmd

F32 = mybir.dt.float32
BF16 = mybir.dt.bfloat16
I32 = mybir.dt.int32
ACT = mybir.ActivationFunctionType
ALU = mybir.AluOpType
AX = mybir.AxisListType
AP = bass.AP

D, T, BPC = 1024, 2048, 2
NT = BPC * T
H, DH = 16, 64
NCH, NTAU = 8, 32
IFF, NIC = 2816, 22
MH, NMH = 4096, 32
EPS = 1e-5
TWO_PI = 2.0 * math.pi
PI_CLAMP = 3.1415920
RND_CAP = 128.49

DBG = []


def build_program():
    nc = bass.Bass("TRN2", target_bir_lowering=False)

    def din(name, shape):
        return nc.dram_tensor(name, shape, F32, kind="ExternalInput")

    x_in = din("x", [NT, D])
    wq, wk, wv, wg, wp, wo = (din(n, [D, D]) for n in
                              ("wq", "wk", "wv", "wgt", "wp", "wo"))
    wclk = din("wclk", [D, H])
    b_gate, b_out, b_clk = din("b_gate", [D]), din("b_out", [D]), din("b_clk", [H])
    g1v, b1v = din("g1", [D]), din("b1", [D])
    g2v, b2v = din("g2", [D]), din("b2", [D])
    w1, bm1, w2, bm2 = din("w1", [D, MH]), din("bm1", [MH]), din("w2", [MH, D]), din("bm2", [D])
    clipx, clipd = din("clipx", [1]), din("clipd", [1])
    wgate, wup, wdown = din("wgate", [IFF, D]), din("wup", [IFF, D]), din("wdown", [D, IFF])
    c_idbf = nc.dram_tensor("c_idbf", [128, 128], BF16, kind="ExternalInput")
    c_ut = nc.dram_tensor("c_ut", [128, 128], BF16, kind="ExternalInput")
    c_sut = nc.dram_tensor("c_sut", [128, 128], BF16, kind="ExternalInput")
    c_onesb = nc.dram_tensor("c_onesb", [128, 128], BF16, kind="ExternalInput")
    c_onesf = nc.dram_tensor("c_onesf", [128, 128], F32, kind="ExternalInput")
    c_iota128 = nc.dram_tensor("c_iota128", [128, 1], F32, kind="ExternalInput")
    c_iota32 = nc.dram_tensor("c_iota32", [1, 32], F32, kind="ExternalInput")

    y_out = nc.dram_tensor("y", [NT, D], F32, kind="ExternalOutput")

    xn_spill = nc.dram_tensor("xn_spill", [D, NT], BF16)
    out_spill = nc.dram_tensor("out_spill", [D, NT], BF16)
    s_spill = nc.dram_tensor("s_spill", [D, NT], BF16)
    mid_spill = nc.dram_tensor("mid_spill", [IFF, NT], BF16)
    w1bf = nc.dram_tensor("w1bf", [NMH, NCH, 128, 128], BF16)
    wgT_sp = nc.dram_tensor("wgT_sp", [D, IFF], BF16)
    wuT_sp = nc.dram_tensor("wuT_sp", [D, IFF], BF16)
    wdT_sp = nc.dram_tensor("wdT_sp", [IFF, D], BF16)
    maxrt = nc.dram_tensor("maxrt", [1, 128], F32)
    maxrt2 = nc.dram_tensor("maxrt2", [1, 128], F32)
    cc1i, cc1o = nc.dram_tensor("cc1i", [1, 128], F32), nc.dram_tensor("cc1o", [1, 128], F32)
    cc2i, cc2o = nc.dram_tensor("cc2i", [1, 128], F32), nc.dram_tensor("cc2o", [1, 128], F32)

    taps = {}

    def tap(name, shape, dtype):
        if name in DBG and name not in taps:
            taps[name] = nc.dram_tensor("tap_" + name, shape, dtype,
                                        kind="ExternalOutput")
        return taps.get(name)

    stack = ExitStack()
    sems, cnt = {}, {}

    def sem(name):
        if name not in sems:
            sems[name] = stack.enter_context(nc.semaphore(name))
            cnt[name] = 0
        return sems[name]

    def inc(instr, name, k=1):
        instr.then_inc(sem(name), k)
        cnt[name] += k
        return cnt[name]

    def cget(name):
        sem(name)
        return cnt[name]

    dw_recent = {"vector": [], "scalar": [], "gpsimd": []}

    class DW:
        _PASS = {"wait_ge", "sem_inc", "nop"}
        _NOTRACK = {"dma_start"}
        WINDOW = 2

        def __init__(self, eng):
            self._e = eng
            tn = type(eng).__name__.lower()
            self._k = ("vector" if "vector" in tn else
                       "scalar" if "scalar" in tn else "gpsimd")

        def drain(self):
            dw_recent[self._k].clear()
            return self._e.drain()

        def __getattr__(self, n):
            f = getattr(self._e, n)
            if n in DW._PASS or n in DW._NOTRACK or not callable(f):
                return f
            rec = dw_recent[self._k]
            def g(*a, **k):
                names = []
                for o in list(a) + list(k.values()):
                    t = getattr(o, "tensor", None)
                    if t is not None:
                        names.append(getattr(t, "name", None) or id(t))
                outs, ins = set(names[:1]), set(names[1:])
                if any((ins & ro) or (outs & ro) or (outs & ri)
                       for ri, ro in rec):
                    self._e.drain()
                    rec.clear()
                r = f(*a, **k)
                rec.append((ins, outs))
                while len(rec) > DW.WINDOW:
                    rec.pop(0)
                return r
            return g

    def chain(steps, name="ch"):
        base = cget(name)
        with nc.Block() as blk:
            for en in ENGS:
                if not any(e == en for e, _ in steps):
                    continue

                def body(eng, en=en):
                    if en in ("vector", "scalar", "gpsimd"):
                        eng = DW(eng)
                    for i, (e2, fn) in enumerate(steps):
                        if e2 != en:
                            continue
                        if i > 0:
                            eng.wait_ge(sem(name), base + i)
                        last = fn(eng)
                        assert last is not None, f"chain step {i} returned None"
                        inc(last, name)
                getattr(blk, en)(body)

    def chain2(steps, name="c2"):
        if "serial2" in DBG or ("ser:" + name) in DBG:
            steps = [(e, fn, [i - 1] if i > 0 else [])
                     for i, (e, fn, deps) in enumerate(steps)]
        ordinals = []
        count = {}
        for e, fn, deps in steps:
            ordinals.append(count.get(e, 0))
            count[e] = count.get(e, 0) + 1
        bases = {e: cget(f"{name}_{e}") for e in count}
        with nc.Block() as blk:
            for en in list(count):
                def body(eng, en=en):
                    if en in ("vector", "scalar", "gpsimd"):
                        eng = DW(eng)
                    for i, (e, fn, deps) in enumerate(steps):
                        if e != en:
                            continue
                        need = {}
                        for d in deps:
                            de = steps[d][0]
                            if de == en:
                                continue
                            need[de] = max(need.get(de, -1), ordinals[d])
                        for de, o in need.items():
                            eng.wait_ge(sem(f"{name}_{de}"), bases[de] + o + 1)
                        last = fn(eng)
                        assert last is not None, f"chain2 step {i} returned None"
                        inc(last, f"{name}_{en}")
                getattr(blk, en)(body)

    bar_tgt = [0]
    ENGS = ["sync", "tensor", "vector", "scalar", "gpsimd"]

    def barrier():
        bar = sem("bar")
        bar_tgt[0] += 5
        tgt = bar_tgt[0]
        with nc.Block() as blk:
            for en in ENGS:
                def mk(en=en):
                    @getattr(blk, en)
                    def _(eng):
                        if en != "sync":
                            eng.drain()
                        eng.sem_inc(bar, 1)
                        eng.wait_ge(bar, tgt)
                mk()
        for rec in dw_recent.values():
            rec.clear()

    def sbuf(name, shape, dtype, st=None):
        return (st or stack).enter_context(nc.sbuf_tensor(name, shape, dtype))

    def psum(name, shape, dtype=F32, st=None):
        return (st or stack).enter_context(nc.psum_tensor(name, shape, dtype))

    idbf = sbuf("idbf", [128, 128], BF16)
    utb = sbuf("utb", [128, 128], BF16)
    onesb = sbuf("onesb", [128, 128], BF16)
    onesf = sbuf("onesf", [128, 128], F32)
    iota128 = sbuf("iota128", [128, 1], F32)
    iota32 = sbuf("iota32", [1, 32], F32)

    bocol = sbuf("bocol", [128, 8], F32)
    g2col = sbuf("g2col", [128, 8], F32)
    b2col = sbuf("b2col", [128, 8], F32)
    bm2col = sbuf("bm2col", [128, 8], F32)
    bm1col = sbuf("bm1col", [128, 32], F32)

    rmaxcol = sbuf("rmaxcol", [128, 1], F32)
    rmaxcol2 = sbuf("rmaxcol2", [128, 1], F32)
    rowm = sbuf("rowm", [1, 128], F32)
    ccrow = sbuf("ccrow", [1, 128], F32)
    cc2row = sbuf("cc2row", [1, 128], F32)
    sxv = sbuf("sxv", [128, 1], F32)
    rsxv = sbuf("rsxv", [128, 1], F32)
    sdv = sbuf("sdv", [128, 1], F32)
    rsdv = sbuf("rsdv", [128, 1], F32)
    scg = sbuf("scg", [128, NIC], F32)
    scu = sbuf("scu", [128, NIC], F32)
    scd = sbuf("scd", [128, NCH], F32)
    kcol = sbuf("kcol", [128, 4], F32)

    stXN1 = ExitStack()
    xn1_fm = sbuf("xn1_fm", [128, NCH * NT], BF16, stXN1)
    stTab = ExitStack()
    cosT2 = sbuf("cosT2", [128, NTAU * 64], F32, stTab)
    sinT = sbuf("sinT", [128, NTAU * 64], F32, stTab)
    clockS = sbuf("clockS", [128, NTAU * H], F32, stTab)
    maskadd = sbuf("maskadd", [128, 4 * 512], BF16, stTab)
    brepc = sbuf("brepc", [128, H], F32, stTab)
    growB = sbuf("growB", [1, D], F32, stTab)
    st_early = ExitStack()
    brow = sbuf("brow", [1, 4096], F32, st_early)
    g1rep = sbuf("g1rep", [128, D], F32, st_early)
    b1rep = sbuf("b1rep", [128, D], F32, st_early)
    stArena = ExitStack()
    arena1 = sbuf("arena1", [128, NCH * NT], BF16, stArena)

    psA = psum("psA", [128, 1024], BF16)
    psB = psum("psB", [128, 1024], BF16)
    psP = psum("psP", [128, 512], F32)
    psQ = psum("psQ", [128, 512], F32)
    psS1 = psum("psS1", [128, 512], F32)
    psS2 = psum("psS2", [128, 512], F32)
    psO = psum("psO", [128, 512], F32)
    psR = psum("psR", [128, 512], F32)

    with nc.Block() as blk:
        @blk.sync
        def _(sy):
            for dst, src in [(idbf, c_idbf), (utb, c_ut),
                             (onesb, c_onesb), (onesf, c_onesf),
                             (iota128, c_iota128), (iota32, c_iota32)]:
                inc(sy.dma_start(out=dst[:, :], in_=src[:, :]), "dA", 16)

        @blk.gpsimd
        def _(g):
            for kc in range(NCH):
                inc(g.dma_start(
                    out=AP(w1bf, kc * 128 * 128,
                           [[128, 128], [8 * 128 * 128, NMH], [1, 128]]),
                    in_=AP(w1, kc * 128 * MH,
                           [[MH, 128], [128, NMH], [1, 128]])), "dW1c", 16)

        @blk.vector
        def _(v):
            v = DW(v)
            v.wait_ge(sem("dA"), cget("dA"))
            for s_ in range(4):
                base = s_ * 512
                if s_ > 0:
                    v.memset(maskadd[:, base:base + 128 * s_], -4000.0)
                v.tensor_scalar(maskadd[:, base + 128 * s_:
                                        base + 128 * (s_ + 1)],
                                utb[:, :], -1.0, 4000.0, ALU.add, ALU.mult)
                if s_ < 3:
                    v.memset(maskadd[:, base + 128 * (s_ + 1):base + 512], 0.0)
            v.memset(rmaxcol[:, :], 0.0)
            v.memset(rmaxcol2[:, :], 0.0)
            v.memset(ccrow[:, :], 0.0)
            v.memset(cc2row[:, :], 0.0)
            v.memset(kcol[:, 0:1], EPS)
            v.memset(kcol[:, 1:2], EPS * EPS)
            v.memset(kcol[:, 2:3], 1.0)
            v.memset(kcol[:, 3:4], 0.5)

    barrier()

    st0 = ExitStack()
    thS = sbuf("thS", [128, NTAU * 32], F32, st0)
    thC = sbuf("thC", [128, NTAU * 32], F32, st0)
    snR = sbuf("snR", [128, NTAU * 32], F32, st0)
    rpF = sbuf("rpF", [128, 32], F32, st0)
    rpB = sbuf("rpB", [128, 32], F32, st0)
    rpC = sbuf("rpC", [128, 32], F32, st0)
    rpI = sbuf("rpI", [128, 32], I32, st0)
    tvcol = sbuf("tvcol", [128, 1], F32, st0)

    def range_reduce(v, src_ap, dst_ap, shift):
        v.tensor_scalar(rpB[:, :], src_ap, float(shift), 1.0 / TWO_PI,
                        ALU.add, ALU.mult)
        v.tensor_copy(rpI[:, :], rpB[:, :])
        v.tensor_scalar(rpC[:, :], src_ap, float(shift), None, ALU.add)
        v.scalar_tensor_tensor(rpB[:, :], rpI[:, :], -TWO_PI, rpC[:, :],
                               ALU.mult, ALU.add)
        v.tensor_scalar(rpC[:, :], rpB[:, :], math.pi, -TWO_PI,
                        ALU.is_ge, ALU.mult)
        v.tensor_tensor(rpB[:, :], rpB[:, :], rpC[:, :], ALU.add)
        v.tensor_scalar(rpC[:, :], rpB[:, :], -math.pi, TWO_PI,
                        ALU.is_le, ALU.mult)
        v.tensor_tensor(rpB[:, :], rpB[:, :], rpC[:, :], ALU.add)
        return v.tensor_scalar(dst_ap, rpB[:, :], PI_CLAMP, -PI_CLAMP,
                               ALU.min, ALU.max)

    with nc.Block() as blk:
        @blk.scalar
        def _(s):
            s = DW(s)
            inc(s.activation(brow[0:1, 0:32], iota32[0:1, :], ACT.Exp,
                             bias=0.0, scale=-math.log(10000.0) / 32.0), "sA")
            s.wait_ge(sem("vA_rope"), 1)
            s.activation(snR[:, :], thS[:, :], ACT.Sin)
            inc(s.activation(thC[:, :], thC[:, :], ACT.Sin), "sA")

        @blk.tensor
        def _(t):
            t.wait_ge(sem("sA"), cget("sA") - 1)
            inc(t.matmul(psR[:, 0:32], onesf[0:1, :], brow[0:1, 0:32],
                         start=True, stop=True), "tA")

        @blk.vector
        def _(v):
            v = DW(v)
            v.wait_ge(sem("tA"), cget("tA"))
            v.tensor_copy(rpF[:, :], psR[:, 0:32])
            for tau in range(NTAU):
                pos = (tau % 16) * 128
                v.tensor_scalar(tvcol[:, :], iota128[:, :], float(pos), None, ALU.add)
                v.tensor_scalar(thS[:, tau * 32:(tau + 1) * 32], rpF[:, :],
                                tvcol[:, 0:1], None, ALU.mult)
                range_reduce(v, thS[:, tau * 32:(tau + 1) * 32],
                             thC[:, tau * 32:(tau + 1) * 32], math.pi / 2)
                last = range_reduce(v, thS[:, tau * 32:(tau + 1) * 32],
                                    thS[:, tau * 32:(tau + 1) * 32], 0.0)
            inc(last, "vA_rope")
            v.wait_ge(sem("sA"), cget("sA"))
            v.tensor_scalar(AP(sinT, 0, [[NTAU * 64, 128], [2, NTAU * 32]]),
                            snR[:, :], -1.0, None, ALU.mult)
            v.tensor_copy(AP(sinT, 1, [[NTAU * 64, 128], [2, NTAU * 32]]),
                          snR[:, :])
            v.tensor_copy(AP(cosT2, 0, [[NTAU * 64, 128], [2, NTAU * 32]]),
                          thC[:, :])
            inc(v.tensor_copy(AP(cosT2, 1, [[NTAU * 64, 128], [2, NTAU * 32]]),
                              thC[:, :]), "vA")

    barrier()
    st0.close()

    jobs = [
        ("rep", g1v, D, g1rep), ("rep", b1v, D, b1rep), ("rep", b_clk, H, brepc),
        ("col", b_out, D, bocol), ("col", g2v, D, g2col), ("col", b2v, D, b2col),
        ("col", bm2, D, bm2col), ("col", bm1, MH, bm1col),
    ]
    with nc.Block() as blk:
        @blk.sync
        def _(sy):
            inc(sy.dma_start(out=growB[0:1, :], in_=b_gate.ap()), "dB", 16)
    nvb = []
    for kind, src_, n, dst in jobs:
        nvb.append(1 if kind == "rep" else (n // 128 + 7) // 8)

    with nc.Block() as blk:
        @blk.sync
        def _(sy):
            base_vb = cget("vB")
            for j, (kind, src_, n, dst) in enumerate(jobs):
                if j > 0:
                    sy.wait_ge(sem("vB"), base_vb + sum(nvb[:j]))
                inc(sy.dma_start(out=brow[0:1, 0:n], in_=src_.ap()), "dB", 16)

        @blk.tensor
        def _(t):
            base_db = cget("dB") - 16 * len(jobs)
            base_vb = cget("vB")
            base_tb = cget("tB")
            vb_done = 0
            for j, (kind, src_, n, dst) in enumerate(jobs):
                t.wait_ge(sem("dB"), base_db + 16 * (j + 1))
                if kind == "rep":
                    for o in range(0, n, 512):
                        w_ = min(512, n - o)
                        pst = psR if o == 0 else psQ
                        mm = t.matmul(pst[:, 0:w_], onesf[0:1, :],
                                      brow[0:1, o:o + w_], start=True, stop=True)
                    inc(mm, "tB")
                    vb_done += 1
                else:
                    ncol = n // 128
                    for gidx in range(0, ncol, 8):
                        if vb_done > 0 or gidx > 0:
                            t.wait_ge(sem("vB"), base_vb + vb_done)
                        for jj in range(gidx, min(gidx + 8, ncol)):
                            mm = t.matmul(psR[:, (jj % 8):(jj % 8) + 1],
                                          brow[0:1, 128 * jj:128 * (jj + 1)],
                                          onesf[0:1, 0:1], start=True, stop=True)
                        inc(mm, "tB")
                        vb_done += 1

        @blk.vector
        def _(v):
            v = DW(v)
            base_tb = cget("tB") - sum(nvb)
            ng = 0
            for j, (kind, src_, n, dst) in enumerate(jobs):
                if kind == "rep":
                    ng += 1
                    v.wait_ge(sem("tB"), base_tb + ng)
                    for o in range(0, n, 512):
                        w_ = min(512, n - o)
                        pst = psR if o == 0 else psQ
                        cp = v.tensor_copy(dst[:, o:o + w_], pst[:, 0:w_])
                    inc(cp, "vB")
                else:
                    ncol = n // 128
                    for gidx in range(0, ncol, 8):
                        ng += 1
                        v.wait_ge(sem("tB"), base_tb + ng)
                        for jj in range(gidx, min(gidx + 8, ncol)):
                            cp = v.tensor_copy(dst[:, jj:jj + 1],
                                               psR[:, (jj % 8):(jj % 8) + 1])
                        inc(cp, "vB")

    barrier()

    stA = ExitStack()
    xa = [sbuf("xa0", [128, D], F32, stA), sbuf("xa1", [128, D], F32, stA)]
    xcf = sbuf("xcf", [128, D], F32, stA)
    xsq = sbuf("xsq", [128, D], F32, stA)
    xnb = [sbuf("xnb0", [128, D], BF16, stA), sbuf("xnb1", [128, D], BF16, stA)]
    x1b = [sbuf("x1b0", [128, D], BF16, stA), sbuf("x1b1", [128, D], BF16, stA)]
    smA = sbuf("smA", [128, 1], F32, stA)
    smB = sbuf("smB", [128, 1], F32, stA)
    smC = sbuf("smC", [128, 1], F32, stA)
    smD = sbuf("smD", [128, 1], F32, stA)
    smE = sbuf("smE", [128, 1], F32, stA)
    smF = sbuf("smF", [128, 1], F32, stA)
    smG = sbuf("smG", [128, 1], F32, stA)
    smH = sbuf("smH", [128, 1], F32, stA)
    smI = sbuf("smI", [128, 1], F32, stA)
    statT = sbuf("statT", [128, 128], F32, stA)

    with nc.Block() as blk:
        @blk.sync
        def _(sy):
            for tau in range(NTAU):
                if tau >= 2:
                    sy.wait_ge(sem("vA2"), tau - 1)
                inc(sy.dma_start(out=xa[tau % 2][:, :],
                                 in_=x_in[128 * tau:128 * (tau + 1), :]),
                    f"dX{tau % 2}", 16)

        @blk.vector
        def _(v):
            v = DW(v)
            for tau in range(NTAU):
                a = xa[tau % 2]
                v.wait_ge(sem(f"dX{tau % 2}"), 16 * (tau // 2 + 1))
                v.reduce_sum(smA[:, :], a[:, :], AX.X)
                v.tensor_scalar(smB[:, :], smA[:, :], 1.0 / D, None, ALU.mult)
                v.tensor_scalar(xcf[:, :], a[:, :], smB[:, 0:1], None, ALU.subtract)
                v.tensor_tensor(xsq[:, :], xcf[:, :], xcf[:, :], ALU.mult)
                inc(v.reduce_sum(smC[:, :], xsq[:, :], AX.X), "vA1")
                if tau > 0:
                    v.wait_ge(sem("tA2"), tau)
                    v.tensor_copy(
                        AP(arena1, 128 * (tau - 1), [[NCH * NT, 128], [NT, NCH], [1, 128]]),
                        psA[:, :])
                    inc(v.tensor_copy(
                        AP(xn1_fm, 128 * (tau - 1), [[NCH * NT, 128], [NT, NCH], [1, 128]]),
                        psB[:, :]), "vE")
                v.wait_ge(sem("sA2"), tau + 1)
                v.reciprocal(smF[:, :], smD[:, :])
                v.reciprocal(smG[:, :], smE[:, :])
                inc(v.tensor_scalar(xnb[tau % 2][:, :], xcf[:, :], smF[:, 0:1],
                                    None, ALU.mult), "vAxn")
                v.reduce_max(smH[:, :], xcf[:, :], AX.X, apply_absolute_value=True)
                v.tensor_tensor(smH[:, :], smH[:, :], smF[:, :], ALU.mult)
                v.tensor_tensor(rmaxcol[:, :], rmaxcol[:, :], smH[:, :], ALU.max)
                v.scalar_tensor_tensor(xsq[:, :], xcf[:, :], smG[:, 0:1],
                                       g1rep[:, :], ALU.mult, ALU.mult)
                v.tensor_copy(statT[:, tau:tau + 1], smB[:, :])
                v.tensor_copy(statT[:, 32 + tau:33 + tau], smF[:, :])
                v.tensor_copy(statT[:, 64 + tau:65 + tau], smG[:, :])
                v.tensor_copy(statT[:, 96 + tau:97 + tau], smC[:, :])
                inc(v.tensor_tensor(x1b[tau % 2][:, :], xsq[:, :], b1rep[:, :],
                                    ALU.add), "vA2")
            v.wait_ge(sem("tA2"), NTAU)
            v.tensor_copy(
                AP(arena1, 128 * (NTAU - 1), [[NCH * NT, 128], [NT, NCH], [1, 128]]),
                psA[:, :])
            inc(v.tensor_copy(
                AP(xn1_fm, 128 * (NTAU - 1), [[NCH * NT, 128], [NT, NCH], [1, 128]]),
                psB[:, :]), "vE")

        @blk.scalar
        def _(s):
            s = DW(s)
            for tau in range(NTAU):
                s.wait_ge(sem("vA1"), tau + 1)
                s.activation(smD[:, :], smC[:, :], ACT.Sqrt,
                             bias=kcol[:, 0:1], scale=1.0 / D)
                s.activation(smE[:, :], smC[:, :], ACT.Sqrt,
                             bias=kcol[:, 1:2], scale=(1.0 + EPS) / D)
                inc(s.activation(smI[:, :], smD[:, :], ACT.Copy), "sA2")

        @blk.tensor
        def _(t):
            for tau in range(NTAU):
                t.wait_ge(sem("vAxn"), tau + 1)
                if tau > 0:
                    t.wait_ge(sem("vE"), tau)
                for c in range(NCH):
                    t.transpose(psA[:, 128 * c:128 * (c + 1)],
                                xnb[tau % 2][:, 128 * c:128 * (c + 1)], idbf[:, :])
                t.wait_ge(sem("vA2"), tau + 1)
                for c in range(NCH):
                    tr = t.transpose(psB[:, 128 * c:128 * (c + 1)],
                                     x1b[tau % 2][:, 128 * c:128 * (c + 1)], idbf[:, :])
                inc(tr, "tA2")

    barrier()

    with nc.Block() as blk:
        @blk.sync
        def _(sy):
            for c in range(NCH):
                inc(sy.dma_start(out=xn_spill[128 * c:128 * (c + 1), :],
                                 in_=arena1[:, NT * c:NT * (c + 1)]), "dO", 16)
            inc(sy.dma_start(out=maxrt[0:1, :], in_=rmaxcol[:, 0:1]), "dO", 16)
            sy.wait_ge(sem("dO"), cget("dO"))
            inc(sy.dma_start(out=rowm[0:1, :], in_=maxrt[0:1, :]), "dA", 16)

        @blk.vector
        def _(v):
            v = DW(v)
            v.wait_ge(sem("dA"), cget("dA"))
            inc(v.reduce_max(ccrow[0:1, 0:1], rowm[0:1, :], AX.X), "vA")

        @blk.gpsimd
        def _(g):
            g = DW(g)
            g.wait_ge(sem("vA"), cget("vA"))
            inc(g.dma_start(out=cc1i[0:1, :], in_=ccrow[0:1, :]), "dW", 16)
            g.wait_ge(sem("dW"), cget("dW"))
            if "nocc" in DBG:
                inc(g.dma_start(out=cc1o[0:1, :], in_=cc1i[0:1, :]), "cc", 16)
            else:
                inc(g.collective_compute(
                    "AllReduce", ALU.max, replica_groups=[list(range(8))],
                    ins=[cc1i.ap().opt()], outs=[cc1o.ap().opt()]), "cc", 1)

    barrier()

    tp0 = tap("stats", [128, 128], F32)
    tp1 = tap("xn_fm", [128, NCH * NT], BF16)
    tp2 = tap("xn1_fm", [128, NCH * NT], BF16)
    tp3 = tap("gmax", [1, 128], F32)
    if any(t is not None for t in (tp0, tp1, tp2, tp3)):
        with nc.Block() as blk:
            @blk.sync
            def _(sy):
                if tp0 is not None:
                    inc(sy.dma_start(out=tp0[:, :], in_=statT[:, :]), "dO", 16)
                if tp1 is not None:
                    inc(sy.dma_start(out=tp1[:, :], in_=arena1[:, :]), "dO", 16)
                if tp2 is not None:
                    inc(sy.dma_start(out=tp2[:, :], in_=xn1_fm[:, :]), "dO", 16)
                if tp3 is not None:
                    sy.wait_ge(sem("cc"), cget("cc"))
                    inc(sy.dma_start(out=rowm[0:1, :], in_=cc1o[0:1, :]), "dA", 16)
                    sy.wait_ge(sem("dA"), cget("dA"))
                    inc(sy.dma_start(out=tp3[0:1, :], in_=rowm[0:1, :]), "dO", 16)
                sy.wait_ge(sem("dO"), cget("dO"))
        barrier()


    stA.close()
    stArena.close()
    st_early.close()

    stB = ExitStack()
    Wall = sbuf("Wall", [128, 8 * 640], BF16, stB)
    wstg = [sbuf("wstg0", [128, 640], F32, stB),
            sbuf("wstg1", [128, 640], F32, stB)]
    bufQ = sbuf("bufQ", [128, NT], BF16, stB)
    bufK = sbuf("bufK", [128, NT], BF16, stB)
    bufG = sbuf("bufG", [128, NT], BF16, stB)
    bufW1 = sbuf("bufW1", [128, NT], BF16, stB)
    bufW2 = sbuf("bufW2", [128, NT], BF16, stB)
    gcpT = sbuf("gcpT", [128, NT], BF16, stB)
    pexpT = sbuf("pexpT", [128, NT], BF16, stB)
    pcsT = sbuf("pcsT", [128, NT], BF16, stB)
    v0c = sbuf("v0c", [128, NT], BF16, stB)
    v1c = sbuf("v1c", [128, NT], BF16, stB)
    eSb01 = [sbuf("eSb0", [128, 512], BF16, stB),
             sbuf("eSb1", [128, 512], BF16, stB)]
    den2 = sbuf("den2", [128, 512], F32, stB)
    spB = sbuf("spB", [128, 1024], F32, stB)
    nrmB = sbuf("nrmB", [128, 64], F32, stB)
    rqk = sbuf("rqk", [128, 64], F32, stB)
    negpm = sbuf("negpm", [128, 2], F32, stB)
    offF = sbuf("offF", [1, 128], F32, stB)
    offB = sbuf("offB", [1, 128], BF16, stB)
    totF = sbuf("totF", [1, 128], F32, stB)
    cgf = sbuf("cgf", [128, 256], F32, stB)
    brepg = sbuf("brepg", [128, 128], F32, stB)
    clkw = sbuf("clkw", [128, 128], BF16, stB)
    clkstgF = sbuf("clkstgF", [128, 128], F32, stB)

    with nc.Block() as blk:
        @blk.sync
        def _(sy):
            for kc in range(NCH):
                inc(sy.dma_start(out=clkstgF[:, 16 * kc:16 * (kc + 1)],
                                 in_=wclk[128 * kc:128 * (kc + 1), :]), "dCLK", 16)

        @blk.vector
        def _(v):
            v = DW(v)
            v.wait_ge(sem("dCLK"), cget("dCLK"))
            inc(v.tensor_copy(clkw[:, :], clkstgF[:, :]), "vCLK")

    with nc.Block() as blk:
        @blk.tensor
        def _(t):
            t.wait_ge(sem("vCLK"), cget("vCLK"))

    steps = []
    for tau in range(NTAU):
        def t_clk(t, tau=tau):
            for kc in range(NCH):
                mm = t.matmul(psP[:, 0:16],
                              xn1_fm[:, kc * NT + 128 * tau:kc * NT + 128 * (tau + 1)],
                              clkw[:, 16 * kc:16 * (kc + 1)],
                              start=(kc == 0), stop=(kc == NCH - 1))
            return mm
        def v_clk(v, tau=tau):
            return v.tensor_tensor(clockS[:, 16 * tau:16 * (tau + 1)],
                                   psP[:, 0:16], brepc[:, :], ALU.add)
        steps.append(("tensor", t_clk))
        steps.append(("vector", v_clk))
    steps += [
        ("scalar", lambda s: s.activation(spB[:, 0:512], clockS[:, :], ACT.Abs)),
        ("scalar", lambda s: s.activation(den2[:, :], spB[:, 0:512], ACT.Exp,
                                          bias=0.0, scale=-1.0)),
        ("scalar", lambda s: s.activation(spB[:, 0:512], den2[:, :], ACT.Ln,
                                          bias=1.0, scale=1.0)),
        ("vector", lambda v: v.tensor_scalar(den2[:, :], clockS[:, :], 0.0, None,
                                             ALU.max)),
        ("vector", lambda v: v.tensor_tensor(clockS[:, :], den2[:, :],
                                             spB[:, 0:512], ALU.add)),
    ]
    chain(steps)
    barrier()

    def clock_ap(c):
        return AP(clockS, 2 * c, [[NTAU * H, 128], [H, NTAU], [1, 2], [0, 64]])

    COS_AP = AP(cosT2, 0, [[NTAU * 64, 128], [64, NTAU], [0, 2], [1, 64]])
    SIN_AP = AP(sinT, 0, [[NTAU * 64, 128], [64, NTAU], [0, 2], [1, 64]])
    RQK_AP = AP(rqk, 0, [[64, 128], [2, NTAU], [1, 2], [0, 64]])
    NRM_OUT = AP(nrmB, 0, [[64, 128], [2, NTAU], [1, 2]])

    def ev_even(buf):
        return AP(buf, 0, [[NT, 128], [128, NTAU], [64, 2], [2, 32]])

    def ev_odd(buf):
        return AP(buf, 1, [[NT, 128], [128, NTAU], [64, 2], [2, 32]])

    for c in range(NCH):
        wl = "wlB"
        with nc.Block() as blk:
            @blk.gpsimd
            def _(g):
                if c > 0:
                    g.wait_ge(sem("pp_tensor"), cget("pp_tensor"))
                if "oldwl" in DBG:
                    return
                for widx, w in enumerate([wq, wk, wg, wv, wp]):
                    inc(g.dma_start(
                        out=AP(Wall, 128 * widx,
                               [[8 * 640, 128], [640, 8], [1, 128]]),
                        in_=AP(w, 128 * c,
                               [[1024, 128], [128 * 1024, 8], [1, 128]])),
                        wl + "d", 16)

        if "oldwl" in DBG:
            with nc.Block() as blk:
                @blk.sync
                def _(sy):
                    for kc in range(NCH):
                        if kc >= 2:
                            sy.wait_ge(sem(wl + "v"), cget(wl + "v") - 1)
                        for widx, w in enumerate([wq, wk, wg, wv, wp]):
                            inc(sy.dma_start(
                                out=wstg[kc % 2][:, 128 * widx:128 * (widx + 1)],
                                in_=w[128 * kc:128 * (kc + 1),
                                      128 * c:128 * (c + 1)]),
                                wl + f"d{kc % 2}", 16)

                @blk.vector
                def _(v):
                    v = DW(v)
                    for kc in range(NCH):
                        v.wait_ge(sem(wl + f"d{kc % 2}"),
                                  cget(wl + f"d{kc % 2}") - 80 * ((NCH - 1 - kc) // 2))
                        inc(v.tensor_copy(Wall[:, 640 * kc:640 * (kc + 1)],
                                          wstg[kc % 2][:, :]), wl + "v")

        with nc.Block() as blk:
            @blk.tensor
            def _(t):
                if "oldwl" in DBG:
                    t.wait_ge(sem(wl + "v"), cget(wl + "v"))
                else:
                    t.wait_ge(sem(wl + "d"), cget(wl + "d"))
                inc(t.matmul(psR[:, 0:128], onesf[0:1, :],
                             growB[0:1, 128 * c:128 * (c + 1)],
                             start=True, stop=True), "tBG")

            @blk.vector
            def _(v):
                v = DW(v)
                v.wait_ge(sem("tBG"), cget("tBG"))
                v.tensor_copy(brepg[:, :], psR[:, 0:128])
                v.memset(v0c[:, :], 1.0)
                inc(v.memset(v1c[:, :], 1.0), "vBG")

        with nc.Block() as blk:
            @blk.tensor
            def _(t):
                t.wait_ge(sem("vBG"), cget("vBG"))

        steps1 = []
        for tau in range(NTAU):
            ps = psP if tau % 2 == 0 else psQ
            def t_proj(t, tau=tau, ps=ps):
                for kc in range(NCH):
                    mm = t.matmul(ps[:, 0:512],
                                  xn1_fm[:, kc * NT + 128 * tau:kc * NT + 128 * (tau + 1)],
                                  Wall[:, 640 * kc:640 * kc + 512],
                                  start=(kc == 0), stop=(kc == NCH - 1))
                return mm
            def v_ev(v, tau=tau, ps=ps):
                o = 128 * tau
                v.tensor_copy(bufQ[:, o:o + 128], ps[:, 0:128])
                v.tensor_copy(bufK[:, o:o + 128], ps[:, 128:256])
                v.tensor_tensor(bufG[:, o:o + 128], ps[:, 256:384], brepg[:, :],
                                ALU.add)
                v.tensor_copy(v0c[:, o:o + 64], ps[:, 384:448])
                return v.tensor_copy(v1c[:, o + 64:o + 128], ps[:, 448:512])
            ti = len(steps1)
            steps1.append(("tensor", t_proj,
                           [ti - 3] if tau >= 2 else []))
            steps1.append(("vector", v_ev, [ti]))
        chain2(steps1, name="pj")
        barrier()

        chain([
            ("scalar", lambda s: s.activation(bufW1[:, :], bufG[:, :], ACT.Abs)),
            ("scalar", lambda s: s.activation(bufW2[:, :], bufW1[:, :], ACT.Exp,
                                              bias=0.0, scale=-1.0)),
            ("scalar", lambda s: s.activation(bufW1[:, :], bufW2[:, :], ACT.Ln,
                                              bias=1.0, scale=1.0)),
            ("vector", lambda v: v.tensor_scalar(bufW2[:, :], bufG[:, :], 0.0,
                                                 None, ALU.max)),
            ("vector", lambda v: v.scalar_tensor_tensor(bufG[:, :], bufW2[:, :],
                                                        -1.0, bufW1[:, :],
                                                        ALU.mult, ALU.subtract)),
        ])

        def cumsum_chain(srcT, dstT, clip_exp, tag):
            steps = []
            v_idx = {}
            s_idx = {}
            it = 0
            for b in range(BPC):
                for tt in range(16):
                    tau = 16 * b + tt
                    ps = psP if tt % 2 == 0 else psQ
                    cg = cgf[:, 128 * (tt % 2):128 * (tt % 2 + 1)]
                    def t_cs(t, b=b, tt=tt, ps=ps):
                        for tp in range(tt + 1):
                            o = 128 * (16 * b + tp)
                            mm = t.matmul(ps[:, 0:128],
                                          utb[:, :] if tp == tt else onesb[:, :],
                                          srcT[:, o:o + 128],
                                          start=(tp == 0), stop=(tp == tt))
                        return mm
                    ti = len(steps)
                    steps.append(("tensor", t_cs,
                                  [v_idx[it - 2]] if it >= 2 else []))
                    if clip_exp:
                        def v_cs(v, ps=ps, cg=cg):
                            return v.tensor_scalar(cg, ps[:, 0:128], 50.0, -60.0,
                                                   ALU.min, ALU.max)
                        def s_cs(s, tau=tau, cg=cg):
                            o = 128 * tau
                            return s.activation(dstT[:, o:o + 128], cg, ACT.Exp)
                        vi = len(steps)
                        steps.append(("vector", v_cs,
                                      [ti] + ([s_idx[it - 2]] if it >= 2 else [])))
                        s_idx[it] = len(steps)
                        steps.append(("scalar", s_cs, [vi]))
                        v_idx[it] = vi
                    else:
                        def v_cs(v, tau=tau, ps=ps):
                            o = 128 * tau
                            return v.tensor_copy(dstT[:, o:o + 128], ps[:, 0:128])
                        v_idx[it] = len(steps)
                        steps.append(("vector", v_cs, [ti]))
                    it += 1
            chain2(steps, name=tag)
            barrier()

        barrier()
        cumsum_chain(bufG, gcpT, True, "cg")

        steps = []
        vproj = {}
        for m in range(8):
            ps = psP if m % 2 == 0 else psQ
            def t_p(t, m=m, ps=ps):
                for kc in range(NCH):
                    mm = t.matmul(ps[:, 0:512],
                                  Wall[:, 640 * kc + 512:640 * (kc + 1)],
                                  xn1_fm[:, kc * NT + 512 * m:kc * NT + 512 * (m + 1)],
                                  start=(kc == 0), stop=(kc == NCH - 1))
                return mm
            def v_p(v, m=m, ps=ps):
                return v.tensor_copy(bufW1[:, 512 * m:512 * (m + 1)], ps[:, 0:512])
            ti = len(steps)
            steps.append(("tensor", t_p, [vproj[m - 2]] if m >= 2 else []))
            steps.append(("vector", v_p, [ti]))
            vproj[m] = ti + 1
        spe = {}
        for b in range(BPC):
            def v_pm(v, b=b):
                return v.reduce_max(negpm[:, b:b + 1], bufW1[:, T * b:T * (b + 1)],
                                    AX.X, negate=True)
            def s_pe(s, b=b):
                return s.activation(bufW2[:, T * b:T * (b + 1)],
                                    bufW1[:, T * b:T * (b + 1)], ACT.Exp,
                                    bias=negpm[:, b:b + 1], scale=1.0)
            vi = len(steps)
            steps.append(("vector", v_pm, [vproj[4 * b + 3]]))
            steps.append(("scalar", s_pe, [vi]))
            spe[b] = vi + 1
        vtr = {}
        for g in range(8):
            bank = psA if g % 2 == 0 else psB
            def t_tr(t, g=g, bank=bank):
                for u in range(4):
                    tau = 4 * g + u
                    tr = t.transpose(bank[:, 128 * u:128 * (u + 1)],
                                     bufW2[:, 128 * tau:128 * (tau + 1)], idbf[:, :])
                return tr
            def v_tr(v, g=g, bank=bank):
                return v.tensor_copy(bufW1[:, 512 * g:512 * (g + 1)],
                                     bank[:, 0:512])
            ti = len(steps)
            deps = [spe[g // 4]]
            if g >= 2:
                deps.append(vtr[g - 2])
            steps.append(("tensor", t_tr, deps))
            steps.append(("vector", v_tr, [ti]))
            vtr[g] = ti + 1
        steps.append(("vector", lambda v: v.tensor_tensor(
            pexpT[:, :], bufW1[:, :], clock_ap(c), ALU.mult)), )
        steps[-1] = ("vector", steps[-1][0] if False else steps[-1][1], [vtr[7]])
        chain2(steps, name="pp")
        barrier()

        cumsum_chain(pexpT, pcsT, False, "cp")

        def qk_prep(bufX, wnum, wden):
            chain([
                ("vector", lambda v: v.tensor_tensor(bufW1[:, :], bufX[:, :],
                                                     bufX[:, :], ALU.mult)),
                ("vector", lambda v: v.reduce_sum(
                    NRM_OUT,
                    AP(bufW1, 0, [[NT, 128], [128, NTAU], [64, 2], [1, 64]]),
                    AX.X)),
                ("vector", lambda v: v.tensor_scalar(nrmB[:, :], nrmB[:, :],
                                                     1e-24, None, ALU.max)),
                ("scalar", lambda s: s.activation(den2[:, 0:64], nrmB[:, :],
                                                  ACT.Ln)),
                ("scalar", lambda s: s.activation(rqk[:, :], den2[:, 0:64],
                                                  ACT.Exp, bias=0.0, scale=-0.5)),
                ("vector", lambda v: v.tensor_tensor(bufW2[:, :], bufX[:, :],
                                                     RQK_AP, ALU.mult)),
                ("vector", lambda v: v.tensor_copy(ev_even(bufW1), ev_odd(bufW2))),
                ("vector", lambda v: v.tensor_copy(ev_odd(bufW1), ev_even(bufW2))),
                ("vector", lambda v: v.tensor_tensor(bufX[:, :], bufW1[:, :],
                                                     SIN_AP, ALU.mult)),
                ("vector", lambda v: v.tensor_tensor(bufW1[:, :], bufW2[:, :],
                                                     COS_AP, ALU.mult)),
                ("vector", lambda v: v.tensor_tensor(bufW2[:, :], bufX[:, :],
                                                     bufW1[:, :], ALU.add)),
                ("vector", lambda v: v.tensor_scalar(bufW1[:, :], wden[:, :],
                                                     1e-8, None, ALU.add)),
            ] + [
                it for m in range(4) for it in [
                    ("scalar", (lambda m: (lambda s: (
                        s.activation(spB[:, :], bufW1[:, 1024 * m:1024 * (m + 1)],
                                     ACT.Ln),
                        s.activation(den2[:, :],
                                     spB[:, 0:512], ACT.Exp, bias=0.0,
                                     scale=-1.0))[-1]))(m)),
                    ("scalar", (lambda m: (lambda s: s.activation(
                        spB[:, 0:512], spB[:, 512:1024], ACT.Exp, bias=0.0,
                        scale=-1.0))(m))),
                    ("vector", (lambda m: (lambda v: (
                        v.tensor_tensor(bufW1[:, 1024 * m:1024 * m + 512],
                                        wnum[:, 1024 * m:1024 * m + 512],
                                        den2[:, :], ALU.mult),
                        v.tensor_tensor(bufW1[:, 1024 * m + 512:1024 * (m + 1)],
                                        wnum[:, 1024 * m + 512:1024 * (m + 1)],
                                        spB[:, 0:512], ALU.mult))[-1]))(m)),
                ]
            ] + [
                ("vector", lambda v: v.tensor_tensor(bufX[:, :], bufW2[:, :],
                                                     bufW1[:, :], ALU.mult)),
            ])

        qk_prep(bufQ, gcpT, pcsT)
        barrier()
        qk_prep(bufK, pexpT, gcpT)
        barrier()

        steps = []
        vtr2 = {}
        for part in range(2):
            srcb = bufQ if part == 0 else bufK
            dstb = bufW1 if part == 0 else bufW2
            for g in range(8):
                gg = 8 * part + g
                bank = psA if gg % 2 == 0 else psB
                def t_t2(t, g=g, srcb=srcb, bank=bank):
                    for u in range(4):
                        tau = 4 * g + u
                        tr = t.transpose(bank[:, 128 * u:128 * (u + 1)],
                                         srcb[:, 128 * tau:128 * (tau + 1)],
                                         idbf[:, :])
                    return tr
                def v_t2(v, g=g, dstb=dstb, bank=bank):
                    return v.tensor_copy(dstb[:, 512 * g:512 * (g + 1)],
                                         bank[:, 0:512])
                ti = len(steps)
                steps.append(("tensor", t_t2, [vtr2[gg - 2]] if gg >= 2 else []))
                steps.append(("vector", v_t2, [ti]))
                vtr2[gg] = ti + 1
        chain2(steps, name="t2")
        barrier()

        qfm, kfm, outfm = bufW1, bufW2, bufG
        psS = [psS1, psS2]
        steps = []

        def add(eng, fn, deps):
            steps.append((eng, fn, [d for d in deps if d is not None]))
            return len(steps) - 1

        last_psS_rd = [None, None]
        last_pv_eS = [None, None]
        last_spB_rd = [None, None]
        last_vout = [None]
        last_vout_bank = [None, None]
        pending_den = [None]
        psOx = [psO, psP]
        mcount = 0

        for b in range(BPC):
            for hl in range(2):
                vtm = v0c if hl == 0 else v1c
                for m in range(4):
                    ob = mcount % 2
                    psOb = psOx[ob]
                    mcount += 1
                    qc0 = T * b + 512 * m
                    nj = 4 * m + 4
                    iex = {}

                    def mk_pv(j, b=b, nj=nj, vtm=vtm, psOb=psOb, ob=ob):
                        eS = eSb01[j % 2]
                        def t_pv(t, j=j, eS=eS):
                            return t.matmul(
                                psOb[:, 0:512],
                                vtm[:, 128 * (16 * b + j):128 * (16 * b + j + 1)],
                                eS[:, :],
                                start=(j == 0), stop=(j == nj - 1))
                        deps = [iex[j]]
                        if j == 0:
                            deps.append(last_vout_bank[ob])
                        ipv = add("tensor", t_pv, deps)
                        last_pv_eS[j % 2] = ipv
                        return ipv

                    for j in range(nj):
                        p = j % 2
                        hp = slice(512 * p, 512 * (p + 1))
                        eS = eSb01[p]

                        def t_qk(t, b=b, hl=hl, qc0=qc0, j=j, p=p):
                            return t.matmul(
                                psS[p][:, 0:512],
                                kfm[64 * hl:64 * hl + 64,
                                    T * b + 128 * j:T * b + 128 * (j + 1)],
                                qfm[64 * hl:64 * hl + 64, qc0:qc0 + 512],
                                start=True, stop=True)
                        iqk = add("tensor", t_qk, [last_psS_rd[p]])

                        if j >= 4 * m:
                            def v_cm(v, p=p, hp=hp, mv=j - 4 * m):
                                return v.scalar_tensor_tensor(
                                    spB[:, hp], psS[p][:, 0:512], 500.0,
                                    maskadd[:, 512 * mv:512 * (mv + 1)],
                                    ALU.min, ALU.add)
                            icm = add("vector", v_cm, [iqk, last_spB_rd[p]])
                            last_psS_rd[p] = icm

                            def s_ex(s, eS=eS, hp=hp):
                                return s.activation(eS[:, :], spB[:, hp],
                                                    ACT.Exp, bias=0.0,
                                                    scale=0.125)
                            iex[j] = add("scalar", s_ex,
                                         [icm, last_pv_eS[p]])
                            last_spB_rd[p] = iex[j]
                        else:
                            def s_ex(s, eS=eS, p=p):
                                return s.activation(eS[:, :], psS[p][:, 0:512],
                                                    ACT.Exp, bias=0.0,
                                                    scale=0.125)
                            iex[j] = add("scalar", s_ex,
                                         [iqk, last_pv_eS[p]])
                            last_psS_rd[p] = iex[j]

                        if j == 1 and pending_den[0] is not None:
                            pending_den[0]()
                            pending_den[0] = None
                        if j >= 1:
                            ipv = mk_pv(j - 1)
                    ipv_last = mk_pv(nj - 1)

                    def den_emit(hl=hl, qc0=qc0, ipv_last=ipv_last,
                                 psOb=psOb, ob=ob):
                        global_last = last_vout[0]
                        if hl == 0:
                            def s_rc(s):
                                s.activation(den2[64:65, :], psOb[64:65, 0:512],
                                             ACT.Ln)
                                return s.activation(den2[64:65, :],
                                                    den2[64:65, :],
                                                    ACT.Exp, bias=0.0,
                                                    scale=-1.0)
                            def t_bc(t):
                                return t.matmul(psR[0:64, :],
                                                onesf[64:65, 0:64],
                                                den2[64:65, :], start=True,
                                                stop=True,
                                                tile_position=(64, 0))
                            def v_out(v, qc0=qc0, psOb=psOb):
                                v.tensor_copy(den2[0:64, 0:512], psR[0:64, :])
                                return v.tensor_tensor(
                                    outfm[0:64, qc0:qc0 + 512],
                                    psOb[0:64, 0:512],
                                    den2[0:64, 0:512], ALU.mult)
                        else:
                            def s_rc(s):
                                s.activation(den2[0:1, :], psOb[0:1, 0:512],
                                             ACT.Ln)
                                return s.activation(den2[0:1, :], den2[0:1, :],
                                                    ACT.Exp, bias=0.0,
                                                    scale=-1.0)
                            def t_bc(t):
                                return t.matmul(psR[64:128, :],
                                                onesf[0:1, 0:64],
                                                den2[0:1, :], start=True,
                                                stop=True,
                                                tile_position=(0, 64))
                            def v_out(v, qc0=qc0, psOb=psOb):
                                v.tensor_copy(den2[64:128, 0:512],
                                              psR[64:128, :])
                                return v.tensor_tensor(
                                    outfm[64:128, qc0:qc0 + 512],
                                    psOb[64:128, 0:512],
                                    den2[64:128, 0:512], ALU.mult)
                        irc = add("scalar", s_rc, [ipv_last])
                        itb = add("tensor", t_bc, [irc, global_last])
                        ivo = add("vector", v_out, [itb])
                        last_vout[0] = ivo
                        last_vout_bank[ob] = ivo
                        return ivo

                    pending_den[0] = den_emit
        if pending_den[0] is not None:
            pending_den[0]()
            pending_den[0] = None
        chain2(steps, name="at")

        with nc.Block() as blk:
            @blk.sync
            def _(sy):
                sy.wait_ge(sem("at_vector"), cget("at_vector"))
                inc(sy.dma_start(out=out_spill[128 * c:128 * (c + 1), :],
                                 in_=outfm[:, :]), "dO", 16)
                sy.wait_ge(sem("dO"), cget("dO"))

        if c == 0:
            tb = {nm: tap(nm, [128, NT], BF16)
                  for nm in ("qfm0", "kfm0", "gcp0", "pexp0", "pcs0", "outfm0",
                             "v0c0")}
            srcs = {"qfm0": bufW1, "kfm0": bufW2, "gcp0": gcpT, "pexp0": pexpT,
                    "pcs0": pcsT, "outfm0": bufG, "v0c0": v0c}
            if any(t_ is not None for t_ in tb.values()):
                with nc.Block() as blk:
                    @blk.sync
                    def _(sy):
                        sy.wait_ge(sem("at_vector"), cget("at_vector"))
                        for nm, t_ in tb.items():
                            if t_ is not None:
                                inc(sy.dma_start(out=t_[:, :], in_=srcs[nm][:, :]),
                                    "dO", 16)
                        sy.wait_ge(sem("dO"), cget("dO"))
        barrier()

    stB.close()



    if "stopB" in DBG:
        stZ2 = ExitStack()
        zb2 = sbuf("zb2", [128, D], F32, stZ2)
        with nc.Block() as blk:
            @blk.vector
            def _(v):
                inc(v.memset(zb2[:, :], 0.0), "vZ2")

            @blk.sync
            def _(sy):
                sy.wait_ge(sem("vZ2"), cget("vZ2"))
                for tau in range(NTAU):
                    inc(sy.dma_start(out=y_out[128 * tau:128 * (tau + 1), :],
                                     in_=zb2[:, :]), "dO", 16)
                sy.wait_ge(sem("dO"), cget("dO"))
        stZ2.close()
        stB.close()
        stTab.close()
        stXN1.close()
        stack.close()
        return nc, taps

    stP = ExitStack()
    stgF = [sbuf("stgF0", [128, MH], F32, stP), sbuf("stgF1", [128, IFF], F32, stP)]
    sA_ = sbuf("sA_", [128, IFF], F32, stP)
    sB_ = sbuf("sB_", [128, IFF], F32, stP)
    sI_ = sbuf("sI_", [128, IFF], I32, stP)
    sG_ = sbuf("sG_", [128, IFF], BF16, stP)
    wI_ = sbuf("wI_", [128, IFF], BF16, stP)
    wT_ = sbuf("wT_", [128, IFF], BF16, stP)
    rmx = sbuf("rmx", [128, 1], F32, stP)

    barrier()

    wT2_ = sbuf("wT2_", [128, IFF], BF16, stP)
    wTd = [wT_, wT2_]

    qw_specs = ([(wgate, rc, D, scg, wgT_sp) for rc in range(NIC)] +
                [(wup, rc, D, scu, wuT_sp) for rc in range(NIC)] +
                [(wdown, rc, IFF, scd, wdT_sp) for rc in range(NCH)])
    steps = []

    def addq(eng, fn, deps):
        steps.append((eng, fn, [d for d in deps if d is not None]))
        return len(steps) - 1

    i_in, i_sc, i_ab, i_rd, i_out = {}, {}, {}, {}, {}
    i_ttr, i_vtr = {}, {}
    last_vtr_bank = [None, None]
    gctr = 0
    for it, (wsrc, rc, ncol, sccol, dst_sp) in enumerate(qw_specs):
        stg = stgF[it % 2]
        wT = wTd[it % 2]
        ncc = ncol // 128

        def g_in(g, wsrc=wsrc, rc=rc, ncol=ncol, stg=stg):
            inc(g.dma_start(out=stg[:, 0:ncol],
                            in_=wsrc[128 * rc:128 * (rc + 1), :]), "dQW", 16)
            g.wait_ge(sem("dQW"), cget("dQW"))
            return g.nop()
        i_in[it] = addq("gpsimd", g_in, [i_sc.get(it - 2)])

        def v_scale(v, rc=rc, ncol=ncol, sccol=sccol, stg=stg):
            v.reduce_max(rmx[:, :], stg[:, 0:ncol], AX.X,
                         apply_absolute_value=True)
            v.tensor_scalar(sccol[:, rc:rc + 1], rmx[:, :], 1.0 / 127.0,
                            None, ALU.mult)
            v.reciprocal(rmx[:, :], rmx[:, :])
            v.tensor_scalar(rmx[:, :], rmx[:, :], 127.0, None, ALU.mult)
            return v.tensor_scalar(sA_[:, 0:ncol], stg[:, 0:ncol],
                                   rmx[:, 0:1], 128.5, ALU.mult, ALU.add)
        i_sc[it] = addq("vector", v_scale, [i_in[it]])

        def v_round(v, ncol=ncol):
            v.tensor_copy(sI_[:, 0:ncol], sA_[:, 0:ncol])
            v.scalar_tensor_tensor(sB_[:, 0:ncol], sI_[:, 0:ncol], 1.0,
                                   sA_[:, 0:ncol], ALU.mult, ALU.is_le)
            v.tensor_tensor(sA_[:, 0:ncol], sI_[:, 0:ncol], sB_[:, 0:ncol],
                            ALU.add)
            return v.tensor_scalar(wI_[:, 0:ncol], sA_[:, 0:ncol], -129.0,
                                   None, ALU.add)
        i_rd[it] = addq("vector", v_round,
                        [i_sc[it], i_ttr.get((it - 1, -1))])

        ngrp = (ncc + 3) // 4
        for gi, g0 in enumerate(range(0, ncc, 4)):
            gw = min(4, ncc - g0)
            bank = psA if gctr % 2 == 0 else psB

            def t_tr(t, g0=g0, gw=gw, bank=bank):
                for u in range(gw):
                    tr = t.transpose(bank[:, 128 * u:128 * (u + 1)],
                                     wI_[:, 128 * (g0 + u):128 * (g0 + u + 1)],
                                     idbf[:, :])
                return tr
            i_ttr[(it, gi)] = addq("tensor", t_tr,
                                   [i_rd[it], last_vtr_bank[gctr % 2]])

            def v_tr(s, g0=g0, gw=gw, bank=bank, wT=wT):
                return s.activation(wT[:, 128 * g0:128 * (g0 + gw)],
                                    bank[:, 0:128 * gw], ACT.Copy)
            deps = [i_ttr[(it, gi)]]
            if gi == 0:
                deps.append(i_out.get(it - 2))
            i_vtr[(it, gi)] = addq("scalar", v_tr, deps)
            last_vtr_bank[gctr % 2] = i_vtr[(it, gi)]
            gctr += 1
        i_ttr[(it, -1)] = i_ttr[(it, ngrp - 1)]

        def sy_out(sy, rc=rc, ncc=ncc, dst_sp=dst_sp, wT=wT):
            for cc in range(ncc):
                inc(sy.dma_start(
                    out=dst_sp[128 * cc:128 * (cc + 1),
                               128 * rc:128 * (rc + 1)],
                    in_=wT[:, 128 * cc:128 * (cc + 1)]), "dQWo", 16)
            sy.wait_ge(sem("dQWo"), cget("dQWo"))
            return sy.nop()
        i_out[it] = addq("sync", sy_out, [i_vtr[(it, ngrp - 1)]])
    chain2(steps, name="qw")
    barrier()
    stP.close()
    stTab.close()
    stXN1.close()

    stC1 = ExitStack()
    x1fm = sbuf("x1fm", [128, NCH * NT], BF16, stC1)
    stC2 = ExitStack()
    outR = sbuf("outR", [128, NCH * NT], BF16, stC2)
    woS = sbuf("woS", [128, NCH * 1024], BF16, stC2)
    wstgC = sbuf("wstgC", [128, 1024], F32, stC2)
    xnstg = sbuf("xnstg", [128, NT], BF16, stC2)
    spC = sbuf("spC", [128, 512], F32, stC2)

    with nc.Block() as blk:
        @blk.sync
        def _(sy):
            for cc in range(NCH):
                inc(sy.dma_start(out=outR[:, NT * cc:NT * (cc + 1)],
                                 in_=out_spill[128 * cc:128 * (cc + 1), :]),
                    "dC1", 16)

        @blk.gpsimd
        def _(g):
            for kc in range(NCH):
                inc(g.dma_start(out=woS[:, 1024 * kc:1024 * (kc + 1)],
                                in_=wo[128 * kc:128 * (kc + 1), :]), "dC2", 16)

    with nc.Block() as blk:
        @blk.sync
        def _(sy):
            sy.wait_ge(sem("dC1"), cget("dC1"))
            for oc in range(NCH):
                if oc >= 1:
                    sy.wait_ge(sem("vC2"), oc)
                inc(sy.dma_start(out=xnstg[:, :],
                                 in_=xn_spill[128 * oc:128 * (oc + 1), :]),
                    "dC3", 16)

        @blk.tensor
        def _(t):
            t.wait_ge(sem("dC2"), cget("dC2"))
            t.wait_ge(sem("dC1"), cget("dC1"))
            for oc in range(NCH):
                for m in range(8):
                    idx = 8 * oc + m
                    if idx >= 2:
                        t.wait_ge(sem("vC2x"), idx - 1)
                    ps = psP if m % 2 == 0 else psQ
                    for kc in range(NCH):
                        mm = t.matmul(
                            ps[:, 0:512],
                            woS[:, 1024 * kc + 128 * oc:1024 * kc + 128 * (oc + 1)],
                            outR[:, NT * kc + 512 * m:NT * kc + 512 * (m + 1)],
                            start=(kc == 0), stop=(kc == NCH - 1))
                    inc(mm, "tC1")

        @blk.vector
        def _(v):
            v = DW(v)
            for oc in range(NCH):
                v.wait_ge(sem("dC3"), 16 * (oc + 1))
                for m in range(8):
                    idx = 8 * oc + m
                    v.wait_ge(sem("tC1"), idx + 1)
                    ps = psP if m % 2 == 0 else psQ
                    v.tensor_scalar(spC[:, 0:512], ps[:, 0:512],
                                    bocol[:, oc:oc + 1], None, ALU.add)
                    last = v.tensor_tensor(
                        x1fm[:, NT * oc + 512 * m:NT * oc + 512 * (m + 1)],
                        spC[:, 0:512], xnstg[:, 512 * m:512 * (m + 1)], ALU.add)
                    inc(last, "vC2x")
                inc(v.memset(spC[0:1, 0:1], 0.0), "vC2")
    barrier()
    stC2.close()

    stC3 = ExitStack()
    w2S = sbuf("w2S", [128, NMH * 1024], BF16, stC3)
    w1blk = sbuf("w1blk", [128, 2048], BF16, stC3)
    hact = sbuf("hact", [128, NMH * 512], BF16, stC3)
    h0m = sbuf("h0m", [128, NCH * 512], BF16, stC3)
    sqm = sbuf("sqm", [128, NCH * 512], BF16, stC3)
    sm1 = sbuf("sm1", [128, 512], F32, stC3)
    sm2 = sbuf("sm2", [128, 512], F32, stC3)
    sm3 = sbuf("sm3", [128, 512], F32, stC3)
    wstgC2 = sbuf("wstgC2", [128, 1024], F32, stC3)
    sOut = sbuf("sOut", [128, 1024], BF16, stC3)

    with nc.Block() as blk:
        @blk.sync
        def _(sy):
            for hc in range(NMH):
                if hc >= 1:
                    sy.wait_ge(sem("vC3"), hc)
                inc(sy.dma_start(out=wstgC2[:, :],
                                 in_=w2[128 * hc:128 * (hc + 1), :]), "dC4", 16)

        @blk.vector
        def _(v):
            v = DW(v)
            for hc in range(NMH):
                v.wait_ge(sem("dC4"), 16 * (hc + 1))
                inc(v.tensor_copy(w2S[:, 1024 * hc:1024 * (hc + 1)],
                                  wstgC2[:, :]), "vC3")

    for m in range(8):
        steps = []

        def add(eng, fn, deps):
            steps.append((eng, fn, [d for d in deps if d is not None]))
            return len(steps) - 1

        def t_mu(t, m=m):
            for kc in range(NCH):
                mm = t.matmul(psQ[0:1, 0:512], onesb[:, 0:1],
                              x1fm[:, NT * kc + 512 * m:NT * kc + 512 * (m + 1)],
                              start=(kc == 0), stop=(kc == NCH - 1))
            return mm
        i_tmu = add("tensor", t_mu, [])

        def v_sq(v, m=m):
            for kc in range(NCH):
                cp = v.tensor_tensor(
                    sqm[:, 512 * kc:512 * (kc + 1)],
                    x1fm[:, NT * kc + 512 * m:NT * kc + 512 * (m + 1)],
                    x1fm[:, NT * kc + 512 * m:NT * kc + 512 * (m + 1)], ALU.mult)
            return cp
        i_vsq = add("vector", v_sq, [])

        def t_sq(t, m=m):
            for kc in range(NCH):
                mm = t.matmul(psR[0:1, 0:512], onesb[:, 0:1],
                              sqm[:, 512 * kc:512 * (kc + 1)],
                              start=(kc == 0), stop=(kc == NCH - 1))
            return mm
        i_tsq = add("tensor", t_sq, [i_vsq])

        def v_stats(v, m=m):
            v.tensor_scalar(sm1[0:1, :], psQ[0:1, 0:512], 1.0 / D, None, ALU.mult)
            v.tensor_scalar(sm2[0:1, :], psR[0:1, 0:512], 1.0 / D, None, ALU.mult)
            v.tensor_tensor(sm3[0:1, :], sm1[0:1, :], sm1[0:1, :], ALU.mult)
            return v.tensor_tensor(sm2[0:1, :], sm2[0:1, :], sm3[0:1, :],
                                   ALU.subtract)
        i_vst = add("vector", v_stats, [i_tmu, i_tsq])
        i_srs = add("scalar",
                    lambda s: s.activation(sm3[0:1, :], sm2[0:1, :], ACT.Sqrt,
                                           bias=kcol[0:1, 0:1], scale=1.0),
                    [i_vst])
        i_vrs = add("vector", lambda v: v.reciprocal(sm2[0:1, :], sm3[0:1, :]),
                    [i_srs])

        def t_bc(t, m=m):
            t.matmul(psQ[:, 0:512], onesf[0:1, :], sm1[0:1, :],
                     start=True, stop=True)
            return t.matmul(psR[:, 0:512], onesf[0:1, :], sm2[0:1, :],
                            start=True, stop=True)
        i_tbc = add("tensor", t_bc, [i_vrs])

        def v_bc(v, m=m):
            v.tensor_copy(sm1[:, :], psQ[:, 0:512])
            return v.tensor_copy(sm2[:, :], psR[:, 0:512])
        i_vbc = add("vector", v_bc, [i_tbc])

        def v_nrm(v, m=m):
            for kc in range(NCH):
                v.tensor_tensor(sm3[:, :],
                                x1fm[:, NT * kc + 512 * m:NT * kc + 512 * (m + 1)],
                                sm1[:, :], ALU.subtract)
                v.tensor_tensor(sm3[:, :], sm3[:, :], sm2[:, :], ALU.mult)
                cp = v.tensor_scalar(h0m[:, 512 * kc:512 * (kc + 1)], sm3[:, :],
                                     g2col[:, kc:kc + 1], b2col[:, kc:kc + 1],
                                     ALU.mult, ALU.add)
            return cp
        i_vnrm = add("vector", v_nrm, [i_vbc])

        i_sy = {}
        i_t1 = {}
        i_s1 = {}
        for hc in range(NMH):
            wslc = slice(1024 * (hc % 2), 1024 * (hc % 2 + 1))

            def sy_w1(sy, m=m, hc=hc, wslc=wslc):
                if hc == 0:
                    sy.wait_ge(sem("dW1c"), 128)
                inc(sy.dma_start(
                    out=w1blk[:, wslc],
                    in_=AP(w1bf, hc * NCH * 128 * 128,
                           [[128, 128], [128 * 128, NCH], [1, 128]])), "dW1b", 16)
                sy.wait_ge(sem("dW1b"), cget("dW1b"))
                return sy.nop()
            i_sy[hc] = add("sync", sy_w1, [i_t1.get(hc - 2)])

            def t_m1(t, m=m, hc=hc, wslc=wslc):
                ps = psP if hc % 2 == 0 else psQ
                for kc in range(NCH):
                    mm = t.matmul(ps[:, 0:512],
                                  w1blk[:, wslc][:, 128 * kc:128 * (kc + 1)],
                                  h0m[:, 512 * kc:512 * (kc + 1)],
                                  start=(kc == 0), stop=(kc == NCH - 1))
                return mm
            i_t1[hc] = add("tensor", t_m1,
                           [i_sy[hc], i_vnrm, i_s1.get(hc - 2)])

            def s_m1(s, m=m, hc=hc):
                ps = psP if hc % 2 == 0 else psQ
                return s.activation(hact[:, 512 * hc:512 * (hc + 1)],
                                    ps[:, 0:512], ACT.Gelu,
                                    bias=bm1col[:, hc:hc + 1], scale=1.0)
            i_s1[hc] = add("scalar", s_m1, [i_t1[hc]])

        i_t2 = {}
        i_v2 = {}
        i_sp = {}
        for oc in range(NCH):
            oslc = slice(512 * (oc % 2), 512 * (oc % 2 + 1))

            def t_m2(t, m=m, oc=oc):
                ps = psS1 if oc % 2 == 0 else psS2
                for hc in range(NMH):
                    mm = t.matmul(
                        ps[:, 0:512],
                        w2S[:, 1024 * hc + 128 * oc:1024 * hc + 128 * (oc + 1)],
                        hact[:, 512 * hc:512 * (hc + 1)],
                        start=(hc == 0), stop=(hc == NMH - 1))
                return mm
            i_t2[oc] = add("tensor", t_m2,
                           [i_s1[NMH - 1], i_v2.get(oc - 2)])

            def v_m2(v, m=m, oc=oc, oslc=oslc):
                ps = psS1 if oc % 2 == 0 else psS2
                v.tensor_scalar(sm3[:, :], ps[:, 0:512], bm2col[:, oc:oc + 1],
                                None, ALU.add)
                return v.tensor_tensor(
                    sOut[:, oslc], sm3[:, :],
                    x1fm[:, NT * oc + 512 * m:NT * oc + 512 * (m + 1)], ALU.add)
            i_v2[oc] = add("vector", v_m2, [i_t2[oc], i_sp.get(oc - 2)])

            def sy_sp(sy, m=m, oc=oc, oslc=oslc):
                inc(sy.dma_start(out=s_spill[128 * oc:128 * (oc + 1),
                                             512 * m:512 * (m + 1)],
                                 in_=sOut[:, oslc]), "dSo", 16)
                sy.wait_ge(sem("dSo"), cget("dSo"))
                return sy.nop()
            i_sp[oc] = add("sync", sy_sp, [i_v2[oc]])
        chain2(steps, name="c2m")
        barrier()
    barrier()
    stC3.close()

    tcx = tap("x1fm", [128, NCH * NT], BF16)
    if tcx is not None:
        with nc.Block() as blk:
            @blk.sync
            def _(sy):
                inc(sy.dma_start(out=tcx[:, :], in_=x1fm[:, :]), "dO", 16)
                sy.wait_ge(sem("dO"), cget("dO"))
    stC1.close()

    stD = ExitStack()
    xqA = sbuf("xqA", [128, NCH * NT], BF16, stD)
    stD1 = ExitStack()
    wgS = sbuf("wgS", [128, NCH * IFF], BF16, stD1)
    wuS = sbuf("wuS", [128, NCH * IFF], BF16, stD1)
    xnstg2 = sbuf("xnstg2", [128, 2 * NT], BF16, stD1)
    dq1 = sbuf("dq1", [128, 1024], F32, stD1)
    dq2 = sbuf("dq2", [128, 1024], F32, stD1)
    dqI = sbuf("dqI", [128, 1024], I32, stD1)
    dqS = sbuf("dqS", [128, 512], BF16, stD1)
    svg = sbuf("svg", [128, NIC], F32, stD1)
    svu = sbuf("svu", [128, NIC], F32, stD1)
    clpS = sbuf("clpS", [1, 2], F32, stD1)
    midT = sbuf("midT", [128, 1024], BF16, stD1)
    sgT = sbuf("sgT", [128, 1024], BF16, stD1)
    dmx = sbuf("dmx", [128, 1], F32, stD1)

    with nc.Block() as blk:
        @blk.sync
        def _(sy):
            sy.wait_ge(sem("cc"), cget("cc"))
            inc(sy.dma_start(out=rowm[0:1, :], in_=cc1o[0:1, :]), "dD0", 16)
            inc(sy.dma_start(out=clpS[0:1, 0:1], in_=clipx.ap()), "dD0", 16)
            inc(sy.dma_start(out=clpS[0:1, 1:2], in_=clipd.ap()), "dD0", 16)

        @blk.scalar
        def _(s):
            s = DW(s)
            s.wait_ge(sem("dD0"), cget("dD0"))
            inc(s.activation(clpS[0:1, 0:1], clpS[0:1, 0:1], ACT.Exp,
                             bias=0.0, scale=-1.0), "sD0")

        @blk.vector
        def _(v):
            v = DW(v)
            v.wait_ge(sem("sD0"), cget("sD0"))
            v.tensor_scalar(clpS[0:1, 0:1], clpS[0:1, 0:1], 1.0, None, ALU.add)
            v.reciprocal(clpS[0:1, 0:1], clpS[0:1, 0:1])
            v.tensor_tensor(ccrow[0:1, 1:2], clpS[0:1, 0:1], rowm[0:1, 0:1],
                            ALU.mult)
            v.tensor_scalar(ccrow[0:1, 1:2], ccrow[0:1, 1:2], 1.0 / 127.0, None,
                            ALU.mult)
            inc(v.reciprocal(ccrow[0:1, 2:3], ccrow[0:1, 1:2]), "vD0")

        @blk.tensor
        def _(t):
            t.wait_ge(sem("vD0"), cget("vD0"))
            t.matmul(psR[:, 0:1], onesf[0:1, :], ccrow[0:1, 1:2],
                     start=True, stop=True)
            inc(t.matmul(psR[:, 1:2], onesf[0:1, :], ccrow[0:1, 2:3],
                         start=True, stop=True), "tD0")

    with nc.Block() as blk:
        @blk.vector
        def _(v):
            v = DW(v)
            v.wait_ge(sem("tD0"), cget("tD0"))
            v.tensor_copy(sxv[:, :], psR[:, 0:1])
            v.tensor_copy(rsxv[:, :], psR[:, 1:2])
            v.tensor_scalar(svg[:, :], scg[:, :], sxv[:, 0:1], None, ALU.mult)
            inc(v.tensor_scalar(svu[:, :], scu[:, :], sxv[:, 0:1], None,
                                ALU.mult), "vD1")

        @blk.gpsimd
        def _(g):
            for kc in range(NCH):
                inc(g.dma_start(out=wgS[:, IFF * kc:IFF * (kc + 1)],
                                in_=wgT_sp[128 * kc:128 * (kc + 1), :]), "dDg", 16)
                inc(g.dma_start(out=wuS[:, IFF * kc:IFF * (kc + 1)],
                                in_=wuT_sp[128 * kc:128 * (kc + 1), :]), "dDg", 16)

    def quant_tile(v, srcap, dstap, rvec, do_clip):
        v.tensor_scalar(dq1[:, :], srcap, rvec, None, ALU.mult)
        return None

    steps = []
    i_d1in, i_d1v = {}, {}

    def add1(eng, fn, deps):
        steps.append((eng, fn, [d for d in deps if d is not None]))
        return len(steps) - 1

    for cc in range(NCH):
        st0 = NT * (cc % 2)

        def sy_xn(sy, cc=cc, st0=st0):
            inc(sy.dma_start(out=xnstg2[:, st0:st0 + NT],
                             in_=xn_spill[128 * cc:128 * (cc + 1), :]),
                "dD1", 16)
            sy.wait_ge(sem("dD1"), cget("dD1"))
            return sy.nop()
        i_d1in[cc] = add1("sync", sy_xn, [i_d1v.get(cc - 2)])

        def v_q(v, cc=cc, st0=st0):
            for mp in range(4):
                v.tensor_scalar(dq1[:, :],
                                xnstg2[:, st0 + 1024 * mp:st0 + 1024 * (mp + 1)],
                                rsxv[:, 0:1], 128.5, ALU.mult, ALU.add)
                v.tensor_scalar(dq1[:, :], dq1[:, :], 0.0, None, ALU.max)
                v.tensor_copy(dqI[:, :], dq1[:, :])
                v.scalar_tensor_tensor(dq2[:, :], dqI[:, :], 1.0, dq1[:, :],
                                       ALU.mult, ALU.is_le)
                v.tensor_tensor(dq1[:, :], dqI[:, :], dq2[:, :], ALU.add)
                last = v.tensor_scalar(
                    xqA[:, NT * cc + 1024 * mp:NT * cc + 1024 * (mp + 1)],
                    dq1[:, :], -129.0, 127.0, ALU.add, ALU.min)
            return last
        i_d1v[cc] = add1("vector", v_q, [i_d1in[cc]])
    with nc.Block() as blk:
        @blk.vector
        def _(v):
            v.wait_ge(sem("vD1"), cget("vD1"))
        @blk.sync
        def _(sy):
            sy.wait_ge(sem("dDg"), cget("dDg"))
    chain2(steps, name="d1")

    with nc.Block() as blk:
        @blk.tensor
        def _(t):
            t.wait_ge(sem("dDg"), cget("dDg"))
    barrier()

    steps = []

    def addd(eng, fn, deps):
        steps.append((eng, fn, [d for d in deps if d is not None]))
        return len(steps) - 1

    i_t = {}
    i_s = {}
    i_v = {}
    i_y = {}
    it = 0
    for m in range(8):
        for ic in range(NIC):
            pg = psP if it % 2 == 0 else psS1
            pu = psQ if it % 2 == 0 else psS2
            hp = slice(512 * (it % 2), 512 * (it % 2 + 1))

            def t_gu(t, m=m, ic=ic, pg=pg, pu=pu):
                for kc in range(NCH):
                    t.matmul(pg[:, 0:512],
                             wgS[:, IFF * kc + 128 * ic:IFF * kc + 128 * (ic + 1)],
                             xqA[:, NT * kc + 512 * m:NT * kc + 512 * (m + 1)],
                             start=(kc == 0), stop=(kc == NCH - 1))
                for kc in range(NCH):
                    mm = t.matmul(
                        pu[:, 0:512],
                        wuS[:, IFF * kc + 128 * ic:IFF * kc + 128 * (ic + 1)],
                        xqA[:, NT * kc + 512 * m:NT * kc + 512 * (m + 1)],
                        start=(kc == 0), stop=(kc == NCH - 1))
                return mm
            i_t[it] = addd("tensor", t_gu, [i_s.get(it - 2), i_v.get(it - 2)])

            def s_gu(s, m=m, ic=ic, pg=pg, hp=hp):
                return s.activation(sgT[:, hp], pg[:, 0:512], ACT.Silu,
                                    bias=0.0, scale=svg[:, ic:ic + 1])
            i_s[it] = addd("scalar", s_gu, [i_t[it], i_v.get(it - 2)])

            def v_gu(v, m=m, ic=ic, pu=pu, hp=hp):
                v.scalar_tensor_tensor(midT[:, hp], pu[:, 0:512],
                                       svu[:, ic:ic + 1], sgT[:, hp],
                                       ALU.mult, ALU.mult)
                v.reduce_max(dmx[:, :], midT[:, hp], AX.X,
                             apply_absolute_value=True)
                return v.tensor_tensor(rmaxcol2[:, :], rmaxcol2[:, :], dmx[:, :],
                                       ALU.max)
            i_v[it] = addd("vector", v_gu, [i_s[it], i_y.get(it - 2)])

            def sy_gu(sy, m=m, ic=ic, hp=hp):
                inc(sy.dma_start(out=mid_spill[128 * ic:128 * (ic + 1),
                                               512 * m:512 * (m + 1)],
                                 in_=midT[:, hp]), "dMo", 16)
                sy.wait_ge(sem("dMo"), cget("dMo"))
                return sy.nop()
            i_y[it] = addd("sync", sy_gu, [i_v[it]])
            it += 1
    chain2(steps, name="d2")
    barrier()
    stD1.close()

    stD2 = ExitStack()
    wdS = sbuf("wdS", [128, NIC * 1024], BF16, stD2)
    mqA = sbuf("mqA", [128, 2 * NIC * 512], BF16, stD2)
    mres = sbuf("mres", [128, 2 * NCH * 512], BF16, stD2)
    sStg = sbuf("sStg", [128, 1024], BF16, stD2)
    xStg = sbuf("xStg", [128, 2048], F32, stD2)
    ysb = sbuf("ysb", [128, 2048], F32, stD2)
    dq1b = sbuf("dq1b", [128, 1024], F32, stD2)
    dq2b = sbuf("dq2b", [128, 1024], F32, stD2)
    dqIb = sbuf("dqIb", [128, 1024], I32, stD2)
    dqSb = sbuf("dqSb", [128, 512], BF16, stD2)
    sgT2 = sbuf("sgT2", [128, 1024], BF16, stD2)
    svdb = sbuf("svdb", [128, NCH], F32, stD2)
    clp2 = sbuf("clp2", [1, 2], F32, stD2)

    with nc.Block() as blk:
        @blk.sync
        def _(sy):
            inc(sy.dma_start(out=maxrt2[0:1, :], in_=rmaxcol2[:, 0:1]), "dD2", 16)
            sy.wait_ge(sem("dD2"), cget("dD2"))
            inc(sy.dma_start(out=rowm[0:1, :], in_=maxrt2[0:1, :]), "dD3", 16)

        @blk.vector
        def _(v):
            v = DW(v)
            v.wait_ge(sem("dD3"), cget("dD3"))
            inc(v.reduce_max(cc2row[0:1, 0:1], rowm[0:1, :], AX.X), "vD3")

        @blk.gpsimd
        def _(g):
            g.wait_ge(sem("vD3"), cget("vD3"))
            inc(g.dma_start(out=cc2i[0:1, :], in_=cc2row[0:1, :]), "dW", 16)
            g.wait_ge(sem("dW"), cget("dW"))
            if "nocc" in DBG:
                inc(g.dma_start(out=cc2o[0:1, :], in_=cc2i[0:1, :]), "cc2", 16)
            else:
                inc(g.collective_compute(
                    "AllReduce", ALU.max, replica_groups=[list(range(8))],
                    ins=[cc2i.ap().opt()], outs=[cc2o.ap().opt()]), "cc2", 1)

    with nc.Block() as blk:
        @blk.sync
        def _(sy):
            sy.wait_ge(sem("cc2"), cget("cc2"))
            inc(sy.dma_start(out=rowm[0:1, :], in_=cc2o[0:1, :]), "dD4", 16)
            inc(sy.dma_start(out=clp2[0:1, 1:2], in_=clipd.ap()), "dD4", 16)
            for ic in range(NIC):
                inc(sy.dma_start(out=wdS[:, 1024 * ic:1024 * (ic + 1)],
                                 in_=wdT_sp[128 * ic:128 * (ic + 1), :]),
                    "dD5", 16)

        @blk.scalar
        def _(s):
            s = DW(s)
            s.wait_ge(sem("dD4"), cget("dD4"))
            inc(s.activation(clp2[0:1, 1:2], clp2[0:1, 1:2], ACT.Exp,
                             bias=0.0, scale=-1.0), "sD1")

        @blk.vector
        def _(v):
            v = DW(v)
            v.wait_ge(sem("sD1"), cget("sD1"))
            v.tensor_scalar(clp2[0:1, 1:2], clp2[0:1, 1:2], 1.0, None, ALU.add)
            v.reciprocal(clp2[0:1, 1:2], clp2[0:1, 1:2])
            v.tensor_tensor(ccrow[0:1, 5:6], clp2[0:1, 1:2], rowm[0:1, 0:1],
                            ALU.mult)
            v.tensor_scalar(ccrow[0:1, 5:6], ccrow[0:1, 5:6], 1.0 / 127.0, None,
                            ALU.mult)
            inc(v.reciprocal(ccrow[0:1, 6:7], ccrow[0:1, 5:6]), "vD4")

        @blk.tensor
        def _(t):
            t.wait_ge(sem("vD4"), cget("vD4"))
            t.matmul(psR[:, 2:3], onesf[0:1, :], ccrow[0:1, 5:6],
                     start=True, stop=True)
            inc(t.matmul(psR[:, 3:4], onesf[0:1, :], ccrow[0:1, 6:7],
                         start=True, stop=True), "tD1")

    with nc.Block() as blk:
        @blk.vector
        def _(v):
            v = DW(v)
            v.wait_ge(sem("tD1"), cget("tD1"))
            v.tensor_copy(sdv[:, :], psR[:, 2:3])
            v.tensor_copy(rsdv[:, :], psR[:, 3:4])
            inc(v.tensor_scalar(svdb[:, :], scd[:, :], sdv[:, 0:1], None,
                                ALU.mult), "vD5")

    steps = []

    def addf(eng, fn, deps):
        steps.append((eng, fn, [d for d in deps if d is not None]))
        return len(steps) - 1

    i_mid = {}
    i_q = {}
    i_tdn = {}
    i_sdn = {}
    i_vdn = {}
    i_tf = {}
    i_vf = {}
    i_yy = {}
    for m in range(8):
        mq0 = NIC * 512 * (m % 2)

        def sy_mid(sy, m=m, mq0=mq0):
            for ic in range(NIC):
                inc(sy.dma_start(out=mqA[:, mq0 + 512 * ic:mq0 + 512 * (ic + 1)],
                                 in_=mid_spill[128 * ic:128 * (ic + 1),
                                               512 * m:512 * (m + 1)]),
                    "dD6", 16)
            sy.wait_ge(sem("dD6"), cget("dD6"))
            return sy.nop()
        i_mid[m] = addf("sync", sy_mid, [i_tdn.get((m - 2, NCH - 1))])

        for ip in range(NIC // 2):
            def v_mq(v, m=m, ip=ip, mq0=mq0):
                v.tensor_scalar(dq1b[:, :],
                                mqA[:, mq0 + 1024 * ip:mq0 + 1024 * (ip + 1)],
                                rsdv[:, 0:1], 128.5, ALU.mult, ALU.add)
                v.tensor_scalar(dq1b[:, :], dq1b[:, :], 0.0, None, ALU.max)
                v.tensor_copy(dqIb[:, :], dq1b[:, :])
                v.scalar_tensor_tensor(dq2b[:, :], dqIb[:, :], 1.0, dq1b[:, :],
                                       ALU.mult, ALU.is_le)
                v.tensor_tensor(dq1b[:, :], dqIb[:, :], dq2b[:, :], ALU.add)
                return v.tensor_scalar(mqA[:, mq0 + 1024 * ip:mq0 + 1024 * (ip + 1)],
                                       dq1b[:, :], -129.0, 127.0, ALU.add,
                                       ALU.min)
            i_q[(m, 2 * ip + 1)] = addf("vector", v_mq, [i_mid[m]])

        for oc in range(NCH):
            it2 = 8 * m + oc
            pd = psP if it2 % 2 == 0 else psQ
            hp = slice(512 * (it2 % 2), 512 * (it2 % 2 + 1))

            def t_dn(t, m=m, oc=oc, pd=pd, mq0=mq0):
                for ic in range(NIC):
                    mm = t.matmul(
                        pd[:, 0:512],
                        wdS[:, 1024 * ic + 128 * oc:1024 * ic + 128 * (oc + 1)],
                        mqA[:, mq0 + 512 * ic:mq0 + 512 * (ic + 1)],
                        start=(ic == 0), stop=(ic == NIC - 1))
                return mm
            i_tdn[(m, oc)] = addf("tensor", t_dn,
                                  [i_q[(m, NIC - 1)], i_sdn.get((m, oc - 2)),
                                   i_tf.get((m - 1, 3))])

            def s_dn(s, m=m, oc=oc, pd=pd, hp=hp):
                return s.activation(sgT2[:, hp], pd[:, 0:512], ACT.Silu,
                                    bias=0.0, scale=svdb[:, oc:oc + 1])
            i_sdn[(m, oc)] = addf("scalar", s_dn,
                                  [i_tdn[(m, oc)], i_vdn.get((m, oc - 2))])

            def sy_s(sy, m=m, oc=oc, hp=hp):
                inc(sy.dma_start(out=sStg[:, hp],
                                 in_=s_spill[128 * oc:128 * (oc + 1),
                                             512 * m:512 * (m + 1)]), "dD7", 16)
                sy.wait_ge(sem("dD7"), cget("dD7"))
                return sy.nop()
            isys = addf("sync", sy_s, [i_vdn.get((m, oc - 2))])

            def v_dn(g, m=m, oc=oc, hp=hp):
                return g.tensor_tensor(
                    mres[:, NCH * 512 * (m % 2) + 512 * oc:
                         NCH * 512 * (m % 2) + 512 * (oc + 1)],
                    sgT2[:, hp], sStg[:, hp], ALU.mult)
            i_vdn[(m, oc)] = addf("gpsimd", v_dn,
                                  [i_sdn[(m, oc)], isys,
                                   i_tf.get((m - 1, 3))])

        for u in range(4):
            tau = 4 * m + u
            bank = psA if u % 2 == 0 else psB
            xh = slice(1024 * (u % 2), 1024 * (u % 2 + 1))

            def t_fin(t, m=m, u=u, bank=bank):
                for oc in range(NCH):
                    tr = t.transpose(
                        bank[:, 128 * oc:128 * (oc + 1)],
                        mres[:, NCH * 512 * (m % 2) + 512 * oc + 128 * u:
                             NCH * 512 * (m % 2) + 512 * oc + 128 * (u + 1)],
                        idbf[:, :])
                return tr
            i_tf[(m, u)] = addf("tensor", t_fin,
                                [i_vdn[(m, NCH - 1)], i_vf.get((m, u - 2))])

            def sy_x(sy, tau=tau, xh=xh):
                inc(sy.dma_start(out=xStg[:, xh],
                                 in_=x_in[128 * tau:128 * (tau + 1), :]), "dD8", 16)
                sy.wait_ge(sem("dD8"), cget("dD8"))
                return sy.nop()
            isyx = addf("sync", sy_x, [i_vf.get((m, u - 2))])

            def v_fin(v, tau=tau, bank=bank, xh=xh):
                return v.tensor_tensor(ysb[:, xh], bank[:, 0:1024], xStg[:, xh],
                                       ALU.add)
            i_vf[(m, u)] = addf("vector", v_fin,
                                [i_tf[(m, u)], isyx, i_yy.get((m, u - 2))])

            def sy_y(sy, tau=tau, xh=xh):
                inc(sy.dma_start(out=y_out[128 * tau:128 * (tau + 1), :],
                                 in_=ysb[:, xh]), "dY", 16)
                sy.wait_ge(sem("dY"), cget("dY"))
                return sy.nop()
            i_yy[(m, u)] = addf("sync", sy_y, [i_vf[(m, u)]])
    with nc.Block() as blk:
        @blk.tensor
        def _(t):
            t.wait_ge(sem("vD5"), cget("vD5"))
        @blk.sync
        def _(sy):
            sy.wait_ge(sem("dD5"), cget("dD5"))
    barrier()
    chain2(steps, name="d4")
    barrier()
    stD2.close()
    stD.close()


    stack.close()
    return nc, taps


def _consts():
    one = np.ones((128, 128), dtype=np.float32)
    return {
        "c_idbf": np.eye(128, dtype=ml_dtypes.bfloat16),
        "c_ut": np.triu(one).astype(ml_dtypes.bfloat16),
        "c_sut": np.triu(one, 1).astype(ml_dtypes.bfloat16),
        "c_onesb": one.astype(ml_dtypes.bfloat16),
        "c_onesf": one,
        "c_iota128": np.arange(128, dtype=np.float32).reshape(128, 1),
        "c_iota32": np.arange(32, dtype=np.float32).reshape(1, 32),
    }


def _in_maps(inputs):
    x = np.ascontiguousarray(np.asarray(inputs["L0_attn_x"], dtype=np.float32))
    shared = {
        "wq": inputs["L0_attn_Wq"], "wk": inputs["L0_attn_Wk"],
        "wv": inputs["L0_attn_Wv"], "wgt": inputs["L0_attn_gate_w"],
        "wp": inputs["L0_attn_p_w"], "wo": inputs["L0_attn_out_w"],
        "wclk": inputs["L0_attn_clock_w"],
        "b_gate": inputs["L0_attn_gate_b"], "b_out": inputs["L0_attn_out_b"],
        "b_clk": inputs["L0_attn_clock_b"],
        "g1": inputs["L0_attn_ln1_g"], "b1": inputs["L0_attn_ln1_b"],
        "g2": inputs["L0_attn_ln2_g"], "b2": inputs["L0_attn_ln2_b"],
        "w1": inputs["L0_attn_mlp_w1"], "bm1": inputs["L0_attn_mlp_b1"],
        "w2": inputs["L0_attn_mlp_w2"], "bm2": inputs["L0_attn_mlp_b2"],
        "clipx": inputs["L0_ffn_clip_x"], "clipd": inputs["L0_ffn_clip_down"],
        "wgate": inputs["L0_ffn_w_gate"], "wup": inputs["L0_ffn_w_up"],
        "wdown": inputs["L0_ffn_w_down"],
    }
    shared = {k: np.ascontiguousarray(np.asarray(v, dtype=np.float32))
              for k, v in shared.items()}
    shared.update(_consts())
    maps = []
    for core in range(8):
        m = dict(shared)
        m["x"] = np.ascontiguousarray(
            x[core * BPC:(core + 1) * BPC].reshape(NT, D))
        maps.append(m)
    return maps


_RES = {}


def kernel(**inputs):
    nc, taps = build_program()
    maps = _in_maps(inputs)
    res = run_bass_kernel_spmd(nc, maps, core_ids=list(range(8)))
    _RES["results"] = res.results
    ys = [np.asarray(res.results[c]["y"]).reshape(BPC, T, D) for c in range(8)]
    return np.concatenate(ys, axis=0)
